# version 12
# baseline (speedup 1.0000x reference)
import contextlib
import numpy as np
import ml_dtypes

import concourse.bass as bass
import concourse.mybir as mybir
from concourse.bass_utils import run_bass_kernel_spmd

F32 = mybir.dt.float32
BF16 = mybir.dt.bfloat16
AF = mybir.ActivationFunctionType
OP = mybir.AluOpType
AX = mybir.AxisListType.X

NCORES = 8
TC = 1024
NTT = 8
D = 2048
DT = 16
GT = 8
H = 8192
NPAIR = 64
NPK = 64
W = 64
CH = 128
NCH = 8
NW = 16

MAGIC = 12582912.0
TWO_PI = 6.283185307179586
INV_2PI = 0.15915494309189535
HALF_PI = 1.5707963267948966
LNH = -0.34657359027997264
TINY = 1e-35
LN_EPS = 1e-5
RMS_EPS = 1.1920929e-07

_DEBUG = False
BARRIER_AG = False
RACE_DETECT = False
RAW_GROUPS = {"lrli", "tts1", "rr", "w1w2", "big", "tts2", "ext"}
A_LEVEL = 3


class _SerEngine:

    _PASS = {"wait_ge", "drain", "sem_inc", "nop", "engine_nop", "wait_op"}
    _HAZARD = {"tensor_reduce", "reduce_sum", "reduce_max", "reciprocal"}

    def __init__(self, eng, always_drain):
        self._e = eng
        self._always = always_drain

    def __getattr__(self, n):
        f = getattr(self._e, n)
        if n in self._PASS or not callable(f):
            return f
        def wrap(*a, **k):
            r = f(*a, **k)
            if RACE_DETECT or self._always:
                self._e.drain()
            return r
        return wrap


class Ctr:
    def __init__(self):
        self.v = 0

    def add(self, n=1):
        self.v += n
        return self.v


def build_nc(debug=False, stop=None, skip_prepass=False, skip_ag=False):
    def _R(eng, grp="x"):
        return eng._e if (grp in RAW_GROUPS and not RACE_DETECT) else eng

    nc = bass.Bass("TRN2", target_bir_lowering=False, debug=False,
                   detect_race_conditions=RACE_DETECT)

    x_in = nc.dram_tensor("x", [TC, D], F32, kind="ExternalInput").ap()
    w_in = nc.dram_tensor("w_in", [D, D], F32, kind="ExternalInput").ap()
    b_in_rs = nc.dram_tensor("b_in_rs", [128, DT], F32, kind="ExternalInput").ap()
    w_gx = nc.dram_tensor("w_gx", [GT, 2, 128, 128], F32, kind="ExternalInput").ap()
    w_ga = nc.dram_tensor("w_ga", [GT, 2, 128, 128], F32, kind="ExternalInput").ap()
    b_gx_rs = nc.dram_tensor("b_gx_rs", [128, GT], F32, kind="ExternalInput").ap()
    b_ga_rs = nc.dram_tensor("b_ga_rs", [128, GT], F32, kind="ExternalInput").ap()
    ar_rs = nc.dram_tensor("ar_rs", [128, GT], F32, kind="ExternalInput").ap()
    ai_rs = nc.dram_tensor("ai_rs", [128, GT], F32, kind="ExternalInput").ap()
    w_out = nc.dram_tensor("w_out", [D, D], F32, kind="ExternalInput").ap()
    b_out_bc = nc.dram_tensor("b_out_bc", [128, D], F32, kind="ExternalInput").ap()
    wgu_sh = nc.dram_tensor("wgu_sh", [16, 128, D], F32, kind="ExternalInput").ap()
    wpj_sh = nc.dram_tensor("wpj_sh", [8, 16, 128, 128], F32,
                            kind="ExternalInput").ap()
    mask_lt = nc.dram_tensor("mask_lt", [128, NCORES], F32,
                             kind="ExternalInput").ap()
    ident_in = nc.dram_tensor("ident", [128, 128], BF16, kind="ExternalInput").ap()

    out_t = nc.dram_tensor("out", [TC, D], F32, kind="ExternalOutput").ap()

    dbg = {}
    if debug:
        for name, shape, dt_ in [
            ("dbg_xh", [128, DT * TC], F32),
            ("dbg_ga", [128, GT * TC], F32),
            ("dbg_gx", [128, GT * TC], F32),
            ("dbg_h", [128, DT * TC], BF16),
            ("dbg_ln1", [NTT * 128, D], F32),
            ("dbg_inv", [128, 16], F32),
            ("dbg_wq", [24 * 128, D], BF16),
            ("dbg_hin", [128, 256], F32),
            ("dbg_scr", [NCH * 4 * 128, 1024], F32),
            ("dbg_summ", [128, 512], F32),
        ]:
            dbg[name] = nc.dram_tensor(name, shape, dt_, kind="ExternalOutput").ap()

    wshard = nc.dram_tensor("wshard", [24 * 128, D], BF16).ap()
    wgath1 = nc.dram_tensor("wgath1", [NCORES * 16 * 128, D], BF16).ap()
    wgath2 = nc.dram_tensor("wgath2", [NCORES * 8 * 128, D], BF16).ap()
    scanscr = nc.dram_tensor("scanscr", [NCH * 4 * 128, 1024], F32).ap()
    cc_in = nc.dram_tensor("cc_in", [128, 32], F32).ap()
    cc_out = nc.dram_tensor("cc_out", [NCORES * 128, 32], F32).ap()
    ln1scr = nc.dram_tensor("ln1scr", [NTT * 128, D], F32).ap()
    hhscr = nc.dram_tensor("hhscr", [NPK * 128, TC], BF16).ap()
    outscr = nc.dram_tensor("outscr", [NTT * 128, D], F32).ap()

    def wg_tile(owner, pos):
        if pos < 16:
            base = (owner * 16 + pos) * 128
            return wgath1[base:base + 128, :]
        base = (owner * 8 + pos - 16) * 128
        return wgath2[base:base + 128, :]

    with contextlib.ExitStack() as ctx:
        E = ctx.enter_context
        sLd = E(nc.semaphore("sLd"))
        sSt = E(nc.semaphore("sSt"))
        sGp = E(nc.semaphore("sGp"))
        sCc = E(nc.semaphore("sCc"))
        sPe = E(nc.semaphore("sPe"))
        sDv = E(nc.semaphore("sDv"))
        sSc = E(nc.semaphore("sSc"))
        cLd, cSt, cGp, cCc, cPe, cDv, cSc = (Ctr() for _ in range(7))

        _block = E(nc.Block())

        class _BlockShim:
            def __init__(self, b):
                self._b = b

            def vector(self, f):
                return self._b.vector(lambda eng: f(_SerEngine(eng, True)))

            def scalar(self, f):
                return self._b.scalar(lambda eng: f(_SerEngine(eng, False)))

            def __getattr__(self, n):
                return getattr(self._b, n)

        block = _BlockShim(_block)

        sCst = E(nc.semaphore("sCst"))
        sWqL = E(nc.semaphore("sWqL"))
        sWsT = E(nc.semaphore("sWsT"))
        sWsP = E(nc.semaphore("sWsP"))
        sXl0 = E(nc.semaphore("sXl0"))
        sXl1 = E(nc.semaphore("sXl1"))
        sWn0 = E(nc.semaphore("sWn0"))
        sWn1 = E(nc.semaphore("sWn1"))
        sGts = E(nc.semaphore("sGts"))
        sScr = E(nc.semaphore("sScr"))
        sScr1 = E(nc.semaphore("sScr1"))
        sCo0 = E(nc.semaphore("sCo0"))
        sCo1 = E(nc.semaphore("sCo1"))
        sXr0 = E(nc.semaphore("sXr0"))
        sXr1 = E(nc.semaphore("sXr1"))
        sBo = E(nc.semaphore("sBo"))
        sWo = E(nc.semaphore("sWo"))
        sL1s = E(nc.semaphore("sL1s"))
        sLbb = E(nc.semaphore("sLbb"))
        sGu0 = E(nc.semaphore("sGu0"))
        sGu1 = E(nc.semaphore("sGu1"))
        sHhS = E(nc.semaphore("sHhS"))
        sPj0 = E(nc.semaphore("sPj0"))
        sPj1 = E(nc.semaphore("sPj1"))
        sPj2 = E(nc.semaphore("sPj2"))
        sPj3 = E(nc.semaphore("sPj3"))
        sLnr = E(nc.semaphore("sLnr"))
        sOsc = E(nc.semaphore("sOsc"))
        sOsc1 = E(nc.semaphore("sOsc1"))
        sJl0 = E(nc.semaphore("sJl0"))
        sJl1 = E(nc.semaphore("sJl1"))
        sCgI = E(nc.semaphore("sCgI"))
        (cCst, cWqL, cWsT, cWsP, cGts, cScr, cScr1, cWo, cL1s, cLbb, cHhS,
         cLnr, cOsc, cOsc1, cCgI) = (Ctr() for _ in range(15))
        cXl = [Ctr(), Ctr()]
        cWn = [Ctr(), Ctr()]
        cCo = [Ctr(), Ctr()]
        cXr = [Ctr(), Ctr()]
        cGu = [Ctr(), Ctr()]
        cPj = [Ctr(), Ctr(), Ctr(), Ctr()]
        cJl = [Ctr(), Ctr()]

        ident = E(nc.sbuf_tensor("identB", [128, 128], BF16))
        consts = E(nc.sbuf_tensor("consts", [128, 64], F32))
        onesb = E(nc.sbuf_tensor("onesb", [128, 1], BF16))
        nc.const_aps.register(F32, HALF_PI, consts[:, 56:57])
        nc.const_aps.register(F32, LNH, consts[:, 57:58])
        nc.const_aps.register(F32, 0.25, consts[:, 58:59])
        nc.const_aps.register(F32, MAGIC, consts[:, 59:60])
        nc.const_aps.register(F32, -MAGIC, consts[:, 60:61])
        ones64 = E(nc.sbuf_tensor("ones64", [128, W], F32))
        summ = E(nc.sbuf_tensor("summ", [128, 4 * 128], F32))
        hinb = E(nc.sbuf_tensor("hinb", [128, 256], F32))
        sgath = E(nc.sbuf_tensor("sgath", [128, NCORES * 32], F32))

        @block.sync
        def _(sync):
            sync.dma_start(out=ident[:, :], in_=ident_in).then_inc(sCst, 16)
            sync.dma_start(out=consts[:, 0:16], in_=b_in_rs).then_inc(sCst, 16)
            sync.dma_start(out=consts[:, 16:24], in_=b_gx_rs).then_inc(sCst, 16)
            sync.dma_start(out=consts[:, 24:32], in_=b_ga_rs).then_inc(sCst, 16)
            sync.dma_start(out=consts[:, 32:40], in_=ar_rs).then_inc(sCst, 16)
            sync.dma_start(out=consts[:, 40:48], in_=ai_rs).then_inc(sCst, 16)
            sync.dma_start(out=consts[:, 48:56], in_=mask_lt).then_inc(sCst, 16)
        consts_ld = cCst.add(16 * 7)

        @block.scalar
        def _(scalar):
            scalar.wait_ge(sCst, consts_ld)
            scalar.activation(consts[:, 32:40], consts[:, 32:40], AF.Exp)
            scalar.activation(consts[:, 32:40], consts[:, 32:40], AF.Ln,
                              bias=1.0).then_inc(sSc, 1)
        cSc.add()

        @block.vector
        def _(vector):
            vector.wait_ge(sSc, cSc.v)
            vector.tensor_scalar_mul(consts[:, 32:40], consts[:, 32:40], -8.0)
            vector.memset(consts[:, 56:57], HALF_PI)
            vector.memset(consts[:, 57:58], LNH)
            vector.memset(consts[:, 58:59], 0.25)
            vector.memset(consts[:, 59:60], MAGIC)
            vector.memset(consts[:, 60:61], -MAGIC)
            vector.memset(onesb[:, :], 1.0)
            vector.memset(ones64[:, :], 1.0).then_inc(sDv, 1)
        consts_rdy = cDv.add()

        g_xh = nc.sbuf_tensor("xh", [128, DT * TC], F32)
        xh = g_xh.__enter__()
        g_fb = nc.sbuf_tensor("fbuf", [128, 1112], F32)
        fbuf = g_fb.__enter__()
        g_ga = nc.sbuf_tensor("ga", [128, GT * TC], F32)
        ga = g_ga.__enter__()
        g_gx = nc.sbuf_tensor("gx", [128, GT * TC], F32)
        gx = g_gx.__enter__()

        pctx = contextlib.ExitStack()
        p1_qrdy = {}
        if not skip_prepass:
            Ep = pctx.enter_context
            wq_b = Ep(nc.sbuf_tensor("wq_b", [128, D], BF16, side="right"))
            wq_s = Ep(nc.sbuf_tensor("wq_s", [128, 32], F32, side="right"))
            wq_sb = Ep(nc.sbuf_tensor("wq_sb", [128, 32], BF16, side="right"))
            wq_q = Ep(nc.sbuf_tensor("wq_q", [128, D], F32, side="right"))
            wq_hat = Ep(nc.sbuf_tensor("wq_hat", [128, D], BF16, side="right"))
            wq_T = Ep(nc.sbuf_tensor("wq_T", [128, D], BF16, side="right"))
            pq = Ep(nc.psum_tensor("pq", [128, 512], BF16, side="right"))

        def emit_p1_a(pos):
            if skip_prepass:
                return

            @block.gpsimd
            def _(gpsimd, pos=pos):
                if pos > 0:
                    gpsimd.wait_ge(sDv, cDv.v)
                if pos < 16:
                    gpsimd.dma_start(out=wq_b[:, :], in_=wgu_sh[pos]
                                     ).then_inc(sWqL, 16)
                else:
                    src = wpj_sh[pos - 16].rearrange("a p f -> p a f")
                    dst = wq_b.ap().rearrange("p (a f) -> p a f", f=128)
                    gpsimd.dma_start(out=dst, in_=src).then_inc(sWqL, 16)
            cWqL.add(16)

            @block.vector
            def _(vector, pos=pos):
                vector.wait_ge(sWqL, cWqL.v)
                if pos > 0:
                    vector.wait_ge(sPe, cPe.v)
                    if pos <= 16:
                        vector.wait_ge(sWsT, cWsT.v)
                    else:
                        vector.wait_ge(sWsP, cWsP.v)
                b3 = wq_b.ap().rearrange("p (g j) -> p g j", j=W)
                vector.tensor_reduce(wq_s[:, :], b3, AX, OP.add,
                                     apply_absolute_value=True)
                vector.tensor_scalar_mul(wq_sb[:, :], wq_s[:, :], 1.0 / W)
                vector.tensor_scalar_max(wq_q[:, 0:32], wq_sb[:, :], 1e-8)
                vector.reciprocal(wq_s[:, :], wq_q[:, 0:32])
                rb = wq_s.ap()[:, :, None].broadcast_to([128, 32, W])
                h3 = wq_hat.ap().rearrange("p (g j) -> p g j", j=W)
                vector.tensor_tensor(h3, b3, rb, OP.mult)
                vector.tensor_scalar(wq_q[:, :], wq_hat[:, :], MAGIC, MAGIC,
                                     OP.add, OP.subtract)
                vector.tensor_scalar(wq_q[:, :], wq_q[:, :], 1.0, -1.0,
                                     OP.min, OP.max)
                sb3 = wq_sb.ap()[:, :, None].broadcast_to([128, 32, W])
                q3 = wq_q.ap().rearrange("p (g j) -> p g j", j=W)
                vector.tensor_tensor(h3, q3, sb3, OP.mult).then_inc(sDv, 1)
            cDv.add()
            p1_qrdy[pos] = cDv.v

        def emit_p1_b(pos):
            if skip_prepass:
                return
            q_rdy = p1_qrdy[pos]
            for grp in range(4):
                @block.tensor
                def _(tensor, pos=pos, grp=grp, q_rdy=q_rdy):
                    tensor.wait_ge(sDv, q_rdy)
                    tensor.wait_ge(sCst, consts_ld)
                    if not (pos == 0 and grp == 0):
                        tensor.wait_ge(sDv, cDv.v)
                    last = None
                    for j in range(4):
                        k = grp * 4 + j
                        last = tensor.transpose(
                            pq[:, j * 128:(j + 1) * 128],
                            wq_hat[:, k * 128:(k + 1) * 128], ident[:, :])
                    last.then_inc(sPe, 1)
                cPe.add()

                @block.vector
                def _(vector, grp=grp):
                    vector.wait_ge(sPe, cPe.v)
                    vector.tensor_copy(wq_T[:, grp * 512:(grp + 1) * 512],
                                       pq[:, :]).then_inc(sDv, 1)
                cDv.add()

            @block.gpsimd
            def _(gpsimd, pos=pos):
                gpsimd.wait_ge(sDv, cDv.v)
                sem = sWsT if pos < 16 else sWsP
                gpsimd.dma_start(out=wshard[pos * 128:(pos + 1) * 128, :],
                                 in_=wq_T[:, :]).then_inc(sem, 16)
            (cWsT if pos < 16 else cWsP).add(16)

        with contextlib.ExitStack() as actx:
            Ea = actx.enter_context
            xcm = Ea(nc.sbuf_tensor("xcm", [128, DT * TC], BF16))
            xtmb = Ea(nc.sbuf_tensor("xtmb", [128, D], BF16))
            pA = Ea(nc.psum_tensor("pA", [128, 2048], BF16))

            for tt in range(NTT):
                bo = 0
                sXl = (sXl0, sXl1)[tt % 2]
                cXl_ = cXl[tt % 2]

                @block.gpsimd
                def _(gpsimd, tt=tt, bo=bo, sXl=sXl):
                    if tt >= 1:
                        gpsimd.wait_ge(sPe, cPe.v)
                    gpsimd.dma_start(out=xtmb[:, bo:bo + D],
                                     in_=x_in[tt * 128:(tt + 1) * 128, :]
                                     ).then_inc(sXl, 16)
                x_ld = cXl_.add(16)

                for grp in range(4 if A_LEVEL >= 2 else 0):
                    po = (grp % 2) * 1024

                    @block.tensor
                    def _(tensor, tt=tt, grp=grp, bo=bo, po=po, x_ld=x_ld,
                          sXl=sXl):
                        tensor.wait_ge(sXl, x_ld)
                        tensor.wait_ge(sCst, consts_ld)
                        if tt * 4 + grp >= 2:
                            tensor.wait_ge(sDv, cDv.v)
                        last = None
                        for j in range(4):
                            k = grp * 4 + j
                            last = tensor.transpose(
                                pA[:, po + j * 128: po + (j + 1) * 128],
                                xtmb[:, bo + k * 128: bo + (k + 1) * 128],
                                ident[:, :])
                        last.then_inc(sPe, 1)
                    cPe.add()

                    if A_LEVEL >= 3:
                        @block.vector
                        def _(vector, tt=tt, grp=grp, po=po):
                            vector.wait_ge(sPe, cPe.v)
                            dst = xcm.ap().rearrange("p (j t) -> p j t", t=TC)[
                                :, grp * 4:(grp + 1) * 4, tt * 128:(tt + 1) * 128]
                            src = pA.ap()[:, po:po + 512].rearrange(
                                "p (j t) -> p j t", t=128)
                            vector.tensor_copy(dst, src).then_inc(sDv, 1)
                        cDv.add()
                    else:
                        @block.vector
                        def _(vector):
                            vector.wait_ge(sPe, cPe.v)
                            nc.vector.engine_nop().then_inc(sDv, 1)
                        cDv.add()

            if stop == 2:
                @block.sync
                def _(sync):
                    sync.wait_ge(sDv, cDv.v)
                actx.close()
                g_gx.__exit__(None, None, None)
                g_ga.__exit__(None, None, None)
                g_xh.__exit__(None, None, None)
                return nc

            with contextlib.ExitStack() as bctx:
                Eb = bctx.enter_context
                winj = Eb(nc.sbuf_tensor("winj", [128, 2 * D], BF16))
                pB = Eb(nc.psum_tensor("pB", [128, 1024], F32))

                w_in_r = w_in.rearrange("(k p) (j f) -> p k j f", p=128, f=128)
                x_cm_rdy = cDv.v
                n = 0
                for j in range(DT):
                    wo = (j % 2) * D
                    sWn = (sWn0, sWn1)[j % 2]

                    @block.gpsimd
                    def _(gpsimd, j=j, wo=wo, sWn=sWn):
                        if j >= 2:
                            gpsimd.wait_ge(sPe, cPe.v)
                        dst = winj.ap()[:, wo:wo + D].rearrange(
                            "p (k f) -> p k f", f=128)
                        gpsimd.dma_start(out=dst, in_=w_in_r[:, :, j, :]
                                         ).then_inc(sWn, 16)
                    wj_ld = cWn[j % 2].add(16)

                    for ts_ in range(2):
                        po = (n % 2) * 512

                        @block.tensor
                        def _(tensor, j=j, ts_=ts_, po=po, n=n, wo=wo,
                              wj_ld=wj_ld, sWn=sWn):
                            tensor.wait_ge(sWn, wj_ld)
                            tensor.wait_ge(sDv, x_cm_rdy)
                            if n >= 2:
                                tensor.wait_ge(sSc, cSc.v - 1)
                            last = None
                            for k in range(DT):
                                last = tensor.matmul(
                                    pB[:, po:po + 512],
                                    winj[:, wo + k * 128: wo + (k + 1) * 128],
                                    xcm[:, k * TC + ts_ * 512:
                                        k * TC + ts_ * 512 + 512],
                                    start=(k == 0), stop=(k == DT - 1))
                            last.then_inc(sPe, 1)
                        cPe.add()

                        @block.scalar
                        def _(scalar, j=j, ts_=ts_, po=po):
                            scalar.wait_ge(sPe, cPe.v)
                            scalar.activation(
                                xh[:, j * TC + ts_ * 512: j * TC + ts_ * 512 + 512],
                                pB[:, po:po + 512], AF.Identity,
                                bias=consts[:, j:j + 1]).then_inc(sSc, 1)
                        cSc.add()
                        n += 1
                    if j > 0:
                        emit_p1_b(j - 1)
                    emit_p1_a(j)
                xh_rdy_sc = cSc.v

        if stop == 3:
            @block.sync
            def _(sync):
                sync.wait_ge(sSc, xh_rdy_sc)
            g_gx.__exit__(None, None, None)
            g_ga.__exit__(None, None, None)
            g_xh.__exit__(None, None, None)
            return nc

        if debug:
            @block.sync
            def _(sync):
                sync.wait_ge(sSc, xh_rdy_sc)
                sync.dma_start(out=dbg["dbg_xh"], in_=xh[:, :]).then_inc(sSt, 16)
            cSt.add(16)

        with contextlib.ExitStack() as cctx:
            Ec = cctx.enter_context
            xhb = Ec(nc.sbuf_tensor("xhb", [128, DT * TC], BF16))
            wgb = Ec(nc.sbuf_tensor("wgb", [128, GT * 4 * 128], BF16))
            pC = Ec(nc.psum_tensor("pC", [128, 1024], F32))

            @block.gpsimd
            def _(gpsimd):
                gpsimd.wait_ge(sSc, xh_rdy_sc)
                for r in range(DT):
                    gpsimd.dma_start(out=xhb[:, r * TC:(r + 1) * TC],
                                     in_=xh[:, r * TC:(r + 1) * TC]
                                     ).then_inc(sGts, 16)
                for g in range(GT):
                    for k in range(2):
                        gpsimd.dma_start(
                            out=wgb[:, (g * 4 + k) * 128:(g * 4 + k + 1) * 128],
                            in_=w_gx[g, k]).then_inc(sGts, 16)
                        gpsimd.dma_start(
                            out=wgb[:, (g * 4 + 2 + k) * 128:(g * 4 + 3 + k) * 128],
                            in_=w_ga[g, k]).then_inc(sGts, 16)
            gates_ld = cGts.add(16 * (DT + GT * 4))

            n = 0
            for g in range(GT):
                for half in range(2):
                    for ts_ in range(2):
                        po = (n % 2) * 512

                        @block.tensor
                        def _(tensor, g=g, half=half, ts_=ts_, po=po, n=n):
                            tensor.wait_ge(sGts, gates_ld)
                            if n >= 2:
                                tensor.wait_ge(sSc, cSc.v - 1)
                            last = None
                            for k in range(2):
                                lb = (g * 4 + half * 2 + k) * 128
                                rbase = (2 * g + k) * TC + ts_ * 512
                                last = tensor.matmul(
                                    pC[:, po:po + 512],
                                    wgb[:, lb:lb + 128],
                                    xhb[:, rbase:rbase + 512],
                                    start=(k == 0), stop=(k == 1))
                            last.then_inc(sPe, 1)
                        cPe.add()

                        @block.scalar
                        def _(scalar, g=g, half=half, ts_=ts_, po=po):
                            scalar.wait_ge(sPe, cPe.v)
                            dstbuf = gx if half == 0 else ga
                            bcol = (16 if half == 0 else 24) + g
                            scalar.activation(
                                dstbuf[:, g * TC + ts_ * 512:
                                       g * TC + ts_ * 512 + 512],
                                pC[:, po:po + 512], AF.Sigmoid,
                                bias=consts[:, bcol:bcol + 1]).then_inc(sSc, 1)
                        cSc.add()
                        n += 1
                        if n % 4 == 0:
                            m = n // 4 - 1
                            emit_p1_b(15 + m)
                            emit_p1_a(16 + m)
            gates_rdy = cSc.v

        if stop == 4:
            @block.sync
            def _(sync):
                sync.wait_ge(sSc, gates_rdy)
            g_gx.__exit__(None, None, None)
            g_ga.__exit__(None, None, None)
            g_xh.__exit__(None, None, None)
            return nc

        if debug:
            @block.sync
            def _(sync):
                sync.wait_ge(sSc, gates_rdy)
                sync.dma_start(out=dbg["dbg_ga"], in_=ga[:, :]).then_inc(sSt, 16)
                sync.dma_start(out=dbg["dbg_gx"], in_=gx[:, :]).then_inc(sSt, 16)
            cSt.add(32)

        if not skip_prepass:
            emit_p1_b(23)
        pctx.close()

        def emit_wq_ag():
            if skip_prepass:
                return cCc.v, cCc.v
            if skip_ag:
                @block.gpsimd
                def _(gpsimd):
                    gpsimd.wait_ge(sWsT, cWsT.v)
                    gpsimd.wait_ge(sWsP, cWsP.v)
                    gpsimd.nop().then_inc(sCc, 2)
            else:
                @block.gpsimd
                def _(gpsimd):
                    gpsimd.wait_ge(sWsT, 16 * 16)
                    gpsimd.collective_compute(
                        "AllGather", OP.bypass,
                        replica_groups=[list(range(NCORES))],
                        ins=[wshard[0:16 * 128, :]], outs=[wgath1],
                    ).then_inc(sCc, 1)
                    gpsimd.wait_ge(sWsP, cWsP.v)
                    gpsimd.collective_compute(
                        "AllGather", OP.bypass,
                        replica_groups=[list(range(NCORES))],
                        ins=[wshard[16 * 128:, :]], outs=[wgath2],
                    ).then_inc(sCc, 1)
            return cCc.add(), cCc.add()

        wq_ag1, wq_ag = emit_wq_ag()

        if stop == 1:
            @block.sync
            def _(sync):
                sync.wait_ge(sCc, wq_ag)
            return nc

        @block.vector
        def _(vector):
            vector.wait_ge(sSc, gates_rdy)
            vector.wait_ge(sDv, consts_rdy)
            nc.vector.engine_nop().then_inc(sDv, 1)
        xc_rdy = cDv.add()

        NSLOT = 16
        g_pool = nc.sbuf_tensor("pool", [128, NSLOT * 1024], F32)
        poolt = g_pool.__enter__()

        def _mkP(par):
            def sl(s):
                if s in (16, 17):
                    return s - 16
                if par and s in (2, 3, 4, 5):
                    return s + 10
                return s
            Pf = lambda s: poolt[:, sl(s) * 1024:(sl(s) + 1) * 1024]
            P3f = lambda s, t=CH: poolt.ap()[:, sl(s) * 1024:
                                             (sl(s) + 1) * 1024].rearrange(
                "p (r t) -> p r t", t=t)
            return Pf, P3f

        scr_mark = []
        for c in range(NCH):
            t0 = c * CH
            P_, P3 = _mkP(c % 2)

            def gsl3(buf, off=0, t0=t0):
                return buf.ap().rearrange("p (r t) -> p r t", t=TC)[
                    :, off:off + GT, t0:t0 + CH]

            @block.scalar
            def _(scalar, c=c, t0=t0):
                scalar.wait_ge(sSc, gates_rdy)
                if c == 0:
                    scalar.wait_ge(sDv, xc_rdy)
                if c > 0:
                    scalar.wait_ge(sDv, cDv.v)
                rs = _R(scalar, "lrli")
                last = None
                for r in range(GT):
                    sl = ga[:, r * TC + t0: r * TC + t0 + CH]
                    rs.mul(P_(0)[:, r * CH:(r + 1) * CH], sl,
                           consts[:, 32 + r:33 + r])
                    last = rs.mul(P_(1)[:, r * CH:(r + 1) * CH], sl,
                                  consts[:, 40 + r:41 + r])
                last.then_inc(sSc, 1)
            cSc.add()

            @block.vector
            def _(vector, c=c, t0=t0,
                  sm=(scr_mark[c - 2] if c >= 2 else 0)):
                vector.wait_ge(sSc, cSc.v)
                if c == 0:
                    vector.wait_ge(sWsT, cWsT.v)
                    vector.wait_ge(sWsP, cWsP.v)
                if c >= 2:
                    vector.wait_ge((sScr, sScr1)[c % 2], sm)
                rv = _R(vector, "tts1")
                last = None
                for r in range(GT):
                    for w in range(2):
                        a = r * CH + w * W
                        rv.tensor_tensor_scan(
                            P_(2)[:, a:a + W], ones64[:, :], P_(0)[:, a:a + W],
                            0.0, OP.mult, OP.add)
                        last = rv.tensor_tensor_scan(
                            P_(3)[:, a:a + W], ones64[:, :], P_(1)[:, a:a + W],
                            0.0, OP.mult, OP.add)
                vector.drain()
                last.then_inc(sDv, 1)
            cDv.add()

            @block.scalar
            def _(scalar, c=c):
                scalar.wait_ge(sDv, cDv.v)
                rs = _R(scalar, "luts")
                rs.activation(P_(4), P_(0), AF.Exp, scale=2.0)
                rs.activation(P_(5), P_(0), AF.Exp, scale=4.0)
                rs.activation(P_(6), P_(2), AF.Exp)
                rs.activation(P_(7), P_(2), AF.Exp, scale=-1.0)
                last = rs.activation(P_(8), P_(1), AF.Sin, scale=2.0,
                                     bias=HALF_PI)
                last.then_inc(sSc, 1)
            cSc.add()

            @block.vector
            def _(vector, c=c):
                vector.wait_ge(sSc, cSc.v)
                rv = _R(vector, "rr")
                rv.tensor_scalar(P_(0), P_(3), INV_2PI, 0.25, OP.mult, OP.add)
                rv.tensor_scalar(P_(9), P_(3), INV_2PI, None, OP.mult)
                rv.tensor_scalar(P_(0), P_(0), MAGIC, MAGIC, OP.add, OP.subtract)
                rv.tensor_scalar(P_(9), P_(9), MAGIC, MAGIC, OP.add, OP.subtract)
                rv.scalar_tensor_tensor(P_(0), P_(0), -TWO_PI, P_(3),
                                        OP.mult, OP.add)
                rv.scalar_tensor_tensor(P_(2), P_(9), -TWO_PI, P_(3),
                                        OP.mult, OP.add)
                rv.tensor_scalar(P_(0), P_(0), HALF_PI - 1e-6,
                                 -3.0 * HALF_PI + 1e-6, OP.min, OP.max)
                rv.tensor_scalar(P_(2), P_(2), TWO_PI / 2 - 1e-6,
                                 -TWO_PI / 2 + 1e-6, OP.min, OP.max)
                vector.tensor_mul(P_(9), P_(4), P_(8))
                vector.scalar_tensor_tensor(P_(4), P_(9), -2.0, P_(5),
                                            OP.mult, OP.add)
                vector.tensor_scalar(P_(4), P_(4), 1.0, TINY, OP.add, OP.max
                                     ).then_inc(sDv, 1)
            cDv.add()

            @block.scalar
            def _(scalar, c=c):
                scalar.wait_ge(sDv, cDv.v)
                scalar.activation(P_(5), P_(4), AF.Ln)
                scalar.activation(P_(5), P_(5), AF.Exp, scale=0.5
                                  ).then_inc(sSc, 1)
            cSc.add()

            @block.vector
            def _(vector, c=c):
                vector.wait_ge(sSc, cSc.v)
                rv = _R(vector, "w1w2")
                rv.tensor_sub(P_(8), P_(5), P_(9))
                rv.tensor_add(P_(9), P_(5), P_(9))
                rv.tensor_scalar(P_(8), P_(8), 1.0, TINY, OP.add, OP.max)
                vector.tensor_scalar(P_(9), P_(9), -1.0, TINY, OP.add, OP.max
                                     ).then_inc(sDv, 1)
            cDv.add()

            @block.scalar
            def _(scalar, c=c):
                scalar.wait_ge(sDv, cDv.v)
                scalar.activation(P_(10), P_(8), AF.Ln)
                scalar.activation(P_(10), P_(10), AF.Exp, scale=0.5, bias=LNH)
                scalar.activation(P_(11), P_(9), AF.Ln)
                scalar.activation(P_(11), P_(11), AF.Exp, scale=0.5, bias=LNH)
                scalar.activation(P_(16), P_(0), AF.Sin, bias=HALF_PI)
                scalar.activation(P_(17), P_(2), AF.Sin).then_inc(sSc, 1)
            cSc.add()

            @block.vector
            def _(vector, c=c, t0=t0):
                vector.wait_ge(sSc, cSc.v)
                rv = _R(vector, "big")
                rv.tensor_mul(P_(2), P_(6), P_(16))
                rv.tensor_mul(P_(3), P_(6), P_(17))
                rv.tensor_mul(P_(16), P_(7), P_(16))
                rv.tensor_mul(P_(17), P_(7), P_(17))
                rv.tensor_mul(P_(6), P_(10), P_(16))
                rv.tensor_mul(P_(7), P_(11), P_(17))
                rv.tensor_mul(P_(9), P_(10), P_(17))
                rv.tensor_mul(P_(8), P_(11), P_(16))
                rv.tensor_sub(P_(6), P_(6), P_(7))
                rv.tensor_add(P_(7), P_(9), P_(8))
                rv.tensor_tensor(P3(6), gsl3(gx), P3(6), OP.mult)
                rv.tensor_tensor(P3(7), gsl3(gx), P3(7), OP.mult)
                rv.tensor_tensor(P3(10), gsl3(xh), P3(6), OP.mult)
                rv.tensor_tensor(P3(16), gsl3(xh, 8), P3(6), OP.mult)
                rv.tensor_tensor(P3(11), gsl3(xh, 8), P3(7), OP.mult)
                rv.tensor_tensor(P3(17), gsl3(xh), P3(7), OP.mult)
                rv.tensor_add(P_(10), P_(10), P_(11))
                rv.tensor_sub(P_(9), P_(16), P_(17))
                vector.drain()
                rv2 = _R(vector, "tts2")
                for r in range(GT):
                    for w in range(2):
                        a = r * CH + w * W
                        rv2.tensor_tensor_scan(
                            P_(4)[:, a:a + W], ones64[:, :], P_(10)[:, a:a + W],
                            0.0, OP.mult, OP.add)
                        rv2.tensor_tensor_scan(
                            P_(5)[:, a:a + W], ones64[:, :], P_(9)[:, a:a + W],
                            0.0, OP.mult, OP.add)
                vector.drain()
                rv3 = _R(vector, "ext")
                last = None
                for src, col in ((2, 0), (3, 1), (4, 2), (5, 3)):
                    sap = P3(src)[:, :, W - 1::W]
                    dap = summ.ap()[:, col * 128:(col + 1) * 128].rearrange(
                        "p (r w) -> p r w", w=16)[:, :, 2 * c:2 * c + 2]
                    last = rv3.tensor_copy(dap, sap)
                vector.drain()
                last.then_inc(sDv, 1)
            cDv.add()

            @block.scalar
            def _(scalar, c=c):
                scalar.wait_ge(sDv, cDv.v)
                for i, s in enumerate((4, 5, 2, 3)):
                    base = (c * 4 + i) * 128
                    scalar.dma_start(out=scanscr[base:base + 128, :],
                                     in_=P_(s)).then_inc((sScr, sScr1)[c % 2], 16)
            (cScr, cScr1)[c % 2].add(64)
            scr_mark.append((cScr, cScr1)[c % 2].v)
        scan_dv = cDv.v
        g_pool.__exit__(None, None, None)
        g_gx.__exit__(None, None, None)
        g_ga.__exit__(None, None, None)
        _FO = {0: (0, 512), 1: (512, 256), 2: (768, 40), 3: (808, 32),
               4: (840, 16), 5: (856, 256)}
        P_ = lambda s: fbuf[:, _FO[s][0]:_FO[s][0] + _FO[s][1]]

        if stop == 5:
            @block.sync
            def _(sync):
                sync.wait_ge(sDv, scan_dv)
                sync.wait_ge(sScr, cScr.v)
                sync.wait_ge(sScr1, cScr1.v)
            g_fb.__exit__(None, None, None)
            g_xh.__exit__(None, None, None)
            return nc

        g_ln1cm = nc.sbuf_tensor("ln1cm", [128, DT * TC], BF16, side="right")
        ln1cm = g_ln1cm.__enter__()
        g_wob = nc.sbuf_tensor("woutb", [128, DT * D], BF16, side="right")
        woutb = g_wob.__enter__()

        @block.gpsimd
        def _(gpsimd):
            gpsimd.wait_ge(sDv, scan_dv)
            gpsimd.wait_ge(sScr, cScr.v)
            gpsimd.wait_ge(sScr1, cScr1.v)
            for k in range(DT):
                gpsimd.dma_start(out=woutb[:, k * D:(k + 1) * D],
                                 in_=w_out[k * 128:(k + 1) * 128, :]
                                 ).then_inc(sWo, 16)
        wout_ld = cWo.add(16 * DT)

        def wsl(buf, base, w):
            return buf.ap()[:, base:base + 128].rearrange(
                "p (r w) -> p r w", w=16)[:, :, w:w + 1]

        @block.vector
        def _(vector):
            vector.wait_ge(sDv, scan_dv)
            vector.wait_ge(sScr, cScr.v)
            vector.wait_ge(sScr1, cScr1.v)
            vector.tensor_mul(P_(0)[:, 0:128], summ[:, 0:128], summ[:, 256:384])
            vector.tensor_mul(P_(0)[:, 128:256], summ[:, 128:256], summ[:, 384:512])
            vector.tensor_mul(P_(0)[:, 256:384], summ[:, 0:128], summ[:, 384:512])
            vector.tensor_mul(P_(0)[:, 384:512], summ[:, 128:256], summ[:, 256:384])
            vector.tensor_sub(summ[:, 256:384], P_(0)[:, 0:128], P_(0)[:, 128:256])
            vector.tensor_add(summ[:, 384:512], P_(0)[:, 256:384], P_(0)[:, 384:512])
            vector.memset(hinb[:, :], 0.0)
            vector.memset(P_(1)[:, 0:128], 1.0)
            vector.memset(P_(1)[:, 128:256], 0.0)
            t0_ = P_(2)[:, 0:8].rearrange("p (r w) -> p r w", w=1)
            t1_ = P_(2)[:, 8:16].rearrange("p (r w) -> p r w", w=1)
            for w in range(1, NW):
                ArP, AiP = wsl(summ, 0, w - 1), wsl(summ, 128, w - 1)
                BrP, BiP = wsl(summ, 256, w - 1), wsl(summ, 384, w - 1)
                HrP, HiP = wsl(hinb, 0, w - 1), wsl(hinb, 128, w - 1)
                Hr, Hi = wsl(hinb, 0, w), wsl(hinb, 128, w)
                cArP, cAiP = wsl(fbuf, 512, w - 1), wsl(fbuf, 640, w - 1)
                cAr, cAi = wsl(fbuf, 512, w), wsl(fbuf, 640, w)
                vector.tensor_tensor(t0_, ArP, HrP, OP.mult)
                vector.tensor_tensor(t1_, AiP, HiP, OP.mult)
                vector.tensor_tensor(t0_, t0_, t1_, OP.subtract)
                vector.tensor_tensor(Hr, t0_, BrP, OP.add)
                vector.tensor_tensor(t0_, ArP, HiP, OP.mult)
                vector.tensor_tensor(t1_, AiP, HrP, OP.mult)
                vector.tensor_tensor(t0_, t0_, t1_, OP.add)
                vector.tensor_tensor(Hi, t0_, BiP, OP.add)
                vector.tensor_tensor(t0_, cArP, ArP, OP.mult)
                vector.tensor_tensor(t1_, cAiP, AiP, OP.mult)
                vector.tensor_tensor(cAr, t0_, t1_, OP.subtract)
                vector.tensor_tensor(t0_, cArP, AiP, OP.mult)
                vector.tensor_tensor(t1_, cAiP, ArP, OP.mult)
                vector.tensor_tensor(cAi, t0_, t1_, OP.add)
            ArL, AiL = wsl(summ, 0, NW - 1), wsl(summ, 128, NW - 1)
            BrL, BiL = wsl(summ, 256, NW - 1), wsl(summ, 384, NW - 1)
            HrL, HiL = wsl(hinb, 0, NW - 1), wsl(hinb, 128, NW - 1)
            cArL, cAiL = wsl(fbuf, 512, NW - 1), wsl(fbuf, 640, NW - 1)
            pk = lambda a, b: P_(3)[:, a:b].rearrange("p (r w) -> p r w", w=1)
            vector.tensor_tensor(t0_, cArL, ArL, OP.mult)
            vector.tensor_tensor(t1_, cAiL, AiL, OP.mult)
            vector.tensor_tensor(pk(0, 8), t0_, t1_, OP.subtract)
            vector.tensor_tensor(t0_, cArL, AiL, OP.mult)
            vector.tensor_tensor(t1_, cAiL, ArL, OP.mult)
            vector.tensor_tensor(pk(8, 16), t0_, t1_, OP.add)
            vector.tensor_tensor(t0_, ArL, HrL, OP.mult)
            vector.tensor_tensor(t1_, AiL, HiL, OP.mult)
            vector.tensor_tensor(t0_, t0_, t1_, OP.subtract)
            vector.tensor_tensor(pk(16, 24), t0_, BrL, OP.add)
            vector.tensor_tensor(t0_, ArL, HiL, OP.mult)
            vector.tensor_tensor(t1_, AiL, HrL, OP.mult)
            vector.tensor_tensor(t0_, t0_, t1_, OP.add)
            vector.tensor_tensor(pk(24, 32), t0_, BiL, OP.add).then_inc(sDv, 1)
        summ_rdy = cDv.add()

        @block.gpsimd
        def _(gpsimd):
            gpsimd.wait_ge(sDv, summ_rdy)
            gpsimd.dma_start(out=cc_in, in_=P_(3)[:, 0:32]).then_inc(sCgI, 16)
            gpsimd.wait_ge(sCgI, cCgI.add(16))
            gpsimd.collective_compute(
                "AllGather", OP.bypass,
                replica_groups=[list(range(NCORES))],
                ins=[cc_in], outs=[cc_out],
            ).then_inc(sCc, 1)
            gpsimd.wait_ge(sCc, cCc.add())
            last = None
            for j in range(NCORES):
                last = gpsimd.dma_start(out=sgath[:, 32 * j:32 * (j + 1)],
                                        in_=cc_out[128 * j:128 * (j + 1), :])
                last.then_inc(sCgI, 16)
            cCgI.add(16 * NCORES)
        carry_ld = cCgI.v

        @block.vector
        def _(vector):
            vector.wait_ge(sCgI, carry_ld)
            vector.memset(P_(4)[:, 0:16], 0.0)
            hr = P_(4)[:, 0:8].rearrange("p (r w) -> p r w", w=1)
            hi = P_(4)[:, 8:16].rearrange("p (r w) -> p r w", w=1)
            t0_ = P_(2)[:, 0:8].rearrange("p (r w) -> p r w", w=1)
            t1_ = P_(2)[:, 8:16].rearrange("p (r w) -> p r w", w=1)
            t2_ = P_(2)[:, 16:24].rearrange("p (r w) -> p r w", w=1)
            t3_ = P_(2)[:, 24:32].rearrange("p (r w) -> p r w", w=1)
            t4_ = P_(2)[:, 32:40].rearrange("p (r w) -> p r w", w=1)
            for j in range(NCORES):
                g = lambda a, b: sgath.ap()[:, 32 * j + a:32 * j + b].rearrange(
                    "p (r w) -> p r w", w=1)
                m = consts[:, 48 + j:49 + j]
                vector.tensor_scalar(t2_, g(0, 8), -1.0, None, OP.add)
                vector.tensor_scalar(t2_, t2_, m, 1.0, OP.mult, OP.add)
                vector.tensor_scalar(t3_, g(8, 16), m, None, OP.mult)
                vector.tensor_tensor(t0_, t2_, hr, OP.mult)
                vector.tensor_tensor(t1_, t3_, hi, OP.mult)
                vector.tensor_tensor(t0_, t0_, t1_, OP.subtract)
                vector.tensor_tensor(t1_, t2_, hi, OP.mult)
                vector.tensor_tensor(t4_, t3_, hr, OP.mult)
                vector.tensor_tensor(hi, t1_, t4_, OP.add)
                vector.tensor_scalar(t1_, g(16, 24), m, None, OP.mult)
                vector.tensor_tensor(hr, t0_, t1_, OP.add)
                vector.tensor_scalar(t1_, g(24, 32), m, None, OP.mult)
                vector.tensor_tensor(hi, hi, t1_, OP.add)
            hcr = P_(4)[:, 0:8][:, :, None].broadcast_to([128, 8, 16])
            hci = P_(4)[:, 8:16][:, :, None].broadcast_to([128, 8, 16])
            cAr = P_(1)[:, 0:128].rearrange("p (r w) -> p r w", w=16)
            cAi = P_(1)[:, 128:256].rearrange("p (r w) -> p r w", w=16)
            hR = hinb.ap()[:, 0:128].rearrange("p (r w) -> p r w", w=16)
            hI = hinb.ap()[:, 128:256].rearrange("p (r w) -> p r w", w=16)
            p5a = P_(5)[:, 0:128].rearrange("p (r w) -> p r w", w=16)
            p5b = P_(5)[:, 128:256].rearrange("p (r w) -> p r w", w=16)
            vector.tensor_tensor(p5a, cAr, hcr, OP.mult)
            vector.tensor_tensor(p5b, cAi, hci, OP.mult)
            vector.tensor_tensor(p5a, p5a, p5b, OP.subtract)
            vector.tensor_tensor(hR, hR, p5a, OP.add)
            vector.tensor_tensor(p5a, cAr, hci, OP.mult)
            vector.tensor_tensor(p5b, cAi, hcr, OP.mult)
            vector.tensor_tensor(p5a, p5a, p5b, OP.add)
            vector.tensor_tensor(hI, hI, p5a, OP.add).then_inc(sDv, 1)
        hin_rdy = cDv.add()

        if stop == 6:
            @block.sync
            def _(sync):
                sync.wait_ge(sDv, hin_rdy)
            g_fb.__exit__(None, None, None)
            g_xh.__exit__(None, None, None)
            return nc

        if debug:
            @block.sync
            def _(sync):
                sync.wait_ge(sDv, hin_rdy)
                sync.dma_start(out=dbg["dbg_hin"], in_=hinb[:, :]).then_inc(sSt, 16)
                sync.dma_start(out=dbg["dbg_summ"], in_=summ[:, :]).then_inc(sSt, 16)
                sync.dma_start(out=dbg["dbg_scr"], in_=scanscr).then_inc(sSt, 16)
            cSt.add(48)

        g_fb.__exit__(None, None, None)
        g_xh.__exit__(None, None, None)

        g_hb = nc.sbuf_tensor("hb", [128, DT * TC], BF16, side="right")
        hb = g_hb.__enter__()
        g_corr = nc.sbuf_tensor("corr", [128, 10 * 1024], F32, side="right")
        corr = g_corr.__enter__()
        C_ = lambda s: corr[:, s * 1024:(s + 1) * 1024]
        C3 = lambda s, t=CH: corr.ap()[:, s * 1024:(s + 1) * 1024].rearrange(
            "p (r t) -> p r t", t=t)

        for c in range(NCH):
            b = 2 + (c % 2) * 4
            sCo = (sCo0, sCo1)[c % 2]

            @block.sync
            def _(sync, c=c, b=b, sCo=sCo):
                sync.wait_ge(sDv, hin_rdy)
                if c == 0:
                    sync.wait_ge(sWsT, cWsT.v)
                    sync.wait_ge(sWsP, cWsP.v)
                if c >= 2:
                    sync.wait_ge(sDv, cDv.v)
                for i in range(4):
                    base = (c * 4 + i) * 128
                    sync.dma_start(out=C_(b + i), in_=scanscr[base:base + 128, :]
                                   ).then_inc(sCo, 16)
            co_ld = cCo[c % 2].add(64)

            @block.vector
            def _(vector, c=c, b=b, sCo=sCo, co_ld=co_ld):
                vector.wait_ge(sCo, co_ld)
                Sr, Si, cP, sP = b, b + 1, b + 2, b + 3
                r4 = lambda s: C3(s).rearrange("p r (w t) -> p r w t", t=W)
                hR = hinb.ap()[:, 0:128].rearrange("p (r w) -> p r w", w=16)[
                    :, :, 2 * c:2 * c + 2][:, :, :, None].broadcast_to(
                    [128, 8, 2, W])
                hI = hinb.ap()[:, 128:256].rearrange("p (r w) -> p r w", w=16)[
                    :, :, 2 * c:2 * c + 2][:, :, :, None].broadcast_to(
                    [128, 8, 2, W])
                vector.tensor_tensor(r4(Sr), r4(Sr), hR, OP.add)
                vector.tensor_tensor(r4(Si), r4(Si), hI, OP.add)
                vector.tensor_mul(C_(0), C_(cP), C_(Sr))
                vector.tensor_mul(C_(1), C_(sP), C_(Si))
                hre = hb.ap().rearrange("p (r t) -> p r t", t=TC)[
                    :, 0:8, c * CH:(c + 1) * CH]
                vector.tensor_tensor(hre, C3(0), C3(1), OP.subtract)
                vector.tensor_mul(C_(0), C_(cP), C_(Si))
                vector.tensor_mul(C_(1), C_(sP), C_(Sr))
                him = hb.ap().rearrange("p (r t) -> p r t", t=TC)[
                    :, 8:16, c * CH:(c + 1) * CH]
                vector.tensor_tensor(him, C3(0), C3(1), OP.add).then_inc(sDv, 1)
            cDv.add()
        h_rdy = cDv.v
        g_corr.__exit__(None, None, None)

        if stop == 7:
            @block.sync
            def _(sync):
                sync.wait_ge(sDv, h_rdy)
            return nc

        if debug:
            @block.sync
            def _(sync):
                sync.wait_ge(sDv, h_rdy)
                sync.dma_start(out=dbg["dbg_h"], in_=hb[:, :]).then_inc(sSt, 16)
            cSt.add(16)

        with contextlib.ExitStack() as hctx:
            Eh = hctx.enter_context
            b_out_b = Eh(nc.sbuf_tensor("b_out_b", [128, D], F32))
            xres = Eh(nc.sbuf_tensor("xres", [128, 2 * D], F32))
            r1b = Eh(nc.sbuf_tensor("r1b", [128, 2 * D], F32))
            ln1b = Eh(nc.sbuf_tensor("ln1b", [128, D], F32))
            ln1bb = Eh(nc.sbuf_tensor("ln1bb", [128, 2 * D], BF16))
            sq_s = Eh(nc.sbuf_tensor("sq_s", [128, D], BF16))
            stat = Eh(nc.sbuf_tensor("stat", [128, 8], F32))
            pH = Eh(nc.psum_tensor("pH", [128, 1024], F32))
            pT = Eh(nc.psum_tensor("pT", [128, 512], BF16))

            @block.sync
            def _(sync):
                sync.dma_start(out=b_out_b[:, :], in_=b_out_bc).then_inc(sBo, 16)

            cast_mark = []
            tp_mark = {}
            tp_ev = []

            def emit_ln1_tp(tt):
                for grp in range(4):
                    @block.tensor
                    def _(tensor, tt=tt, grp=grp,
                          ev=(tp_ev[-1] if tp_ev else None)):
                        tensor.wait_ge(sLbb, cast_mark[tt])
                        if ev is not None:
                            tensor.wait_ge(sDv, ev)
                        last = None
                        for j in range(4):
                            k = grp * 4 + j
                            last = tensor.transpose(
                                pT[:, j * 128:(j + 1) * 128],
                                ln1bb[:, (tt % 2) * D + k * 128:
                                      (tt % 2) * D + (k + 1) * 128],
                                ident[:, :])
                        last.then_inc(sPe, 1)
                    cPe.add()

                    @block.vector
                    def _(vector, tt=tt, grp=grp):
                        vector.wait_ge(sPe, cPe.v)
                        dst = ln1cm.ap().rearrange("p (j t) -> p j t", t=TC)[
                            :, grp * 4:(grp + 1) * 4, tt * 128:(tt + 1) * 128]
                        src = pT.ap().rearrange("p (j t) -> p j t", t=128)
                        vector.tensor_copy(dst, src).then_inc(sDv, 1)
                    tp_ev.append(cDv.add())
                tp_mark[tt] = cPe.v

            nH = 0
            for tt in range(NTT):
                bo = (tt % 2) * D

                sXr = (sXr0, sXr1)[tt % 2]

                @block.sync
                def _(sync, tt=tt, bo=bo, sXr=sXr):
                    if tt >= 2:
                        sync.wait_ge(sDv, cDv.v)
                    sync.dma_start(out=xres[:, bo:bo + D],
                                   in_=x_in[tt * 128:(tt + 1) * 128, :]
                                   ).then_inc(sXr, 16)
                xr_ld = cXr[tt % 2].add(16)

                @block.vector
                def _(vector, bo=bo, xr_ld=xr_ld, sXr=sXr, tt=tt):
                    vector.wait_ge(sXr, xr_ld)
                    if tt == 0:
                        vector.wait_ge(sBo, 16)
                    vector.tensor_add(xres[:, bo:bo + D], xres[:, bo:bo + D],
                                      b_out_b[:, :]).then_inc(sDv, 1)
                cDv.add()
                xres_rdy = cDv.v

                for cs in range(4):
                    po = (nH % 2) * 512

                    @block.tensor
                    def _(tensor, tt=tt, cs=cs, po=po, nH=nH, xres_rdy=xres_rdy):
                        tensor.wait_ge(sDv, h_rdy)
                        tensor.wait_ge(sWo, wout_ld)
                        tensor.wait_ge(sDv, xres_rdy)
                        if nH >= 2:
                            tensor.wait_ge(sDv, cDv.v - 1)
                        last = None
                        for k in range(DT):
                            last = tensor.matmul(
                                pH[:, po:po + 512],
                                hb[:, k * TC + tt * 128: k * TC + (tt + 1) * 128],
                                woutb[:, k * D + cs * 512: k * D + (cs + 1) * 512],
                                start=(k == 0), stop=(k == DT - 1))
                        last.then_inc(sPe, 1)
                    cPe.add()

                    @block.vector
                    def _(vector, cs=cs, po=po, bo=bo):
                        vector.wait_ge(sPe, cPe.v)
                        vector.scalar_tensor_tensor(
                            r1b[:, bo + cs * 512: bo + (cs + 1) * 512],
                            pH[:, po:po + 512], 1.0,
                            xres[:, bo + cs * 512: bo + (cs + 1) * 512],
                            OP.mult, OP.add).then_inc(sDv, 1)
                    cDv.add()
                    nH += 1

                @block.vector
                def _(vector, bo=bo):
                    vector.wait_ge(sDv, cDv.v)
                    vector.tensor_reduce(stat[:, 0:1], r1b[:, bo:bo + D], AX,
                                         OP.add).then_inc(sDv, 1)
                cDv.add()

                @block.scalar
                def _(scalar, bo=bo):
                    scalar.wait_ge(sDv, cDv.v)
                    scalar.activation(sq_s[:, :], r1b[:, bo:bo + D], AF.Square,
                                      accum_out=stat[:, 1:2]).then_inc(sSc, 1)
                cSc.add()

                @block.vector
                def _(vector):
                    vector.wait_ge(sSc, cSc.v)
                    vector.tensor_scalar_mul(stat[:, 2:3], stat[:, 0:1], 1.0 / D)
                    vector.tensor_scalar_mul(stat[:, 3:4], stat[:, 1:2], 1.0 / D)
                    vector.tensor_mul(stat[:, 4:5], stat[:, 2:3], stat[:, 2:3])
                    vector.tensor_sub(stat[:, 3:4], stat[:, 3:4], stat[:, 4:5])
                    vector.tensor_scalar_add(stat[:, 3:4], stat[:, 3:4], LN_EPS
                                             ).then_inc(sDv, 1)
                cDv.add()

                @block.scalar
                def _(scalar):
                    scalar.wait_ge(sDv, cDv.v)
                    scalar.activation(stat[:, 5:6], stat[:, 3:4], AF.Ln)
                    scalar.activation(stat[:, 5:6], stat[:, 5:6], AF.Exp,
                                      scale=-0.5).then_inc(sSc, 1)
                cSc.add()

                @block.vector
                def _(vector, bo=bo):
                    vector.wait_ge(sSc, cSc.v)
                    vector.wait_ge(sL1s, cL1s.v)
                    vector.wait_ge(sLbb, cLbb.v)
                    vector.tensor_scalar(ln1b[:, :], r1b[:, bo:bo + D],
                                         stat[:, 2:3], stat[:, 5:6],
                                         OP.subtract, OP.mult).then_inc(sDv, 1)
                cDv.add()
                ln1_rdy = cDv.v

                @block.sync
                def _(sync, tt=tt, ln1_rdy=ln1_rdy):
                    sync.wait_ge(sDv, ln1_rdy)
                    sync.dma_start(out=ln1scr[tt * 128:(tt + 1) * 128, :],
                                   in_=ln1b[:, :]).then_inc(sL1s, 16)
                    if debug:
                        sync.dma_start(out=dbg["dbg_ln1"][tt * 128:(tt + 1) * 128, :],
                                       in_=ln1b[:, :]).then_inc(sSt, 16)
                cL1s.add(16)
                if debug:
                    cSt.add(16)

                @block.gpsimd
                def _(gpsimd, tt=tt, ln1_rdy=ln1_rdy,
                      tm=(tp_mark[tt - 2] if tt >= 2 else 0)):
                    gpsimd.wait_ge(sDv, ln1_rdy)
                    if tt >= 2:
                        gpsimd.wait_ge(sPe, tm)
                    gpsimd.dma_start(out=ln1bb[:, (tt % 2) * D:(tt % 2 + 1) * D],
                                     in_=ln1b[:, :]).then_inc(sLbb, 16)
                cast_mark.append(cLbb.add(16))

                if tt > 0:
                    emit_ln1_tp(tt - 1)
            emit_ln1_tp(NTT - 1)
            ln1_all = cDv.v

        if stop == 8:
            @block.sync
            def _(sync):
                sync.wait_ge(sDv, ln1_all)
                sync.wait_ge(sLd, cLd.v)
            return nc

        g_hb.__exit__(None, None, None)
        g_wob.__exit__(None, None, None)
        with contextlib.ExitStack() as ictx:
            Ei = ictx.enter_context
            hhres = Ei(nc.sbuf_tensor("hhres", [128, NPAIR * TC], BF16))
            gw = Ei(nc.sbuf_tensor("gw", [128, 2 * D], BF16))
            uw = Ei(nc.sbuf_tensor("uw", [128, 2 * D], BF16))
            silu_s = Ei(nc.sbuf_tensor("silu_s", [128, TC], F32))
            sqh = Ei(nc.sbuf_tensor("sqh", [128, 2 * TC], BF16))
            ssrow = Ei(nc.sbuf_tensor("ssrow", [128, TC], F32))
            sscol = Ei(nc.sbuf_tensor("sscol", [128, 16], F32))
            ssb = Ei(nc.sbuf_tensor("ssb", [128, TC], BF16))
            gctx = contextlib.ExitStack()
            Eg = gctx.enter_context
            pG = Eg(nc.psum_tensor("pG", [128, 1024], F32))
            pU = Eg(nc.psum_tensor("pU", [128, 1024], F32))
            pS = Eg(nc.psum_tensor("pS", [1, 1024], F32))
            pR = Eg(nc.psum_tensor("pR", [128, 128], BF16))

            mm_mark = {}
            sq_mark = {}
            for i in range(NPAIR):
                bo = (i % 2) * D
                ho = (i % 2) * TC
                sGu = (sGu0, sGu1)[i % 2]

                @block.sync
                def _(sync, i=i, bo=bo, sGu=sGu,
                      mw=(mm_mark[i - 2] if i >= 2 else 0)):
                    sync.wait_ge(sCc, wq_ag1)
                    if i >= 2:
                        sync.wait_ge(sPe, mw)
                    sync.dma_start(out=gw[:, bo:bo + D],
                                   in_=wg_tile(i % 8, 2 * (i // 8))
                                   ).then_inc(sGu, 16)
                    sync.dma_start(out=uw[:, bo:bo + D],
                                   in_=wg_tile(i % 8, 2 * (i // 8) + 1)
                                   ).then_inc(sGu, 16)
                gu_ld = cGu[i % 2].add(32)

                for ts_ in range(2):
                    po = ts_ * 512

                    @block.tensor
                    def _(tensor, i=i, ts_=ts_, po=po, bo=bo, gu_ld=gu_ld,
                          sGu=sGu):
                        tensor.wait_ge(sGu, gu_ld)
                        tensor.wait_ge(sDv, ln1_all)
                        if i > 0:
                            tensor.wait_ge(sDv, cDv.v - 1)
                        lastg = None
                        for k in range(DT):
                            tensor.matmul(
                                pG[:, po:po + 512],
                                gw[:, bo + k * 128: bo + (k + 1) * 128],
                                ln1cm[:, k * TC + ts_ * 512:
                                      k * TC + ts_ * 512 + 512],
                                start=(k == 0), stop=(k == DT - 1))
                        for k in range(DT):
                            lastg = tensor.matmul(
                                pU[:, po:po + 512],
                                uw[:, bo + k * 128: bo + (k + 1) * 128],
                                ln1cm[:, k * TC + ts_ * 512:
                                      k * TC + ts_ * 512 + 512],
                                start=(k == 0), stop=(k == DT - 1))
                        lastg.then_inc(sPe, 1)
                    cPe.add()

                    @block.scalar
                    def _(scalar, ts_=ts_, po=po):
                        scalar.wait_ge(sPe, cPe.v)
                        scalar.activation(silu_s[:, po:po + 512],
                                          pG[:, po:po + 512], AF.Silu
                                          ).then_inc(sSc, 1)
                    cSc.add()

                    @block.vector
                    def _(vector, i=i, ts_=ts_, po=po):
                        vector.wait_ge(sSc, cSc.v)
                        vector.tensor_mul(hhres[:, i * TC + po: i * TC + po + 512],
                                          silu_s[:, po:po + 512],
                                          pU[:, po:po + 512]).then_inc(sDv, 1)
                    cDv.add()

                    @block.scalar
                    def _(scalar, i=i, ts_=ts_, po=po):
                        scalar.wait_ge(sDv, cDv.v)
                        scalar.activation(sqh[:, ((i % 2) * 2 + ts_) * 512:
                                              ((i % 2) * 2 + ts_) * 512 + 512],
                                          hhres[:, i * TC + po: i * TC + po + 512],
                                          AF.Square).then_inc(sSc, 1)
                    cSc.add()
                mm_mark[i] = cPe.v
                sq_mark[i] = cSc.v

                if i > 0:
                    @block.tensor
                    def _(tensor, i=i, sw=sq_mark[i - 1]):
                        tensor.wait_ge(sSc, sw)
                        for ts_ in range(2):
                            tensor.matmul(pS[:, ts_ * 512:(ts_ + 1) * 512],
                                          onesb[:, :],
                                          sqh[:, (((i - 1) % 2) * 2 + ts_) * 512:
                                              (((i - 1) % 2) * 2 + ts_) * 512 + 512],
                                          start=(i - 1 == 0), stop=False,
                                          skip_group_check=True).then_inc(sPe, 1)
                            cPe.add()

            @block.tensor
            def _(tensor):
                tensor.wait_ge(sSc, cSc.v)
                last = None
                for ts_ in range(2):
                    last = tensor.matmul(
                        pS[:, ts_ * 512:(ts_ + 1) * 512], onesb[:, :],
                        sqh[:, (((NPAIR - 1) % 2) * 2 + ts_) * 512:
                            (((NPAIR - 1) % 2) * 2 + ts_) * 512 + 512],
                        start=False, stop=True, skip_group_check=True)
                last.then_inc(sPe, 1)
            cPe.add()
            gu_done_pe = cPe.v

            @block.vector
            def _(vector):
                vector.wait_ge(sPe, gu_done_pe)
                vector.memset(ssrow[:, :], 0.0)
                vector.tensor_copy(ssrow[0:1, :], pS[:, :])
                vector.tensor_copy(ssb[:, :], ssrow[:, :]).then_inc(sDv, 1)
            cDv.add()
            ss_rdy = cDv.v

            for tt in range(NTT):
                @block.tensor
                def _(tensor, tt=tt, ss_rdy=ss_rdy):
                    tensor.wait_ge(sDv, ss_rdy)
                    if tt > 0:
                        tensor.wait_ge(sDv, cDv.v)
                    tensor.transpose(pR[:, :], ssb[:, tt * 128:(tt + 1) * 128],
                                     ident[:, :]).then_inc(sPe, 1)
                cPe.add()

                @block.vector
                def _(vector, tt=tt):
                    vector.wait_ge(sPe, cPe.v)
                    vector.tensor_copy(sscol[:, tt:tt + 1], pR[:, 0:1]
                                       ).then_inc(sDv, 1)
                cDv.add()

            @block.vector
            def _(vector):
                vector.wait_ge(sDv, cDv.v)
                vector.tensor_scalar(sscol[:, 0:8], sscol[:, 0:8], 1.0 / H,
                                     RMS_EPS, OP.mult, OP.add).then_inc(sDv, 1)
            cDv.add()

            @block.scalar
            def _(scalar):
                scalar.wait_ge(sDv, cDv.v)
                scalar.activation(sscol[:, 8:16], sscol[:, 0:8], AF.Ln)
                scalar.activation(sscol[:, 8:16], sscol[:, 8:16], AF.Exp,
                                  scale=-0.5).then_inc(sSc, 1)
            cSc.add()
            inv_rdy = cSc.v

            if stop == 9:
                @block.sync
                def _(sync):
                    sync.wait_ge(sSc, inv_rdy)
                return nc

            if debug:
                @block.sync
                def _(sync):
                    sync.wait_ge(sSc, inv_rdy)
                    sync.dma_start(out=dbg["dbg_inv"], in_=sscol[:, :]
                                   ).then_inc(sSt, 16)
                cSt.add(16)

            gctx.close()
            g_ln1cm.__exit__(None, None, None)
            pwb = Ei(nc.sbuf_tensor("pwb", [128, 4 * 512], BF16))
            lnrb = Ei(nc.sbuf_tensor("lnrb", [128, NTT * 512], F32))
            pP = Ei(nc.psum_tensor("pP", [128, 4096], F32))

            for cs in range(4):
                @block.scalar
                def _(scalar, cs=cs):
                    if cs == 0:
                        scalar.wait_ge(sL1s, cL1s.v)
                    else:
                        scalar.wait_ge(sDv, cDv.v)
                    for tt in range(NTT):
                        scalar.dma_start(
                            out=lnrb[:, tt * 512:(tt + 1) * 512],
                            in_=ln1scr[tt * 128:(tt + 1) * 128,
                                       cs * 512:(cs + 1) * 512]
                        ).then_inc(sLnr, 16)
                lnr_all = cLnr.add(16 * NTT)

                for k in range(NPK):
                    po = (k % 4) * 512
                    sPj = (sPj0, sPj1, sPj2, sPj3)[k % 4]

                    @block.sync
                    def _(sync, cs=cs, k=k, po=po, sPj=sPj):
                        if cs == 0 and k == 0:
                            sync.wait_ge(sCc, wq_ag)
                            sync.wait_ge(sL1s, cL1s.v)
                        if k >= 4 or cs > 0:
                            sync.wait_ge(sPe, cPe.v - 3)
                        src = wg_tile(k % 8, 16 + k // 8)
                        sync.dma_start(out=pwb[:, po:po + 512],
                                       in_=src[:, cs * 512:(cs + 1) * 512]
                                       ).then_inc(sPj, 16)
                    pj_ld = cPj[k % 4].add(16)

                    @block.tensor
                    def _(tensor, cs=cs, k=k, po=po, pj_ld=pj_ld,
                          sPj=sPj):
                        tensor.wait_ge(sPj, pj_ld)
                        if k == 0 and cs == 0:
                            tensor.wait_ge(sSc, inv_rdy)
                        if k == 0 and cs > 0:
                            tensor.wait_ge(sDv, cDv.v)
                        last = None
                        for tt in range(NTT):
                            last = tensor.matmul(
                                pP[:, tt * 512:(tt + 1) * 512],
                                hhres[:, k * TC + tt * 128: k * TC + (tt + 1) * 128],
                                pwb[:, po:po + 512],
                                start=(k == 0), stop=(k == NPK - 1),
                                skip_group_check=True)
                        last.then_inc(sPe, 1)
                    cPe.add()

                for tt in range(NTT):
                    eo = (tt % 2) * 512

                    @block.vector
                    def _(vector, cs=cs, tt=tt, eo=eo, lnr_all=lnr_all):
                        vector.wait_ge(sLnr, lnr_all)
                        vector.wait_ge(sPe, cPe.v)
                        vector.wait_ge(sSc, inv_rdy)
                        vector.wait_ge((sOsc, sOsc1)[tt % 2],
                                       (cOsc, cOsc1)[tt % 2].v)
                        vector.scalar_tensor_tensor(
                            silu_s[:, eo:eo + 512],
                            pP[:, tt * 512:(tt + 1) * 512],
                            sscol[:, 8 + tt:9 + tt],
                            lnrb[:, tt * 512:(tt + 1) * 512],
                            OP.mult, OP.add).then_inc(sDv, 1)
                    cDv.add()

                    @block.sync
                    def _(sync, cs=cs, tt=tt, eo=eo):
                        sync.wait_ge(sDv, cDv.v)
                        sync.dma_start(
                            out=outscr[tt * 128:(tt + 1) * 128,
                                       cs * 512:(cs + 1) * 512],
                            in_=silu_s[:, eo:eo + 512]
                        ).then_inc((sOsc, sOsc1)[tt % 2], 16)
                    (cOsc, cOsc1)[tt % 2].add(16)
            proj_done = cOsc.v
            proj_done1 = cOsc1.v

            if stop == 10:
                @block.sync
                def _(sync):
                    sync.wait_ge(sOsc, proj_done)
                    sync.wait_ge(sOsc1, proj_done1)
                return nc

        with contextlib.ExitStack() as jctx:
            Ej = jctx.enter_context
            ln2b = Ej(nc.sbuf_tensor("ln2b", [128, 2 * D], F32))
            sq2 = Ej(nc.sbuf_tensor("sq2", [128, D], BF16))
            st2 = Ej(nc.sbuf_tensor("st2", [128, 8], F32))

            for tt in range(NTT):
                bo = (tt % 2) * D
                sJl = (sJl0, sJl1)[tt % 2]

                @block.sync
                def _(sync, tt=tt, bo=bo, sJl=sJl):
                    sync.wait_ge(sOsc, proj_done)
                    sync.wait_ge(sOsc1, proj_done1)
                    if tt >= 2:
                        sync.wait_ge(sSt, cSt.v)
                    sync.dma_start(out=ln2b[:, bo:bo + D],
                                   in_=outscr[tt * 128:(tt + 1) * 128, :]
                                   ).then_inc(sJl, 16)
                o_ld = cJl[tt % 2].add(16)

                @block.vector
                def _(vector, bo=bo, o_ld=o_ld, sJl=sJl):
                    vector.wait_ge(sJl, o_ld)
                    vector.tensor_reduce(st2[:, 0:1], ln2b[:, bo:bo + D], AX,
                                         OP.add).then_inc(sDv, 1)
                cDv.add()

                @block.scalar
                def _(scalar, bo=bo):
                    scalar.wait_ge(sDv, cDv.v)
                    scalar.activation(sq2[:, :], ln2b[:, bo:bo + D], AF.Square,
                                      accum_out=st2[:, 1:2]).then_inc(sSc, 1)
                cSc.add()

                @block.vector
                def _(vector):
                    vector.wait_ge(sSc, cSc.v)
                    vector.tensor_scalar_mul(st2[:, 2:3], st2[:, 0:1], 1.0 / D)
                    vector.tensor_scalar_mul(st2[:, 3:4], st2[:, 1:2], 1.0 / D)
                    vector.tensor_mul(st2[:, 4:5], st2[:, 2:3], st2[:, 2:3])
                    vector.tensor_sub(st2[:, 3:4], st2[:, 3:4], st2[:, 4:5])
                    vector.tensor_scalar_add(st2[:, 3:4], st2[:, 3:4], LN_EPS
                                             ).then_inc(sDv, 1)
                cDv.add()

                @block.scalar
                def _(scalar):
                    scalar.wait_ge(sDv, cDv.v)
                    scalar.activation(st2[:, 5:6], st2[:, 3:4], AF.Ln)
                    scalar.activation(st2[:, 5:6], st2[:, 5:6], AF.Exp,
                                      scale=-0.5).then_inc(sSc, 1)
                cSc.add()

                @block.vector
                def _(vector, bo=bo):
                    vector.wait_ge(sSc, cSc.v)
                    vector.tensor_scalar(ln2b[:, bo:bo + D], ln2b[:, bo:bo + D],
                                         st2[:, 2:3], st2[:, 5:6],
                                         OP.subtract, OP.mult).then_inc(sDv, 1)
                cDv.add()

                @block.sync
                def _(sync, tt=tt, bo=bo):
                    sync.wait_ge(sDv, cDv.v)
                    sync.dma_start(out=out_t[tt * 128:(tt + 1) * 128, :],
                                   in_=ln2b[:, bo:bo + D]).then_inc(sSt, 16)
                cSt.add(16)

            @block.sync
            def _(sync):
                sync.wait_ge(sSt, cSt.v)

    return nc


def _stage_inputs(inputs):
    f32 = np.float32
    x = np.ascontiguousarray(np.asarray(inputs["L0_attn_x"], f32).reshape(8192, D))
    W_in = np.ascontiguousarray(np.asarray(inputs["L0_attn_W_in"], f32))
    b_in = np.asarray(inputs["L0_attn_b_in"], f32)
    W_gx = np.ascontiguousarray(np.asarray(inputs["L0_attn_W_gx"], f32)
                                .reshape(GT, 2, 128, 128))
    W_ga = np.ascontiguousarray(np.asarray(inputs["L0_attn_W_ga"], f32)
                                .reshape(GT, 2, 128, 128))
    b_gx = np.asarray(inputs["L0_attn_b_gx"], f32)
    b_ga = np.asarray(inputs["L0_attn_b_ga"], f32)
    a_real = np.asarray(inputs["L0_attn_a_real"], f32)
    a_imag = np.asarray(inputs["L0_attn_a_imag"], f32)
    W_out = np.ascontiguousarray(np.asarray(inputs["L0_attn_W_out"], f32))
    b_out = np.asarray(inputs["L0_attn_b_out"], f32)
    wgu = np.asarray(inputs["L0_ffn_w_gate_up"], f32)
    wpj = np.asarray(inputs["L0_ffn_w_proj"], f32)

    common = dict(
        w_in=W_in,
        b_in_rs=np.ascontiguousarray(b_in.reshape(DT, 128).T),
        w_gx=W_gx, w_ga=W_ga,
        b_gx_rs=np.ascontiguousarray(b_gx.T),
        b_ga_rs=np.ascontiguousarray(b_ga.T),
        ar_rs=np.ascontiguousarray(a_real.reshape(GT, 128).T),
        ai_rs=np.ascontiguousarray(a_imag.reshape(GT, 128).T),
        w_out=W_out,
        b_out_bc=np.ascontiguousarray(np.broadcast_to(b_out, (128, D))),
        ident=np.eye(128, dtype=ml_dtypes.bfloat16),
    )

    in_maps = []
    for c in range(NCORES):
        gu_rows = []
        for j in range(8):
            i = 8 * j + c
            gu_rows.append(wgu[128 * i:128 * (i + 1)])
            gu_rows.append(wgu[H + 128 * i:H + 128 * (i + 1)])
        wgu_shard = np.ascontiguousarray(np.stack(gu_rows))
        pj_blocks = []
        for m in range(8):
            k = 8 * m + c
            pj_blocks.append(wpj[:, 128 * k:128 * (k + 1)].reshape(16, 128, 128))
        wpj_shard = np.ascontiguousarray(np.stack(pj_blocks))
        m = dict(common)
        m.update(
            x=np.ascontiguousarray(x[TC * c:TC * (c + 1)]),
            wgu_sh=wgu_shard, wpj_sh=wpj_shard,
            mask_lt=np.ascontiguousarray(np.broadcast_to(
                (np.arange(NCORES) < c).astype(f32), (128, NCORES))),
        )
        in_maps.append(m)
    return in_maps


def kernel(**inputs):
    in_maps = _stage_inputs(inputs)
    nc = build_nc(debug=_DEBUG)
    res = run_bass_kernel_spmd(nc, in_maps, core_ids=list(range(NCORES)))
    out = np.concatenate([np.asarray(r["out"]) for r in res.results], axis=0)
    return out.reshape(1, 8192, D)


# version 14
# speedup vs baseline: 1.0330x; 1.0330x over previous
import contextlib
import numpy as np
import ml_dtypes

import concourse.bass as bass
import concourse.mybir as mybir
from concourse.bass_utils import run_bass_kernel_spmd

F32 = mybir.dt.float32
BF16 = mybir.dt.bfloat16
AF = mybir.ActivationFunctionType
OP = mybir.AluOpType
AX = mybir.AxisListType.X

NCORES = 8
TC = 1024
NTT = 8
D = 2048
DT = 16
GT = 8
H = 8192
NPAIR = 64
NPK = 64
W = 64
CH = 128
NCH = 8
NW = 16

MAGIC = 12582912.0
TWO_PI = 6.283185307179586
INV_2PI = 0.15915494309189535
HALF_PI = 1.5707963267948966
LNH = -0.34657359027997264
TINY = 1e-35
LN_EPS = 1e-5
RMS_EPS = 1.1920929e-07

_DEBUG = False
BARRIER_AG = False
RACE_DETECT = False
RAW_GROUPS = {"lrli", "tts1", "rr", "w1w2", "big", "tts2", "ext"}
A_LEVEL = 3


class _SerEngine:

    _PASS = {"wait_ge", "drain", "sem_inc", "nop", "engine_nop", "wait_op"}
    _HAZARD = {"tensor_reduce", "reduce_sum", "reduce_max", "reciprocal"}

    def __init__(self, eng, always_drain):
        self._e = eng
        self._always = always_drain

    def __getattr__(self, n):
        f = getattr(self._e, n)
        if n in self._PASS or not callable(f):
            return f
        def wrap(*a, **k):
            r = f(*a, **k)
            if RACE_DETECT or self._always:
                self._e.drain()
            return r
        return wrap


class Ctr:
    def __init__(self):
        self.v = 0

    def add(self, n=1):
        self.v += n
        return self.v


def build_nc(debug=False, stop=None, skip_prepass=False, skip_ag=False):
    def _R(eng, grp="x"):
        return eng._e if (grp in RAW_GROUPS and not RACE_DETECT) else eng

    nc = bass.Bass("TRN2", target_bir_lowering=False, debug=False,
                   detect_race_conditions=RACE_DETECT)

    x_in = nc.dram_tensor("x", [TC, D], F32, kind="ExternalInput").ap()
    w_in = nc.dram_tensor("w_in", [D, D], F32, kind="ExternalInput").ap()
    b_in_rs = nc.dram_tensor("b_in_rs", [128, DT], F32, kind="ExternalInput").ap()
    w_gx = nc.dram_tensor("w_gx", [GT, 2, 128, 128], F32, kind="ExternalInput").ap()
    w_ga = nc.dram_tensor("w_ga", [GT, 2, 128, 128], F32, kind="ExternalInput").ap()
    b_gx_rs = nc.dram_tensor("b_gx_rs", [128, GT], F32, kind="ExternalInput").ap()
    b_ga_rs = nc.dram_tensor("b_ga_rs", [128, GT], F32, kind="ExternalInput").ap()
    ar_rs = nc.dram_tensor("ar_rs", [128, GT], F32, kind="ExternalInput").ap()
    ai_rs = nc.dram_tensor("ai_rs", [128, GT], F32, kind="ExternalInput").ap()
    w_out = nc.dram_tensor("w_out", [D, D], F32, kind="ExternalInput").ap()
    b_out_bc = nc.dram_tensor("b_out_bc", [128, D], F32, kind="ExternalInput").ap()
    wgu_sh = nc.dram_tensor("wgu_sh", [16, 128, D], F32, kind="ExternalInput").ap()
    wpj_sh = nc.dram_tensor("wpj_sh", [8, 16, 128, 128], F32,
                            kind="ExternalInput").ap()
    mask_lt = nc.dram_tensor("mask_lt", [128, NCORES], F32,
                             kind="ExternalInput").ap()
    ident_in = nc.dram_tensor("ident", [128, 128], BF16, kind="ExternalInput").ap()

    out_t = nc.dram_tensor("out", [TC, D], F32, kind="ExternalOutput").ap()

    dbg = {}
    if debug:
        for name, shape, dt_ in [
            ("dbg_xh", [128, DT * TC], F32),
            ("dbg_ga", [128, GT * TC], F32),
            ("dbg_gx", [128, GT * TC], F32),
            ("dbg_h", [128, DT * TC], BF16),
            ("dbg_ln1", [NTT * 128, D], F32),
            ("dbg_inv", [128, 16], F32),
            ("dbg_wq", [24 * 128, D], BF16),
            ("dbg_hin", [128, 256], F32),
            ("dbg_scr", [NCH * 4 * 128, 1024], F32),
            ("dbg_summ", [128, 512], F32),
        ]:
            dbg[name] = nc.dram_tensor(name, shape, dt_, kind="ExternalOutput").ap()

    wshard = nc.dram_tensor("wshard", [24 * 128, D], BF16).ap()
    wgath1 = nc.dram_tensor("wgath1", [NCORES * 16 * 128, D], BF16).ap()
    wgath2 = nc.dram_tensor("wgath2", [NCORES * 8 * 128, D], BF16).ap()
    scanscr = nc.dram_tensor("scanscr", [NCH * 4 * 128, 1024], F32).ap()
    cc_in = nc.dram_tensor("cc_in", [128, 32], F32).ap()
    cc_out = nc.dram_tensor("cc_out", [NCORES * 128, 32], F32).ap()
    ln1scr = nc.dram_tensor("ln1scr", [NTT * 128, D], F32).ap()
    hhscr = nc.dram_tensor("hhscr", [NPK * 128, TC], BF16).ap()
    outscr = nc.dram_tensor("outscr", [NTT * 128, D], F32).ap()

    def wg_tile(owner, pos):
        if pos < 16:
            base = (owner * 16 + pos) * 128
            return wgath1[base:base + 128, :]
        base = (owner * 8 + pos - 16) * 128
        return wgath2[base:base + 128, :]

    with contextlib.ExitStack() as ctx:
        E = ctx.enter_context
        sLd = E(nc.semaphore("sLd"))
        sSt = E(nc.semaphore("sSt"))
        sGp = E(nc.semaphore("sGp"))
        sCc = E(nc.semaphore("sCc"))
        sPe = E(nc.semaphore("sPe"))
        sDv = E(nc.semaphore("sDv"))
        sSc = E(nc.semaphore("sSc"))
        cLd, cSt, cGp, cCc, cPe, cDv, cSc = (Ctr() for _ in range(7))

        _block = E(nc.Block())

        class _BlockShim:
            def __init__(self, b):
                self._b = b

            def vector(self, f):
                return self._b.vector(lambda eng: f(_SerEngine(eng, True)))

            def scalar(self, f):
                return self._b.scalar(lambda eng: f(_SerEngine(eng, False)))

            def __getattr__(self, n):
                return getattr(self._b, n)

        block = _BlockShim(_block)

        sCst = E(nc.semaphore("sCst"))
        sWqL = E(nc.semaphore("sWqL"))
        sWsT = E(nc.semaphore("sWsT"))
        sWsP = E(nc.semaphore("sWsP"))
        sXl0 = E(nc.semaphore("sXl0"))
        sXl1 = E(nc.semaphore("sXl1"))
        sWn0 = E(nc.semaphore("sWn0"))
        sWn1 = E(nc.semaphore("sWn1"))
        sGts = E(nc.semaphore("sGts"))
        sScr = E(nc.semaphore("sScr"))
        sScr1 = E(nc.semaphore("sScr1"))
        sCo0 = E(nc.semaphore("sCo0"))
        sCo1 = E(nc.semaphore("sCo1"))
        sXr0 = E(nc.semaphore("sXr0"))
        sXr1 = E(nc.semaphore("sXr1"))
        sBo = E(nc.semaphore("sBo"))
        sWo = E(nc.semaphore("sWo"))
        sL1s = E(nc.semaphore("sL1s"))
        sLbb = E(nc.semaphore("sLbb"))
        sGu0 = E(nc.semaphore("sGu0"))
        sGu1 = E(nc.semaphore("sGu1"))
        sHhS = E(nc.semaphore("sHhS"))
        sPj0 = E(nc.semaphore("sPj0"))
        sPj1 = E(nc.semaphore("sPj1"))
        sPj2 = E(nc.semaphore("sPj2"))
        sPj3 = E(nc.semaphore("sPj3"))
        sLnr = E(nc.semaphore("sLnr"))
        sOsc = E(nc.semaphore("sOsc"))
        sOsc1 = E(nc.semaphore("sOsc1"))
        sJl0 = E(nc.semaphore("sJl0"))
        sJl1 = E(nc.semaphore("sJl1"))
        sCgI = E(nc.semaphore("sCgI"))
        (cCst, cWqL, cWsT, cWsP, cGts, cScr, cScr1, cWo, cL1s, cLbb, cHhS,
         cLnr, cOsc, cOsc1, cCgI) = (Ctr() for _ in range(15))
        cXl = [Ctr(), Ctr()]
        cWn = [Ctr(), Ctr()]
        cCo = [Ctr(), Ctr()]
        cXr = [Ctr(), Ctr()]
        cGu = [Ctr(), Ctr()]
        cPj = [Ctr(), Ctr(), Ctr(), Ctr()]
        cJl = [Ctr(), Ctr()]

        ident = E(nc.sbuf_tensor("identB", [128, 128], BF16))
        consts = E(nc.sbuf_tensor("consts", [128, 64], F32))
        onesb = E(nc.sbuf_tensor("onesb", [128, 1], BF16))
        nc.const_aps.register(F32, HALF_PI, consts[:, 56:57])
        nc.const_aps.register(F32, LNH, consts[:, 57:58])
        nc.const_aps.register(F32, 0.25, consts[:, 58:59])
        nc.const_aps.register(F32, MAGIC, consts[:, 59:60])
        nc.const_aps.register(F32, -MAGIC, consts[:, 60:61])
        ones64 = E(nc.sbuf_tensor("ones64", [128, W], F32))
        summ = E(nc.sbuf_tensor("summ", [128, 4 * 128], F32))
        hinb = E(nc.sbuf_tensor("hinb", [128, 256], F32))
        sgath = E(nc.sbuf_tensor("sgath", [128, NCORES * 32], F32))

        @block.sync
        def _(sync):
            sync.dma_start(out=ident[:, :], in_=ident_in).then_inc(sCst, 16)
            sync.dma_start(out=consts[:, 0:16], in_=b_in_rs).then_inc(sCst, 16)
            sync.dma_start(out=consts[:, 16:24], in_=b_gx_rs).then_inc(sCst, 16)
            sync.dma_start(out=consts[:, 24:32], in_=b_ga_rs).then_inc(sCst, 16)
            sync.dma_start(out=consts[:, 32:40], in_=ar_rs).then_inc(sCst, 16)
            sync.dma_start(out=consts[:, 40:48], in_=ai_rs).then_inc(sCst, 16)
            sync.dma_start(out=consts[:, 48:56], in_=mask_lt).then_inc(sCst, 16)
        consts_ld = cCst.add(16 * 7)

        @block.scalar
        def _(scalar):
            scalar.wait_ge(sCst, consts_ld)
            scalar.activation(consts[:, 32:40], consts[:, 32:40], AF.Exp)
            scalar.activation(consts[:, 32:40], consts[:, 32:40], AF.Ln,
                              bias=1.0).then_inc(sSc, 1)
        cSc.add()

        @block.vector
        def _(vector):
            vector.wait_ge(sSc, cSc.v)
            vector.tensor_scalar_mul(consts[:, 32:40], consts[:, 32:40], -8.0)
            vector.memset(consts[:, 56:57], HALF_PI)
            vector.memset(consts[:, 57:58], LNH)
            vector.memset(consts[:, 58:59], 0.25)
            vector.memset(consts[:, 59:60], MAGIC)
            vector.memset(consts[:, 60:61], -MAGIC)
            vector.memset(onesb[:, :], 1.0)
            vector.memset(ones64[:, :], 1.0).then_inc(sDv, 1)
        consts_rdy = cDv.add()

        g_xh = nc.sbuf_tensor("xh", [128, DT * TC], F32)
        xh = g_xh.__enter__()
        g_fb = nc.sbuf_tensor("fbuf", [128, 1112], F32)
        fbuf = g_fb.__enter__()
        g_ga = nc.sbuf_tensor("ga", [128, GT * TC], F32)
        ga = g_ga.__enter__()
        g_gx = nc.sbuf_tensor("gx", [128, GT * TC], F32)
        gx = g_gx.__enter__()

        pctx = contextlib.ExitStack()
        p1_qrdy = {}
        if not skip_prepass:
            Ep = pctx.enter_context
            wq_b = Ep(nc.sbuf_tensor("wq_b", [128, D], BF16, side="right"))
            wq_s = Ep(nc.sbuf_tensor("wq_s", [128, 32], F32, side="right"))
            wq_sb = Ep(nc.sbuf_tensor("wq_sb", [128, 32], BF16, side="right"))
            wq_q = Ep(nc.sbuf_tensor("wq_q", [128, D], F32, side="right"))
            wq_hat = Ep(nc.sbuf_tensor("wq_hat", [128, D], BF16, side="right"))
            wq_T = Ep(nc.sbuf_tensor("wq_T", [128, D], BF16, side="right"))
            pq = Ep(nc.psum_tensor("pq", [128, 512], BF16, side="right"))

        def emit_p1_a(pos):
            if skip_prepass:
                return

            @block.gpsimd
            def _(gpsimd, pos=pos):
                if pos > 0:
                    gpsimd.wait_ge(sDv, cDv.v)
                if pos < 16:
                    gpsimd.dma_start(out=wq_b[:, :], in_=wgu_sh[pos]
                                     ).then_inc(sWqL, 16)
                else:
                    src = wpj_sh[pos - 16].rearrange("a p f -> p a f")
                    dst = wq_b.ap().rearrange("p (a f) -> p a f", f=128)
                    gpsimd.dma_start(out=dst, in_=src).then_inc(sWqL, 16)
            cWqL.add(16)

            @block.vector
            def _(vector, pos=pos):
                vector.wait_ge(sWqL, cWqL.v)
                if pos > 0:
                    vector.wait_ge(sPe, cPe.v)
                    if pos <= 16:
                        vector.wait_ge(sWsT, cWsT.v)
                    else:
                        vector.wait_ge(sWsP, cWsP.v)
                b3 = wq_b.ap().rearrange("p (g j) -> p g j", j=W)
                vector.tensor_reduce(wq_s[:, :], b3, AX, OP.add,
                                     apply_absolute_value=True)
                vector.tensor_scalar_mul(wq_sb[:, :], wq_s[:, :], 1.0 / W)
                vector.tensor_scalar_max(wq_q[:, 0:32], wq_sb[:, :], 1e-8)
                vector.reciprocal(wq_s[:, :], wq_q[:, 0:32])
                rb = wq_s.ap()[:, :, None].broadcast_to([128, 32, W])
                h3 = wq_hat.ap().rearrange("p (g j) -> p g j", j=W)
                vector.tensor_tensor(h3, b3, rb, OP.mult)
                vector.tensor_scalar(wq_q[:, :], wq_hat[:, :], MAGIC, MAGIC,
                                     OP.add, OP.subtract)
                vector.tensor_scalar(wq_q[:, :], wq_q[:, :], 1.0, -1.0,
                                     OP.min, OP.max)
                sb3 = wq_sb.ap()[:, :, None].broadcast_to([128, 32, W])
                q3 = wq_q.ap().rearrange("p (g j) -> p g j", j=W)
                vector.tensor_tensor(h3, q3, sb3, OP.mult).then_inc(sDv, 1)
            cDv.add()
            p1_qrdy[pos] = cDv.v

        def emit_p1_b(pos):
            if skip_prepass:
                return
            q_rdy = p1_qrdy[pos]
            for grp in range(4):
                @block.tensor
                def _(tensor, pos=pos, grp=grp, q_rdy=q_rdy):
                    tensor.wait_ge(sDv, q_rdy)
                    tensor.wait_ge(sCst, consts_ld)
                    if not (pos == 0 and grp == 0):
                        tensor.wait_ge(sDv, cDv.v)
                    last = None
                    for j in range(4):
                        k = grp * 4 + j
                        last = tensor.transpose(
                            pq[:, j * 128:(j + 1) * 128],
                            wq_hat[:, k * 128:(k + 1) * 128], ident[:, :])
                    last.then_inc(sPe, 1)
                cPe.add()

                @block.vector
                def _(vector, grp=grp):
                    vector.wait_ge(sPe, cPe.v)
                    vector.tensor_copy(wq_T[:, grp * 512:(grp + 1) * 512],
                                       pq[:, :]).then_inc(sDv, 1)
                cDv.add()

            @block.gpsimd
            def _(gpsimd, pos=pos):
                gpsimd.wait_ge(sDv, cDv.v)
                sem = sWsT if pos < 16 else sWsP
                gpsimd.dma_start(out=wshard[pos * 128:(pos + 1) * 128, :],
                                 in_=wq_T[:, :]).then_inc(sem, 16)
            (cWsT if pos < 16 else cWsP).add(16)

        with contextlib.ExitStack() as actx:
            Ea = actx.enter_context
            xcm = Ea(nc.sbuf_tensor("xcm", [128, DT * TC], BF16))
            xtmb = Ea(nc.sbuf_tensor("xtmb", [128, D], BF16))
            pA = Ea(nc.psum_tensor("pA", [128, 2048], BF16))

            for tt in range(NTT):
                bo = 0
                sXl = (sXl0, sXl1)[tt % 2]
                cXl_ = cXl[tt % 2]

                @block.gpsimd
                def _(gpsimd, tt=tt, bo=bo, sXl=sXl):
                    if tt >= 1:
                        gpsimd.wait_ge(sPe, cPe.v)
                    gpsimd.dma_start(out=xtmb[:, bo:bo + D],
                                     in_=x_in[tt * 128:(tt + 1) * 128, :]
                                     ).then_inc(sXl, 16)
                x_ld = cXl_.add(16)

                for grp in range(4 if A_LEVEL >= 2 else 0):
                    po = (grp % 2) * 1024

                    @block.tensor
                    def _(tensor, tt=tt, grp=grp, bo=bo, po=po, x_ld=x_ld,
                          sXl=sXl):
                        tensor.wait_ge(sXl, x_ld)
                        tensor.wait_ge(sCst, consts_ld)
                        if tt * 4 + grp >= 2:
                            tensor.wait_ge(sDv, cDv.v)
                        last = None
                        for j in range(4):
                            k = grp * 4 + j
                            last = tensor.transpose(
                                pA[:, po + j * 128: po + (j + 1) * 128],
                                xtmb[:, bo + k * 128: bo + (k + 1) * 128],
                                ident[:, :])
                        last.then_inc(sPe, 1)
                    cPe.add()

                    if A_LEVEL >= 3:
                        @block.vector
                        def _(vector, tt=tt, grp=grp, po=po):
                            vector.wait_ge(sPe, cPe.v)
                            dst = xcm.ap().rearrange("p (j t) -> p j t", t=TC)[
                                :, grp * 4:(grp + 1) * 4, tt * 128:(tt + 1) * 128]
                            src = pA.ap()[:, po:po + 512].rearrange(
                                "p (j t) -> p j t", t=128)
                            vector.tensor_copy(dst, src).then_inc(sDv, 1)
                        cDv.add()
                    else:
                        @block.vector
                        def _(vector):
                            vector.wait_ge(sPe, cPe.v)
                            nc.vector.engine_nop().then_inc(sDv, 1)
                        cDv.add()

            if stop == 2:
                @block.sync
                def _(sync):
                    sync.wait_ge(sDv, cDv.v)
                actx.close()
                g_gx.__exit__(None, None, None)
                g_ga.__exit__(None, None, None)
                g_xh.__exit__(None, None, None)
                return nc

            with contextlib.ExitStack() as bctx:
                Eb = bctx.enter_context
                winj = Eb(nc.sbuf_tensor("winj", [128, 2 * D], BF16))
                pB = Eb(nc.psum_tensor("pB", [128, 1024], F32))

                w_in_r = w_in.rearrange("(k p) (j f) -> p k j f", p=128, f=128)
                x_cm_rdy = cDv.v
                n = 0
                for j in range(DT):
                    wo = (j % 2) * D
                    sWn = (sWn0, sWn1)[j % 2]

                    @block.gpsimd
                    def _(gpsimd, j=j, wo=wo, sWn=sWn):
                        if j >= 2:
                            gpsimd.wait_ge(sPe, cPe.v)
                        dst = winj.ap()[:, wo:wo + D].rearrange(
                            "p (k f) -> p k f", f=128)
                        gpsimd.dma_start(out=dst, in_=w_in_r[:, :, j, :]
                                         ).then_inc(sWn, 16)
                    wj_ld = cWn[j % 2].add(16)

                    for ts_ in range(2):
                        po = (n % 2) * 512

                        @block.tensor
                        def _(tensor, j=j, ts_=ts_, po=po, n=n, wo=wo,
                              wj_ld=wj_ld, sWn=sWn):
                            tensor.wait_ge(sWn, wj_ld)
                            tensor.wait_ge(sDv, x_cm_rdy)
                            if n >= 2:
                                tensor.wait_ge(sSc, cSc.v - 1)
                            last = None
                            for k in range(DT):
                                last = tensor.matmul(
                                    pB[:, po:po + 512],
                                    winj[:, wo + k * 128: wo + (k + 1) * 128],
                                    xcm[:, k * TC + ts_ * 512:
                                        k * TC + ts_ * 512 + 512],
                                    start=(k == 0), stop=(k == DT - 1))
                            last.then_inc(sPe, 1)
                        cPe.add()

                        @block.scalar
                        def _(scalar, j=j, ts_=ts_, po=po):
                            scalar.wait_ge(sPe, cPe.v)
                            scalar.activation(
                                xh[:, j * TC + ts_ * 512: j * TC + ts_ * 512 + 512],
                                pB[:, po:po + 512], AF.Identity,
                                bias=consts[:, j:j + 1]).then_inc(sSc, 1)
                        cSc.add()
                        n += 1
                    if j > 0:
                        emit_p1_b(j - 1)
                    emit_p1_a(j)
                xh_rdy_sc = cSc.v

        if stop == 3:
            @block.sync
            def _(sync):
                sync.wait_ge(sSc, xh_rdy_sc)
            g_gx.__exit__(None, None, None)
            g_ga.__exit__(None, None, None)
            g_xh.__exit__(None, None, None)
            return nc

        if debug:
            @block.sync
            def _(sync):
                sync.wait_ge(sSc, xh_rdy_sc)
                sync.dma_start(out=dbg["dbg_xh"], in_=xh[:, :]).then_inc(sSt, 16)
            cSt.add(16)

        with contextlib.ExitStack() as cctx:
            Ec = cctx.enter_context
            xhb = Ec(nc.sbuf_tensor("xhb", [128, DT * TC], BF16))
            wgb = Ec(nc.sbuf_tensor("wgb", [128, GT * 4 * 128], BF16))
            pC = Ec(nc.psum_tensor("pC", [128, 1024], F32))

            @block.gpsimd
            def _(gpsimd):
                gpsimd.wait_ge(sSc, xh_rdy_sc)
                for r in range(DT):
                    gpsimd.dma_start(out=xhb[:, r * TC:(r + 1) * TC],
                                     in_=xh[:, r * TC:(r + 1) * TC]
                                     ).then_inc(sGts, 16)
                for g in range(GT):
                    for k in range(2):
                        gpsimd.dma_start(
                            out=wgb[:, (g * 4 + k) * 128:(g * 4 + k + 1) * 128],
                            in_=w_gx[g, k]).then_inc(sGts, 16)
                        gpsimd.dma_start(
                            out=wgb[:, (g * 4 + 2 + k) * 128:(g * 4 + 3 + k) * 128],
                            in_=w_ga[g, k]).then_inc(sGts, 16)
            gates_ld = cGts.add(16 * (DT + GT * 4))

            n = 0
            for g in range(GT):
                for half in range(2):
                    for ts_ in range(2):
                        po = (n % 2) * 512

                        @block.tensor
                        def _(tensor, g=g, half=half, ts_=ts_, po=po, n=n):
                            tensor.wait_ge(sGts, gates_ld)
                            if n >= 2:
                                tensor.wait_ge(sSc, cSc.v - 1)
                            last = None
                            for k in range(2):
                                lb = (g * 4 + half * 2 + k) * 128
                                rbase = (2 * g + k) * TC + ts_ * 512
                                last = tensor.matmul(
                                    pC[:, po:po + 512],
                                    wgb[:, lb:lb + 128],
                                    xhb[:, rbase:rbase + 512],
                                    start=(k == 0), stop=(k == 1))
                            last.then_inc(sPe, 1)
                        cPe.add()

                        @block.scalar
                        def _(scalar, g=g, half=half, ts_=ts_, po=po):
                            scalar.wait_ge(sPe, cPe.v)
                            dstbuf = gx if half == 0 else ga
                            bcol = (16 if half == 0 else 24) + g
                            scalar.activation(
                                dstbuf[:, g * TC + ts_ * 512:
                                       g * TC + ts_ * 512 + 512],
                                pC[:, po:po + 512], AF.Sigmoid,
                                bias=consts[:, bcol:bcol + 1]).then_inc(sSc, 1)
                        cSc.add()
                        n += 1
                        if n % 4 == 0:
                            m = n // 4 - 1
                            emit_p1_b(15 + m)
                            emit_p1_a(16 + m)
            gates_rdy = cSc.v

        if stop == 4:
            @block.sync
            def _(sync):
                sync.wait_ge(sSc, gates_rdy)
            g_gx.__exit__(None, None, None)
            g_ga.__exit__(None, None, None)
            g_xh.__exit__(None, None, None)
            return nc

        if debug:
            @block.sync
            def _(sync):
                sync.wait_ge(sSc, gates_rdy)
                sync.dma_start(out=dbg["dbg_ga"], in_=ga[:, :]).then_inc(sSt, 16)
                sync.dma_start(out=dbg["dbg_gx"], in_=gx[:, :]).then_inc(sSt, 16)
            cSt.add(32)

        if not skip_prepass:
            emit_p1_b(23)
        pctx.close()

        def emit_wq_ag():
            if skip_prepass:
                return cCc.v, cCc.v
            if skip_ag:
                @block.gpsimd
                def _(gpsimd):
                    gpsimd.wait_ge(sWsT, cWsT.v)
                    gpsimd.wait_ge(sWsP, cWsP.v)
                    gpsimd.nop().then_inc(sCc, 2)
            else:
                @block.gpsimd
                def _(gpsimd):
                    gpsimd.wait_ge(sWsT, 16 * 16)
                    gpsimd.collective_compute(
                        "AllGather", OP.bypass,
                        replica_groups=[list(range(NCORES))],
                        ins=[wshard[0:16 * 128, :]], outs=[wgath1],
                    ).then_inc(sCc, 1)
                    gpsimd.wait_ge(sWsP, cWsP.v)
                    gpsimd.collective_compute(
                        "AllGather", OP.bypass,
                        replica_groups=[list(range(NCORES))],
                        ins=[wshard[16 * 128:, :]], outs=[wgath2],
                    ).then_inc(sCc, 1)
            return cCc.add(), cCc.add()

        wq_ag1, wq_ag = emit_wq_ag()

        if stop == 1:
            @block.sync
            def _(sync):
                sync.wait_ge(sCc, wq_ag)
            return nc

        @block.vector
        def _(vector):
            vector.wait_ge(sSc, gates_rdy)
            vector.wait_ge(sDv, consts_rdy)
            nc.vector.engine_nop().then_inc(sDv, 1)
        xc_rdy = cDv.add()

        NSLOT = 16
        g_pool = nc.sbuf_tensor("pool", [128, NSLOT * 1024], F32)
        poolt = g_pool.__enter__()

        def _mkP(par):
            def sl(s):
                if s in (16, 17):
                    return s - 16
                if par and s in (2, 3, 4, 5):
                    return s + 10
                return s
            Pf = lambda s: poolt[:, sl(s) * 1024:(sl(s) + 1) * 1024]
            P3f = lambda s, t=CH: poolt.ap()[:, sl(s) * 1024:
                                             (sl(s) + 1) * 1024].rearrange(
                "p (r t) -> p r t", t=t)
            return Pf, P3f

        scr_mark = []
        for c in range(NCH):
            t0 = c * CH
            P_, P3 = _mkP(c % 2)

            def gsl3(buf, off=0, t0=t0):
                return buf.ap().rearrange("p (r t) -> p r t", t=TC)[
                    :, off:off + GT, t0:t0 + CH]

            @block.scalar
            def _(scalar, c=c, t0=t0):
                scalar.wait_ge(sSc, gates_rdy)
                if c == 0:
                    scalar.wait_ge(sDv, xc_rdy)
                if c > 0:
                    scalar.wait_ge(sDv, cDv.v)
                rs = _R(scalar, "lrli")
                last = None
                for r in range(GT):
                    sl = ga[:, r * TC + t0: r * TC + t0 + CH]
                    rs.mul(P_(0)[:, r * CH:(r + 1) * CH], sl,
                           consts[:, 32 + r:33 + r])
                    last = rs.mul(P_(1)[:, r * CH:(r + 1) * CH], sl,
                                  consts[:, 40 + r:41 + r])
                last.then_inc(sSc, 1)
            cSc.add()

            @block.vector
            def _(vector, c=c, t0=t0,
                  sm=(scr_mark[c - 2] if c >= 2 else 0)):
                vector.wait_ge(sSc, cSc.v)
                if c == 0:
                    vector.wait_ge(sWsT, cWsT.v)
                    vector.wait_ge(sWsP, cWsP.v)
                if c >= 2:
                    vector.wait_ge((sScr, sScr1)[c % 2], sm)
                rv = _R(vector, "tts1")
                last = None
                for r in range(GT):
                    for w in range(2):
                        a = r * CH + w * W
                        rv.tensor_tensor_scan(
                            P_(2)[:, a:a + W], ones64[:, :], P_(0)[:, a:a + W],
                            0.0, OP.mult, OP.add)
                        last = rv.tensor_tensor_scan(
                            P_(3)[:, a:a + W], ones64[:, :], P_(1)[:, a:a + W],
                            0.0, OP.mult, OP.add)
                vector.drain()
                last.then_inc(sDv, 1)
            cDv.add()

            @block.scalar
            def _(scalar, c=c):
                scalar.wait_ge(sDv, cDv.v)
                rs = _R(scalar, "luts")
                rs.activation(P_(4), P_(0), AF.Exp, scale=2.0)
                rs.activation(P_(5), P_(0), AF.Exp, scale=4.0)
                rs.activation(P_(6), P_(2), AF.Exp)
                rs.activation(P_(7), P_(2), AF.Exp, scale=-1.0)
                last = rs.activation(P_(8), P_(1), AF.Sin, scale=2.0,
                                     bias=HALF_PI)
                last.then_inc(sSc, 1)
            cSc.add()

            @block.vector
            def _(vector, c=c):
                vector.wait_ge(sSc, cSc.v)
                rv = _R(vector, "rr")
                rv.tensor_scalar(P_(0), P_(3), INV_2PI, 0.25, OP.mult, OP.add)
                rv.tensor_scalar(P_(9), P_(3), INV_2PI, None, OP.mult)
                rv.tensor_scalar(P_(0), P_(0), MAGIC, MAGIC, OP.add, OP.subtract)
                rv.tensor_scalar(P_(9), P_(9), MAGIC, MAGIC, OP.add, OP.subtract)
                rv.scalar_tensor_tensor(P_(0), P_(0), -TWO_PI, P_(3),
                                        OP.mult, OP.add)
                rv.scalar_tensor_tensor(P_(2), P_(9), -TWO_PI, P_(3),
                                        OP.mult, OP.add)
                rv.tensor_scalar(P_(0), P_(0), HALF_PI - 1e-6,
                                 -3.0 * HALF_PI + 1e-6, OP.min, OP.max)
                rv.tensor_scalar(P_(2), P_(2), TWO_PI / 2 - 1e-6,
                                 -TWO_PI / 2 + 1e-6, OP.min, OP.max)
                vector.tensor_mul(P_(9), P_(4), P_(8))
                vector.scalar_tensor_tensor(P_(4), P_(9), -2.0, P_(5),
                                            OP.mult, OP.add)
                vector.tensor_scalar(P_(4), P_(4), 1.0, TINY, OP.add, OP.max
                                     ).then_inc(sDv, 1)
            cDv.add()

            @block.scalar
            def _(scalar, c=c):
                scalar.wait_ge(sDv, cDv.v)
                scalar.activation(P_(5), P_(4), AF.Ln)
                scalar.activation(P_(5), P_(5), AF.Exp, scale=0.5
                                  ).then_inc(sSc, 1)
            cSc.add()

            @block.vector
            def _(vector, c=c):
                vector.wait_ge(sSc, cSc.v)
                rv = _R(vector, "w1w2")
                rv.tensor_sub(P_(8), P_(5), P_(9))
                rv.tensor_add(P_(9), P_(5), P_(9))
                rv.tensor_scalar(P_(8), P_(8), 1.0, TINY, OP.add, OP.max)
                vector.tensor_scalar(P_(9), P_(9), -1.0, TINY, OP.add, OP.max
                                     ).then_inc(sDv, 1)
            cDv.add()

            @block.scalar
            def _(scalar, c=c):
                scalar.wait_ge(sDv, cDv.v)
                scalar.activation(P_(10), P_(8), AF.Ln)
                scalar.activation(P_(10), P_(10), AF.Exp, scale=0.5, bias=LNH)
                scalar.activation(P_(11), P_(9), AF.Ln)
                scalar.activation(P_(11), P_(11), AF.Exp, scale=0.5, bias=LNH)
                scalar.activation(P_(16), P_(0), AF.Sin, bias=HALF_PI)
                scalar.activation(P_(17), P_(2), AF.Sin).then_inc(sSc, 1)
            cSc.add()

            @block.vector
            def _(vector, c=c, t0=t0):
                vector.wait_ge(sSc, cSc.v)
                rv = _R(vector, "big")
                rv.tensor_mul(P_(2), P_(6), P_(16))
                rv.tensor_mul(P_(3), P_(6), P_(17))
                rv.tensor_mul(P_(16), P_(7), P_(16))
                rv.tensor_mul(P_(17), P_(7), P_(17))
                rv.tensor_mul(P_(6), P_(10), P_(16))
                rv.tensor_mul(P_(7), P_(11), P_(17))
                rv.tensor_mul(P_(9), P_(10), P_(17))
                rv.tensor_mul(P_(8), P_(11), P_(16))
                rv.tensor_sub(P_(6), P_(6), P_(7))
                rv.tensor_add(P_(7), P_(9), P_(8))
                rv.tensor_tensor(P3(6), gsl3(gx), P3(6), OP.mult)
                rv.tensor_tensor(P3(7), gsl3(gx), P3(7), OP.mult)
                rv.tensor_tensor(P3(10), gsl3(xh), P3(6), OP.mult)
                rv.tensor_tensor(P3(16), gsl3(xh, 8), P3(6), OP.mult)
                rv.tensor_tensor(P3(11), gsl3(xh, 8), P3(7), OP.mult)
                rv.tensor_tensor(P3(17), gsl3(xh), P3(7), OP.mult)
                rv.tensor_add(P_(10), P_(10), P_(11))
                rv.tensor_sub(P_(9), P_(16), P_(17))
                vector.drain()
                rv2 = _R(vector, "tts2")
                for r in range(GT):
                    for w in range(2):
                        a = r * CH + w * W
                        rv2.tensor_tensor_scan(
                            P_(4)[:, a:a + W], ones64[:, :], P_(10)[:, a:a + W],
                            0.0, OP.mult, OP.add)
                        rv2.tensor_tensor_scan(
                            P_(5)[:, a:a + W], ones64[:, :], P_(9)[:, a:a + W],
                            0.0, OP.mult, OP.add)
                vector.drain()
                rv3 = _R(vector, "ext")
                last = None
                for src, col in ((2, 0), (3, 1), (4, 2), (5, 3)):
                    sap = P3(src)[:, :, W - 1::W]
                    dap = summ.ap()[:, col * 128:(col + 1) * 128].rearrange(
                        "p (r w) -> p r w", w=16)[:, :, 2 * c:2 * c + 2]
                    last = rv3.tensor_copy(dap, sap)
                vector.drain()
                last.then_inc(sDv, 1)
            cDv.add()

            @block.sync
            def _(sync, c=c):
                sync.wait_ge(sDv, cDv.v)
                for i, s in enumerate((4, 5, 2, 3)):
                    base = (c * 4 + i) * 128
                    sync.dma_start(out=scanscr[base:base + 128, :],
                                   in_=P_(s)).then_inc((sScr, sScr1)[c % 2], 16)
            (cScr, cScr1)[c % 2].add(64)
            scr_mark.append((cScr, cScr1)[c % 2].v)
        scan_dv = cDv.v
        g_pool.__exit__(None, None, None)
        g_gx.__exit__(None, None, None)
        g_ga.__exit__(None, None, None)
        _FO = {0: (0, 512), 1: (512, 256), 2: (768, 40), 3: (808, 32),
               4: (840, 16), 5: (856, 256)}
        P_ = lambda s: fbuf[:, _FO[s][0]:_FO[s][0] + _FO[s][1]]

        if stop == 5:
            @block.sync
            def _(sync):
                sync.wait_ge(sDv, scan_dv)
                sync.wait_ge(sScr, cScr.v)
                sync.wait_ge(sScr1, cScr1.v)
            g_fb.__exit__(None, None, None)
            g_xh.__exit__(None, None, None)
            return nc

        g_ln1cm = nc.sbuf_tensor("ln1cm", [128, DT * TC], BF16, side="right")
        ln1cm = g_ln1cm.__enter__()
        g_wob = nc.sbuf_tensor("woutb", [128, DT * D], BF16, side="right")
        woutb = g_wob.__enter__()
        g_bob = nc.sbuf_tensor("b_out_b", [128, D], F32, side="right")
        b_out_b = g_bob.__enter__()
        g_xres = nc.sbuf_tensor("xres", [128, 2 * D], F32, side="right")
        xres = g_xres.__enter__()

        @block.gpsimd
        def _(gpsimd):
            gpsimd.wait_ge(sDv, scan_dv)
            gpsimd.wait_ge(sScr, cScr.v)
            gpsimd.wait_ge(sScr1, cScr1.v)
            for k in range(DT):
                gpsimd.dma_start(out=woutb[:, k * D:(k + 1) * D],
                                 in_=w_out[k * 128:(k + 1) * 128, :]
                                 ).then_inc(sWo, 16)
            gpsimd.dma_start(out=b_out_b[:, :], in_=b_out_bc).then_inc(sBo, 16)
            gpsimd.dma_start(out=xres[:, 0:D],
                             in_=x_in[0:128, :]).then_inc(sOsc, 16)
            gpsimd.dma_start(out=xres[:, D:2 * D],
                             in_=x_in[128:256, :]).then_inc(sOsc1, 16)
        wout_ld = cWo.add(16 * DT)
        cOsc.add(16)
        cOsc1.add(16)

        def wsl(buf, base, w):
            return buf.ap()[:, base:base + 128].rearrange(
                "p (r w) -> p r w", w=16)[:, :, w:w + 1]

        @block.vector
        def _(vector):
            vector.wait_ge(sDv, scan_dv)
            vector.wait_ge(sScr, cScr.v)
            vector.wait_ge(sScr1, cScr1.v)
            vector.tensor_mul(P_(0)[:, 0:128], summ[:, 0:128], summ[:, 256:384])
            vector.tensor_mul(P_(0)[:, 128:256], summ[:, 128:256], summ[:, 384:512])
            vector.tensor_mul(P_(0)[:, 256:384], summ[:, 0:128], summ[:, 384:512])
            vector.tensor_mul(P_(0)[:, 384:512], summ[:, 128:256], summ[:, 256:384])
            vector.tensor_sub(summ[:, 256:384], P_(0)[:, 0:128], P_(0)[:, 128:256])
            vector.tensor_add(summ[:, 384:512], P_(0)[:, 256:384], P_(0)[:, 384:512])
            vector.memset(hinb[:, :], 0.0)
            vector.memset(P_(1)[:, 0:128], 1.0)
            vector.memset(P_(1)[:, 128:256], 0.0)
            t0_ = P_(2)[:, 0:8].rearrange("p (r w) -> p r w", w=1)
            t1_ = P_(2)[:, 8:16].rearrange("p (r w) -> p r w", w=1)
            for w in range(1, NW):
                ArP, AiP = wsl(summ, 0, w - 1), wsl(summ, 128, w - 1)
                BrP, BiP = wsl(summ, 256, w - 1), wsl(summ, 384, w - 1)
                HrP, HiP = wsl(hinb, 0, w - 1), wsl(hinb, 128, w - 1)
                Hr, Hi = wsl(hinb, 0, w), wsl(hinb, 128, w)
                cArP, cAiP = wsl(fbuf, 512, w - 1), wsl(fbuf, 640, w - 1)
                cAr, cAi = wsl(fbuf, 512, w), wsl(fbuf, 640, w)
                vector.tensor_tensor(t0_, ArP, HrP, OP.mult)
                vector.tensor_tensor(t1_, AiP, HiP, OP.mult)
                vector.tensor_tensor(t0_, t0_, t1_, OP.subtract)
                vector.tensor_tensor(Hr, t0_, BrP, OP.add)
                vector.tensor_tensor(t0_, ArP, HiP, OP.mult)
                vector.tensor_tensor(t1_, AiP, HrP, OP.mult)
                vector.tensor_tensor(t0_, t0_, t1_, OP.add)
                vector.tensor_tensor(Hi, t0_, BiP, OP.add)
                vector.tensor_tensor(t0_, cArP, ArP, OP.mult)
                vector.tensor_tensor(t1_, cAiP, AiP, OP.mult)
                vector.tensor_tensor(cAr, t0_, t1_, OP.subtract)
                vector.tensor_tensor(t0_, cArP, AiP, OP.mult)
                vector.tensor_tensor(t1_, cAiP, ArP, OP.mult)
                vector.tensor_tensor(cAi, t0_, t1_, OP.add)
            ArL, AiL = wsl(summ, 0, NW - 1), wsl(summ, 128, NW - 1)
            BrL, BiL = wsl(summ, 256, NW - 1), wsl(summ, 384, NW - 1)
            HrL, HiL = wsl(hinb, 0, NW - 1), wsl(hinb, 128, NW - 1)
            cArL, cAiL = wsl(fbuf, 512, NW - 1), wsl(fbuf, 640, NW - 1)
            pk = lambda a, b: P_(3)[:, a:b].rearrange("p (r w) -> p r w", w=1)
            vector.tensor_tensor(t0_, cArL, ArL, OP.mult)
            vector.tensor_tensor(t1_, cAiL, AiL, OP.mult)
            vector.tensor_tensor(pk(0, 8), t0_, t1_, OP.subtract)
            vector.tensor_tensor(t0_, cArL, AiL, OP.mult)
            vector.tensor_tensor(t1_, cAiL, ArL, OP.mult)
            vector.tensor_tensor(pk(8, 16), t0_, t1_, OP.add)
            vector.tensor_tensor(t0_, ArL, HrL, OP.mult)
            vector.tensor_tensor(t1_, AiL, HiL, OP.mult)
            vector.tensor_tensor(t0_, t0_, t1_, OP.subtract)
            vector.tensor_tensor(pk(16, 24), t0_, BrL, OP.add)
            vector.tensor_tensor(t0_, ArL, HiL, OP.mult)
            vector.tensor_tensor(t1_, AiL, HrL, OP.mult)
            vector.tensor_tensor(t0_, t0_, t1_, OP.add)
            vector.tensor_tensor(pk(24, 32), t0_, BiL, OP.add).then_inc(sDv, 1)
        summ_rdy = cDv.add()

        @block.gpsimd
        def _(gpsimd):
            gpsimd.wait_ge(sDv, summ_rdy)
            gpsimd.dma_start(out=cc_in, in_=P_(3)[:, 0:32]).then_inc(sCgI, 16)
            gpsimd.wait_ge(sCgI, cCgI.add(16))
            gpsimd.collective_compute(
                "AllGather", OP.bypass,
                replica_groups=[list(range(NCORES))],
                ins=[cc_in], outs=[cc_out],
            ).then_inc(sCc, 1)
            gpsimd.wait_ge(sCc, cCc.add())
            last = None
            for j in range(NCORES):
                last = gpsimd.dma_start(out=sgath[:, 32 * j:32 * (j + 1)],
                                        in_=cc_out[128 * j:128 * (j + 1), :])
                last.then_inc(sCgI, 16)
            cCgI.add(16 * NCORES)
        carry_ld = cCgI.v

        @block.vector
        def _(vector):
            vector.wait_ge(sCgI, carry_ld)
            vector.memset(P_(4)[:, 0:16], 0.0)
            hr = P_(4)[:, 0:8].rearrange("p (r w) -> p r w", w=1)
            hi = P_(4)[:, 8:16].rearrange("p (r w) -> p r w", w=1)
            t0_ = P_(2)[:, 0:8].rearrange("p (r w) -> p r w", w=1)
            t1_ = P_(2)[:, 8:16].rearrange("p (r w) -> p r w", w=1)
            t2_ = P_(2)[:, 16:24].rearrange("p (r w) -> p r w", w=1)
            t3_ = P_(2)[:, 24:32].rearrange("p (r w) -> p r w", w=1)
            t4_ = P_(2)[:, 32:40].rearrange("p (r w) -> p r w", w=1)
            for j in range(NCORES):
                g = lambda a, b: sgath.ap()[:, 32 * j + a:32 * j + b].rearrange(
                    "p (r w) -> p r w", w=1)
                m = consts[:, 48 + j:49 + j]
                vector.tensor_scalar(t2_, g(0, 8), -1.0, None, OP.add)
                vector.tensor_scalar(t2_, t2_, m, 1.0, OP.mult, OP.add)
                vector.tensor_scalar(t3_, g(8, 16), m, None, OP.mult)
                vector.tensor_tensor(t0_, t2_, hr, OP.mult)
                vector.tensor_tensor(t1_, t3_, hi, OP.mult)
                vector.tensor_tensor(t0_, t0_, t1_, OP.subtract)
                vector.tensor_tensor(t1_, t2_, hi, OP.mult)
                vector.tensor_tensor(t4_, t3_, hr, OP.mult)
                vector.tensor_tensor(hi, t1_, t4_, OP.add)
                vector.tensor_scalar(t1_, g(16, 24), m, None, OP.mult)
                vector.tensor_tensor(hr, t0_, t1_, OP.add)
                vector.tensor_scalar(t1_, g(24, 32), m, None, OP.mult)
                vector.tensor_tensor(hi, hi, t1_, OP.add)
            hcr = P_(4)[:, 0:8][:, :, None].broadcast_to([128, 8, 16])
            hci = P_(4)[:, 8:16][:, :, None].broadcast_to([128, 8, 16])
            cAr = P_(1)[:, 0:128].rearrange("p (r w) -> p r w", w=16)
            cAi = P_(1)[:, 128:256].rearrange("p (r w) -> p r w", w=16)
            hR = hinb.ap()[:, 0:128].rearrange("p (r w) -> p r w", w=16)
            hI = hinb.ap()[:, 128:256].rearrange("p (r w) -> p r w", w=16)
            p5a = P_(5)[:, 0:128].rearrange("p (r w) -> p r w", w=16)
            p5b = P_(5)[:, 128:256].rearrange("p (r w) -> p r w", w=16)
            vector.tensor_tensor(p5a, cAr, hcr, OP.mult)
            vector.tensor_tensor(p5b, cAi, hci, OP.mult)
            vector.tensor_tensor(p5a, p5a, p5b, OP.subtract)
            vector.tensor_tensor(hR, hR, p5a, OP.add)
            vector.tensor_tensor(p5a, cAr, hci, OP.mult)
            vector.tensor_tensor(p5b, cAi, hcr, OP.mult)
            vector.tensor_tensor(p5a, p5a, p5b, OP.add)
            vector.tensor_tensor(hI, hI, p5a, OP.add).then_inc(sDv, 1)
        hin_rdy = cDv.add()

        if stop == 6:
            @block.sync
            def _(sync):
                sync.wait_ge(sDv, hin_rdy)
            g_fb.__exit__(None, None, None)
            g_xh.__exit__(None, None, None)
            return nc

        if debug:
            @block.sync
            def _(sync):
                sync.wait_ge(sDv, hin_rdy)
                sync.dma_start(out=dbg["dbg_hin"], in_=hinb[:, :]).then_inc(sSt, 16)
                sync.dma_start(out=dbg["dbg_summ"], in_=summ[:, :]).then_inc(sSt, 16)
                sync.dma_start(out=dbg["dbg_scr"], in_=scanscr).then_inc(sSt, 16)
            cSt.add(48)

        g_fb.__exit__(None, None, None)
        g_xh.__exit__(None, None, None)

        g_hb = nc.sbuf_tensor("hb", [128, DT * TC], BF16, side="right")
        hb = g_hb.__enter__()
        g_corr = nc.sbuf_tensor("corr", [128, 10 * 1024], F32, side="right")
        corr = g_corr.__enter__()
        C_ = lambda s: corr[:, s * 1024:(s + 1) * 1024]
        C3 = lambda s, t=CH: corr.ap()[:, s * 1024:(s + 1) * 1024].rearrange(
            "p (r t) -> p r t", t=t)

        g_mark = []
        for c in range(NCH):
            b = 2 + (c % 2) * 4
            sCo = (sCo0, sCo1)[c % 2]

            @block.sync
            def _(sync, c=c, b=b, sCo=sCo):
                sync.wait_ge(sDv, hin_rdy)
                if c == 0:
                    sync.wait_ge(sWsT, cWsT.v)
                    sync.wait_ge(sWsP, cWsP.v)
                if c >= 2:
                    sync.wait_ge(sDv, cDv.v)
                for i in range(4):
                    base = (c * 4 + i) * 128
                    sync.dma_start(out=C_(b + i), in_=scanscr[base:base + 128, :]
                                   ).then_inc(sCo, 16)
            co_ld = cCo[c % 2].add(64)

            @block.vector
            def _(vector, c=c, b=b, sCo=sCo, co_ld=co_ld):
                vector.wait_ge(sCo, co_ld)
                Sr, Si, cP, sP = b, b + 1, b + 2, b + 3
                r4 = lambda s: C3(s).rearrange("p r (w t) -> p r w t", t=W)
                hR = hinb.ap()[:, 0:128].rearrange("p (r w) -> p r w", w=16)[
                    :, :, 2 * c:2 * c + 2][:, :, :, None].broadcast_to(
                    [128, 8, 2, W])
                hI = hinb.ap()[:, 128:256].rearrange("p (r w) -> p r w", w=16)[
                    :, :, 2 * c:2 * c + 2][:, :, :, None].broadcast_to(
                    [128, 8, 2, W])
                vector.tensor_tensor(r4(Sr), r4(Sr), hR, OP.add)
                vector.tensor_tensor(r4(Si), r4(Si), hI, OP.add)
                vector.tensor_mul(C_(0), C_(cP), C_(Sr))
                vector.tensor_mul(C_(1), C_(sP), C_(Si))
                hre = hb.ap().rearrange("p (r t) -> p r t", t=TC)[
                    :, 0:8, c * CH:(c + 1) * CH]
                vector.tensor_tensor(hre, C3(0), C3(1), OP.subtract)
                vector.tensor_mul(C_(0), C_(cP), C_(Si))
                vector.tensor_mul(C_(1), C_(sP), C_(Sr))
                him = hb.ap().rearrange("p (r t) -> p r t", t=TC)[
                    :, 8:16, c * CH:(c + 1) * CH]
                vector.tensor_tensor(him, C3(0), C3(1), OP.add).then_inc(sDv, 1)
            g_mark.append(cDv.add())
        h_rdy = cDv.v
        g_corr.__exit__(None, None, None)

        if stop == 7:
            @block.sync
            def _(sync):
                sync.wait_ge(sDv, h_rdy)
            return nc

        if debug:
            @block.sync
            def _(sync):
                sync.wait_ge(sDv, h_rdy)
                sync.dma_start(out=dbg["dbg_h"], in_=hb[:, :]).then_inc(sSt, 16)
            cSt.add(16)

        with contextlib.ExitStack() as hctx:
            Eh = hctx.enter_context
            r1b = Eh(nc.sbuf_tensor("r1b", [128, 2 * D], F32))
            ln1b = Eh(nc.sbuf_tensor("ln1b", [128, D], F32))
            ln1bb = Eh(nc.sbuf_tensor("ln1bb", [128, 2 * D], BF16))
            sq_s = Eh(nc.sbuf_tensor("sq_s", [128, D], BF16))
            stat = Eh(nc.sbuf_tensor("stat", [128, 8], F32))
            pH = Eh(nc.psum_tensor("pH", [128, 1024], F32))
            pT = Eh(nc.psum_tensor("pT", [128, 512], BF16))

            cast_mark = []
            tp_mark = {}
            tp_ev = []

            def emit_ln1_tp(tt):
                for grp in range(4):
                    @block.tensor
                    def _(tensor, tt=tt, grp=grp,
                          ev=(tp_ev[-1] if tp_ev else None)):
                        tensor.wait_ge(sLbb, cast_mark[tt])
                        if ev is not None:
                            tensor.wait_ge(sDv, ev)
                        last = None
                        for j in range(4):
                            k = grp * 4 + j
                            last = tensor.transpose(
                                pT[:, j * 128:(j + 1) * 128],
                                ln1bb[:, (tt % 2) * D + k * 128:
                                      (tt % 2) * D + (k + 1) * 128],
                                ident[:, :])
                        last.then_inc(sPe, 1)
                    cPe.add()

                    @block.vector
                    def _(vector, tt=tt, grp=grp):
                        vector.wait_ge(sPe, cPe.v)
                        dst = ln1cm.ap().rearrange("p (j t) -> p j t", t=TC)[
                            :, grp * 4:(grp + 1) * 4, tt * 128:(tt + 1) * 128]
                        src = pT.ap().rearrange("p (j t) -> p j t", t=128)
                        vector.tensor_copy(dst, src).then_inc(sDv, 1)
                    tp_ev.append(cDv.add())
                tp_mark[tt] = cPe.v

            nH = 0
            for tt in range(NTT):
                bo = (tt % 2) * D

                sXr = (sXr0, sXr1)[tt % 2]

                if tt >= 2:
                    @block.sync
                    def _(sync, tt=tt, bo=bo, sXr=sXr):
                        sync.wait_ge(sDv, cDv.v)
                        sync.dma_start(out=xres[:, bo:bo + D],
                                       in_=x_in[tt * 128:(tt + 1) * 128, :]
                                       ).then_inc(sXr, 16)
                    cXr[tt % 2].add(16)
                xr_ld = cXr[tt % 2].v

                @block.vector
                def _(vector, bo=bo, xr_ld=xr_ld, sXr=sXr, tt=tt):
                    if tt < 2:
                        vector.wait_ge((sOsc, sOsc1)[tt], 16)
                        vector.wait_ge(sBo, 16)
                    else:
                        vector.wait_ge(sXr, xr_ld)
                    vector.tensor_add(xres[:, bo:bo + D], xres[:, bo:bo + D],
                                      b_out_b[:, :]).then_inc(sDv, 1)
                cDv.add()
                xres_rdy = cDv.v

                for cs in range(4):
                    po = (nH % 2) * 512

                    @block.tensor
                    def _(tensor, tt=tt, cs=cs, po=po, nH=nH, xres_rdy=xres_rdy,
                          gm=None):
                        tensor.wait_ge(sDv, g_mark[tt])
                        tensor.wait_ge(sWo, wout_ld)
                        tensor.wait_ge(sDv, xres_rdy)
                        if nH >= 2:
                            tensor.wait_ge(sDv, cDv.v - 1)
                        last = None
                        for k in range(DT):
                            last = tensor.matmul(
                                pH[:, po:po + 512],
                                hb[:, k * TC + tt * 128: k * TC + (tt + 1) * 128],
                                woutb[:, k * D + cs * 512: k * D + (cs + 1) * 512],
                                start=(k == 0), stop=(k == DT - 1))
                        last.then_inc(sPe, 1)
                    cPe.add()

                    @block.vector
                    def _(vector, cs=cs, po=po, bo=bo):
                        vector.wait_ge(sPe, cPe.v)
                        vector.scalar_tensor_tensor(
                            r1b[:, bo + cs * 512: bo + (cs + 1) * 512],
                            pH[:, po:po + 512], 1.0,
                            xres[:, bo + cs * 512: bo + (cs + 1) * 512],
                            OP.mult, OP.add).then_inc(sDv, 1)
                    cDv.add()
                    nH += 1

                @block.vector
                def _(vector, bo=bo):
                    vector.wait_ge(sDv, cDv.v)
                    vector.tensor_reduce(stat[:, 0:1], r1b[:, bo:bo + D], AX,
                                         OP.add).then_inc(sDv, 1)
                cDv.add()

                @block.scalar
                def _(scalar, bo=bo):
                    scalar.wait_ge(sDv, cDv.v)
                    scalar.activation(sq_s[:, :], r1b[:, bo:bo + D], AF.Square,
                                      accum_out=stat[:, 1:2]).then_inc(sSc, 1)
                cSc.add()

                @block.vector
                def _(vector):
                    vector.wait_ge(sSc, cSc.v)
                    vector.tensor_scalar_mul(stat[:, 2:3], stat[:, 0:1], 1.0 / D)
                    vector.tensor_scalar_mul(stat[:, 3:4], stat[:, 1:2], 1.0 / D)
                    vector.tensor_mul(stat[:, 4:5], stat[:, 2:3], stat[:, 2:3])
                    vector.tensor_sub(stat[:, 3:4], stat[:, 3:4], stat[:, 4:5])
                    vector.tensor_scalar_add(stat[:, 3:4], stat[:, 3:4], LN_EPS
                                             ).then_inc(sDv, 1)
                cDv.add()

                @block.scalar
                def _(scalar):
                    scalar.wait_ge(sDv, cDv.v)
                    scalar.activation(stat[:, 5:6], stat[:, 3:4], AF.Ln)
                    scalar.activation(stat[:, 5:6], stat[:, 5:6], AF.Exp,
                                      scale=-0.5).then_inc(sSc, 1)
                cSc.add()

                @block.vector
                def _(vector, bo=bo):
                    vector.wait_ge(sSc, cSc.v)
                    vector.wait_ge(sL1s, cL1s.v)
                    vector.wait_ge(sLbb, cLbb.v)
                    vector.tensor_scalar(ln1b[:, :], r1b[:, bo:bo + D],
                                         stat[:, 2:3], stat[:, 5:6],
                                         OP.subtract, OP.mult).then_inc(sDv, 1)
                cDv.add()
                ln1_rdy = cDv.v

                @block.sync
                def _(sync, tt=tt, ln1_rdy=ln1_rdy):
                    sync.wait_ge(sDv, ln1_rdy)
                    sync.dma_start(out=ln1scr[tt * 128:(tt + 1) * 128, :],
                                   in_=ln1b[:, :]).then_inc(sL1s, 16)
                    if debug:
                        sync.dma_start(out=dbg["dbg_ln1"][tt * 128:(tt + 1) * 128, :],
                                       in_=ln1b[:, :]).then_inc(sSt, 16)
                cL1s.add(16)
                if debug:
                    cSt.add(16)

                @block.gpsimd
                def _(gpsimd, tt=tt, ln1_rdy=ln1_rdy,
                      tm=(tp_mark[tt - 2] if tt >= 2 else 0)):
                    gpsimd.wait_ge(sDv, ln1_rdy)
                    if tt >= 2:
                        gpsimd.wait_ge(sPe, tm)
                    gpsimd.dma_start(out=ln1bb[:, (tt % 2) * D:(tt % 2 + 1) * D],
                                     in_=ln1b[:, :]).then_inc(sLbb, 16)
                cast_mark.append(cLbb.add(16))

                if tt > 0:
                    emit_ln1_tp(tt - 1)
            emit_ln1_tp(NTT - 1)
            ln1_all = cDv.v

        if stop == 8:
            @block.sync
            def _(sync):
                sync.wait_ge(sDv, ln1_all)
                sync.wait_ge(sLd, cLd.v)
            return nc

        g_hb.__exit__(None, None, None)
        g_xres.__exit__(None, None, None)
        g_bob.__exit__(None, None, None)
        g_wob.__exit__(None, None, None)
        with contextlib.ExitStack() as ictx:
            Ei = ictx.enter_context
            hhres = Ei(nc.sbuf_tensor("hhres", [128, NPAIR * TC], BF16))
            silu_s = Ei(nc.sbuf_tensor("silu_s", [128, TC], F32))
            sqh = Ei(nc.sbuf_tensor("sqh", [128, 2 * TC], BF16))
            ssrow = Ei(nc.sbuf_tensor("ssrow", [128, TC], F32))
            sscol = Ei(nc.sbuf_tensor("sscol", [128, 16], F32))
            ssb = Ei(nc.sbuf_tensor("ssb", [128, TC], BF16))
            gwctx = contextlib.ExitStack()
            gw = gwctx.enter_context(nc.sbuf_tensor("gw", [128, 2 * D], BF16))
            uw = gwctx.enter_context(nc.sbuf_tensor("uw", [128, 2 * D], BF16))
            gctx = contextlib.ExitStack()
            Eg = gctx.enter_context
            pG = Eg(nc.psum_tensor("pG", [128, 1024], F32))
            pU = Eg(nc.psum_tensor("pU", [128, 1024], F32))
            pS = Eg(nc.psum_tensor("pS", [1, 1024], F32))
            pR = Eg(nc.psum_tensor("pR", [128, 128], BF16))

            mm_mark = {}
            sq_mark = {}
            for i in range(NPAIR):
                bo = (i % 2) * D
                ho = (i % 2) * TC
                sGu = (sGu0, sGu1)[i % 2]

                @block.sync
                def _(sync, i=i, bo=bo, sGu=sGu,
                      mw=(mm_mark[i - 2] if i >= 2 else 0)):
                    sync.wait_ge(sCc, wq_ag1)
                    if i >= 2:
                        sync.wait_ge(sPe, mw)
                    sync.dma_start(out=gw[:, bo:bo + D],
                                   in_=wg_tile(i % 8, 2 * (i // 8))
                                   ).then_inc(sGu, 16)
                    sync.dma_start(out=uw[:, bo:bo + D],
                                   in_=wg_tile(i % 8, 2 * (i // 8) + 1)
                                   ).then_inc(sGu, 16)
                gu_ld = cGu[i % 2].add(32)

                for ts_ in range(2):
                    po = ts_ * 512

                    @block.tensor
                    def _(tensor, i=i, ts_=ts_, po=po, bo=bo, gu_ld=gu_ld,
                          sGu=sGu):
                        tensor.wait_ge(sGu, gu_ld)
                        tensor.wait_ge(sDv, ln1_all)
                        if i > 0:
                            tensor.wait_ge(sDv, cDv.v - 1)
                        lastg = None
                        for k in range(DT):
                            tensor.matmul(
                                pG[:, po:po + 512],
                                gw[:, bo + k * 128: bo + (k + 1) * 128],
                                ln1cm[:, k * TC + ts_ * 512:
                                      k * TC + ts_ * 512 + 512],
                                start=(k == 0), stop=(k == DT - 1))
                        for k in range(DT):
                            lastg = tensor.matmul(
                                pU[:, po:po + 512],
                                uw[:, bo + k * 128: bo + (k + 1) * 128],
                                ln1cm[:, k * TC + ts_ * 512:
                                      k * TC + ts_ * 512 + 512],
                                start=(k == 0), stop=(k == DT - 1))
                        lastg.then_inc(sPe, 1)
                    cPe.add()

                    @block.scalar
                    def _(scalar, ts_=ts_, po=po):
                        scalar.wait_ge(sPe, cPe.v)
                        scalar.activation(silu_s[:, po:po + 512],
                                          pG[:, po:po + 512], AF.Silu
                                          ).then_inc(sSc, 1)
                    cSc.add()

                    @block.vector
                    def _(vector, i=i, ts_=ts_, po=po):
                        vector.wait_ge(sSc, cSc.v)
                        if i == 0 and ts_ == 0:
                            vector.wait_ge(sL1s, cL1s.v)
                            vector.wait_ge(sLbb, cLbb.v)
                            if debug:
                                vector.wait_ge(sSt, cSt.v)
                        vector.tensor_mul(hhres[:, i * TC + po: i * TC + po + 512],
                                          silu_s[:, po:po + 512],
                                          pU[:, po:po + 512]).then_inc(sDv, 1)
                    cDv.add()

                    @block.scalar
                    def _(scalar, i=i, ts_=ts_, po=po):
                        scalar.wait_ge(sDv, cDv.v)
                        scalar.activation(sqh[:, ((i % 2) * 2 + ts_) * 512:
                                              ((i % 2) * 2 + ts_) * 512 + 512],
                                          hhres[:, i * TC + po: i * TC + po + 512],
                                          AF.Square).then_inc(sSc, 1)
                    cSc.add()
                mm_mark[i] = cPe.v
                sq_mark[i] = cSc.v

                if i > 0:
                    @block.tensor
                    def _(tensor, i=i, sw=sq_mark[i - 1]):
                        tensor.wait_ge(sSc, sw)
                        for ts_ in range(2):
                            tensor.matmul(pS[:, ts_ * 512:(ts_ + 1) * 512],
                                          onesb[:, :],
                                          sqh[:, (((i - 1) % 2) * 2 + ts_) * 512:
                                              (((i - 1) % 2) * 2 + ts_) * 512 + 512],
                                          start=(i - 1 == 0), stop=False,
                                          skip_group_check=True).then_inc(sPe, 1)
                            cPe.add()

            @block.tensor
            def _(tensor):
                tensor.wait_ge(sSc, cSc.v)
                last = None
                for ts_ in range(2):
                    last = tensor.matmul(
                        pS[:, ts_ * 512:(ts_ + 1) * 512], onesb[:, :],
                        sqh[:, (((NPAIR - 1) % 2) * 2 + ts_) * 512:
                            (((NPAIR - 1) % 2) * 2 + ts_) * 512 + 512],
                        start=False, stop=True, skip_group_check=True)
                last.then_inc(sPe, 1)
            cPe.add()
            gu_done_pe = cPe.v

            @block.vector
            def _(vector):
                vector.wait_ge(sPe, gu_done_pe)
                vector.memset(ssrow[:, :], 0.0)
                vector.tensor_copy(ssrow[0:1, :], pS[:, :])
                vector.tensor_copy(ssb[:, :], ssrow[:, :]).then_inc(sDv, 1)
            cDv.add()
            ss_rdy = cDv.v

            for tt in range(NTT):
                @block.tensor
                def _(tensor, tt=tt, ss_rdy=ss_rdy):
                    tensor.wait_ge(sDv, ss_rdy)
                    if tt > 0:
                        tensor.wait_ge(sDv, cDv.v)
                    tensor.transpose(pR[:, :], ssb[:, tt * 128:(tt + 1) * 128],
                                     ident[:, :]).then_inc(sPe, 1)
                cPe.add()

                @block.vector
                def _(vector, tt=tt):
                    vector.wait_ge(sPe, cPe.v)
                    vector.tensor_copy(sscol[:, tt:tt + 1], pR[:, 0:1]
                                       ).then_inc(sDv, 1)
                cDv.add()

            @block.vector
            def _(vector):
                vector.wait_ge(sDv, cDv.v)
                vector.tensor_scalar(sscol[:, 0:8], sscol[:, 0:8], 1.0 / H,
                                     RMS_EPS, OP.mult, OP.add).then_inc(sDv, 1)
            cDv.add()

            @block.scalar
            def _(scalar):
                scalar.wait_ge(sDv, cDv.v)
                scalar.activation(sscol[:, 8:16], sscol[:, 0:8], AF.Ln)
                scalar.activation(sscol[:, 8:16], sscol[:, 8:16], AF.Exp,
                                  scale=-0.5).then_inc(sSc, 1)
            cSc.add()
            inv_rdy = cSc.v

            if stop == 9:
                @block.sync
                def _(sync):
                    sync.wait_ge(sSc, inv_rdy)
                return nc

            if debug:
                @block.sync
                def _(sync):
                    sync.wait_ge(sSc, inv_rdy)
                    sync.dma_start(out=dbg["dbg_inv"], in_=sscol[:, :]
                                   ).then_inc(sSt, 16)
                cSt.add(16)

            gctx.close()
            gwctx.close()
            g_ln1cm.__exit__(None, None, None)
            outpre = Ei(nc.sbuf_tensor("outpre", [128, NTT * D], BF16))
            pwlctx = contextlib.ExitStack()
            Epw = pwlctx.enter_context
            pwb = Epw(nc.sbuf_tensor("pwb", [128, 4 * 512], BF16))
            lnrb = Epw(nc.sbuf_tensor("lnrb", [128, NTT * 512], F32))
            pP = Ei(nc.psum_tensor("pP", [128, 4096], F32))

            for cs in range(4):
                @block.scalar
                def _(scalar, cs=cs):
                    if cs == 0:
                        scalar.wait_ge(sL1s, cL1s.v)
                        scalar.wait_ge(sPe, gu_done_pe)
                    else:
                        scalar.wait_ge(sDv, cDv.v)
                    for tt in range(NTT):
                        scalar.dma_start(
                            out=lnrb[:, tt * 512:(tt + 1) * 512],
                            in_=ln1scr[tt * 128:(tt + 1) * 128,
                                       cs * 512:(cs + 1) * 512]
                        ).then_inc(sLnr, 16)
                lnr_all = cLnr.add(16 * NTT)

                for k in range(NPK):
                    po = (k % 4) * 512
                    sPj = (sPj0, sPj1, sPj2, sPj3)[k % 4]

                    @block.sync
                    def _(sync, cs=cs, k=k, po=po, sPj=sPj):
                        if cs == 0 and k == 0:
                            sync.wait_ge(sCc, wq_ag)
                            sync.wait_ge(sL1s, cL1s.v)
                            sync.wait_ge(sPe, gu_done_pe)
                        if k >= 4 or cs > 0:
                            sync.wait_ge(sPe, cPe.v - 3)
                        src = wg_tile(k % 8, 16 + k // 8)
                        sync.dma_start(out=pwb[:, po:po + 512],
                                       in_=src[:, cs * 512:(cs + 1) * 512]
                                       ).then_inc(sPj, 16)
                    pj_ld = cPj[k % 4].add(16)

                    @block.tensor
                    def _(tensor, cs=cs, k=k, po=po, pj_ld=pj_ld,
                          sPj=sPj):
                        tensor.wait_ge(sPj, pj_ld)
                        if k == 0 and cs == 0:
                            tensor.wait_ge(sSc, inv_rdy)
                        if k == 0 and cs > 0:
                            tensor.wait_ge(sDv, cDv.v)
                        last = None
                        for tt in range(NTT):
                            last = tensor.matmul(
                                pP[:, tt * 512:(tt + 1) * 512],
                                hhres[:, k * TC + tt * 128: k * TC + (tt + 1) * 128],
                                pwb[:, po:po + 512],
                                start=(k == 0), stop=(k == NPK - 1),
                                skip_group_check=True)
                        last.then_inc(sPe, 1)
                    cPe.add()

                for tt in range(NTT):
                    @block.vector
                    def _(vector, cs=cs, tt=tt, lnr_all=lnr_all):
                        vector.wait_ge(sLnr, lnr_all)
                        vector.wait_ge(sPe, cPe.v)
                        vector.wait_ge(sSc, inv_rdy)
                        vector.scalar_tensor_tensor(
                            outpre[:, tt * D + cs * 512:
                                   tt * D + (cs + 1) * 512],
                            pP[:, tt * 512:(tt + 1) * 512],
                            sscol[:, 8 + tt:9 + tt],
                            lnrb[:, tt * 512:(tt + 1) * 512],
                            OP.mult, OP.add).then_inc(sDv, 1)
                    cDv.add()
            proj_done = cDv.v

            if stop == 10:
                @block.sync
                def _(sync):
                    sync.wait_ge(sDv, proj_done)
                return nc

            pwlctx.close()
            ln2b = Ei(nc.sbuf_tensor("ln2b", [128, 2 * D], F32))
            sq2 = Ei(nc.sbuf_tensor("sq2", [128, D], BF16))
            st2 = Ei(nc.sbuf_tensor("st2", [128, 8], F32))

            for tt in range(NTT):
                bo = (tt % 2) * D
                src = outpre[:, tt * D:(tt + 1) * D]

                @block.vector
                def _(vector, src=src):
                    vector.tensor_reduce(st2[:, 0:1], src, AX,
                                         OP.add).then_inc(sDv, 1)
                cDv.add()

                @block.scalar
                def _(scalar, src=src):
                    scalar.wait_ge(sDv, cDv.v)
                    scalar.activation(sq2[:, :], src, AF.Square,
                                      accum_out=st2[:, 1:2]).then_inc(sSc, 1)
                cSc.add()

                @block.vector
                def _(vector):
                    vector.wait_ge(sSc, cSc.v)
                    vector.tensor_scalar_mul(st2[:, 2:3], st2[:, 0:1], 1.0 / D)
                    vector.tensor_scalar_mul(st2[:, 3:4], st2[:, 1:2], 1.0 / D)
                    vector.tensor_mul(st2[:, 4:5], st2[:, 2:3], st2[:, 2:3])
                    vector.tensor_sub(st2[:, 3:4], st2[:, 3:4], st2[:, 4:5])
                    vector.tensor_scalar_add(st2[:, 3:4], st2[:, 3:4], LN_EPS
                                             ).then_inc(sDv, 1)
                cDv.add()

                @block.scalar
                def _(scalar):
                    scalar.wait_ge(sDv, cDv.v)
                    scalar.activation(st2[:, 5:6], st2[:, 3:4], AF.Ln)
                    scalar.activation(st2[:, 5:6], st2[:, 5:6], AF.Exp,
                                      scale=-0.5).then_inc(sSc, 1)
                cSc.add()

                @block.vector
                def _(vector, tt=tt, bo=bo, src=src):
                    vector.wait_ge(sSc, cSc.v)
                    if tt >= 2:
                        vector.wait_ge((sJl0, sJl1)[tt % 2], cJl[tt % 2].v)
                    vector.tensor_scalar(ln2b[:, bo:bo + D], src,
                                         st2[:, 2:3], st2[:, 5:6],
                                         OP.subtract, OP.mult).then_inc(sDv, 1)
                cDv.add()

                @block.sync
                def _(sync, tt=tt, bo=bo):
                    sync.wait_ge(sDv, cDv.v)
                    sync.dma_start(out=out_t[tt * 128:(tt + 1) * 128, :],
                                   in_=ln2b[:, bo:bo + D]
                                   ).then_inc((sJl0, sJl1)[tt % 2], 16)
                cJl[tt % 2].add(16)

            @block.sync
            def _(sync):
                sync.wait_ge(sJl0, cJl[0].v)
                sync.wait_ge(sJl1, cJl[1].v)
                sync.wait_ge(sSt, cSt.v)

    return nc


def _stage_inputs(inputs):
    f32 = np.float32
    x = np.ascontiguousarray(np.asarray(inputs["L0_attn_x"], f32).reshape(8192, D))
    W_in = np.ascontiguousarray(np.asarray(inputs["L0_attn_W_in"], f32))
    b_in = np.asarray(inputs["L0_attn_b_in"], f32)
    W_gx = np.ascontiguousarray(np.asarray(inputs["L0_attn_W_gx"], f32)
                                .reshape(GT, 2, 128, 128))
    W_ga = np.ascontiguousarray(np.asarray(inputs["L0_attn_W_ga"], f32)
                                .reshape(GT, 2, 128, 128))
    b_gx = np.asarray(inputs["L0_attn_b_gx"], f32)
    b_ga = np.asarray(inputs["L0_attn_b_ga"], f32)
    a_real = np.asarray(inputs["L0_attn_a_real"], f32)
    a_imag = np.asarray(inputs["L0_attn_a_imag"], f32)
    W_out = np.ascontiguousarray(np.asarray(inputs["L0_attn_W_out"], f32))
    b_out = np.asarray(inputs["L0_attn_b_out"], f32)
    wgu = np.asarray(inputs["L0_ffn_w_gate_up"], f32)
    wpj = np.asarray(inputs["L0_ffn_w_proj"], f32)

    common = dict(
        w_in=W_in,
        b_in_rs=np.ascontiguousarray(b_in.reshape(DT, 128).T),
        w_gx=W_gx, w_ga=W_ga,
        b_gx_rs=np.ascontiguousarray(b_gx.T),
        b_ga_rs=np.ascontiguousarray(b_ga.T),
        ar_rs=np.ascontiguousarray(a_real.reshape(GT, 128).T),
        ai_rs=np.ascontiguousarray(a_imag.reshape(GT, 128).T),
        w_out=W_out,
        b_out_bc=np.ascontiguousarray(np.broadcast_to(b_out, (128, D))),
        ident=np.eye(128, dtype=ml_dtypes.bfloat16),
    )

    in_maps = []
    for c in range(NCORES):
        gu_rows = []
        for j in range(8):
            i = 8 * j + c
            gu_rows.append(wgu[128 * i:128 * (i + 1)])
            gu_rows.append(wgu[H + 128 * i:H + 128 * (i + 1)])
        wgu_shard = np.ascontiguousarray(np.stack(gu_rows))
        pj_blocks = []
        for m in range(8):
            k = 8 * m + c
            pj_blocks.append(wpj[:, 128 * k:128 * (k + 1)].reshape(16, 128, 128))
        wpj_shard = np.ascontiguousarray(np.stack(pj_blocks))
        m = dict(common)
        m.update(
            x=np.ascontiguousarray(x[TC * c:TC * (c + 1)]),
            wgu_sh=wgu_shard, wpj_sh=wpj_shard,
            mask_lt=np.ascontiguousarray(np.broadcast_to(
                (np.arange(NCORES) < c).astype(f32), (128, NCORES))),
        )
        in_maps.append(m)
    return in_maps


def kernel(**inputs):
    in_maps = _stage_inputs(inputs)
    nc = build_nc(debug=_DEBUG)
    res = run_bass_kernel_spmd(nc, in_maps, core_ids=list(range(NCORES)))
    out = np.concatenate([np.asarray(r["out"]) for r in res.results], axis=0)
    return out.reshape(1, 8192, D)


# version 15
# speedup vs baseline: 1.0491x; 1.0155x over previous
import contextlib
import numpy as np
import ml_dtypes

import concourse.bass as bass
import concourse.mybir as mybir
from concourse.bass_utils import run_bass_kernel_spmd

F32 = mybir.dt.float32
BF16 = mybir.dt.bfloat16
AF = mybir.ActivationFunctionType
OP = mybir.AluOpType
AX = mybir.AxisListType.X

NCORES = 8
TC = 1024
NTT = 8
D = 2048
DT = 16
GT = 8
H = 8192
NPAIR = 64
NPK = 64
W = 64
CH = 128
NCH = 8
NW = 16

MAGIC = 12582912.0
TWO_PI = 6.283185307179586
INV_2PI = 0.15915494309189535
HALF_PI = 1.5707963267948966
LNH = -0.34657359027997264
TINY = 1e-35
LN_EPS = 1e-5
RMS_EPS = 1.1920929e-07

_DEBUG = False
BARRIER_AG = False
RACE_DETECT = False
RAW_GROUPS = {"lrli", "tts1", "rr", "w1w2", "big", "tts2", "ext"}
A_LEVEL = 3


class _SerEngine:

    _PASS = {"wait_ge", "drain", "sem_inc", "nop", "engine_nop", "wait_op"}
    _HAZARD = {"tensor_reduce", "reduce_sum", "reduce_max", "reciprocal"}

    def __init__(self, eng, always_drain):
        self._e = eng
        self._always = always_drain

    def __getattr__(self, n):
        f = getattr(self._e, n)
        if n in self._PASS or not callable(f):
            return f
        def wrap(*a, **k):
            r = f(*a, **k)
            if RACE_DETECT or self._always:
                self._e.drain()
            return r
        return wrap


class Ctr:
    def __init__(self):
        self.v = 0

    def add(self, n=1):
        self.v += n
        return self.v


def build_nc(debug=False, stop=None, skip_prepass=False, skip_ag=False):
    def _R(eng, grp="x"):
        return eng._e if (grp in RAW_GROUPS and not RACE_DETECT) else eng

    nc = bass.Bass("TRN2", target_bir_lowering=False, debug=False,
                   detect_race_conditions=RACE_DETECT)

    x_in = nc.dram_tensor("x", [TC, D], F32, kind="ExternalInput").ap()
    w_in = nc.dram_tensor("w_in", [D, D], F32, kind="ExternalInput").ap()
    b_in_rs = nc.dram_tensor("b_in_rs", [128, DT], F32, kind="ExternalInput").ap()
    w_gx = nc.dram_tensor("w_gx", [GT, 2, 128, 128], F32, kind="ExternalInput").ap()
    w_ga = nc.dram_tensor("w_ga", [GT, 2, 128, 128], F32, kind="ExternalInput").ap()
    b_gx_rs = nc.dram_tensor("b_gx_rs", [128, GT], F32, kind="ExternalInput").ap()
    b_ga_rs = nc.dram_tensor("b_ga_rs", [128, GT], F32, kind="ExternalInput").ap()
    ar_rs = nc.dram_tensor("ar_rs", [128, GT], F32, kind="ExternalInput").ap()
    ai_rs = nc.dram_tensor("ai_rs", [128, GT], F32, kind="ExternalInput").ap()
    w_out = nc.dram_tensor("w_out", [D, D], F32, kind="ExternalInput").ap()
    b_out_bc = nc.dram_tensor("b_out_bc", [128, D], F32, kind="ExternalInput").ap()
    wgu_sh = nc.dram_tensor("wgu_sh", [16, 128, D], F32, kind="ExternalInput").ap()
    wpj_sh = nc.dram_tensor("wpj_sh", [8, 16, 128, 128], F32,
                            kind="ExternalInput").ap()
    mask_lt = nc.dram_tensor("mask_lt", [128, NCORES], F32,
                             kind="ExternalInput").ap()
    ident_in = nc.dram_tensor("ident", [128, 128], BF16, kind="ExternalInput").ap()

    out_t = nc.dram_tensor("out", [TC, D], F32, kind="ExternalOutput").ap()

    dbg = {}
    if debug:
        for name, shape, dt_ in [
            ("dbg_xh", [128, DT * TC], F32),
            ("dbg_ga", [128, GT * TC], F32),
            ("dbg_gx", [128, GT * TC], F32),
            ("dbg_h", [128, DT * TC], BF16),
            ("dbg_ln1", [NTT * 128, D], F32),
            ("dbg_inv", [128, 16], F32),
            ("dbg_wq", [24 * 128, D], BF16),
            ("dbg_hin", [128, 256], F32),
            ("dbg_scr", [NCH * 4 * 128, 1024], F32),
            ("dbg_summ", [128, 512], F32),
        ]:
            dbg[name] = nc.dram_tensor(name, shape, dt_, kind="ExternalOutput").ap()

    wshard = nc.dram_tensor("wshard", [24 * 128, D], BF16).ap()
    wgath1 = nc.dram_tensor("wgath1", [NCORES * 16 * 128, D], BF16).ap()
    wgath2 = nc.dram_tensor("wgath2", [NCORES * 8 * 128, D], BF16).ap()
    scanscr = nc.dram_tensor("scanscr", [NCH * 4 * 128, 1024], F32).ap()
    cc_in = nc.dram_tensor("cc_in", [128, 32], F32).ap()
    cc_out = nc.dram_tensor("cc_out", [NCORES * 128, 32], F32).ap()
    ln1scr = nc.dram_tensor("ln1scr", [NTT * 128, D], F32).ap()
    hhscr = nc.dram_tensor("hhscr", [NPK * 128, TC], BF16).ap()
    outscr = nc.dram_tensor("outscr", [NTT * 128, D], F32).ap()

    def wg_tile(owner, pos):
        if pos < 16:
            base = (owner * 16 + pos) * 128
            return wgath1[base:base + 128, :]
        base = (owner * 8 + pos - 16) * 128
        return wgath2[base:base + 128, :]

    with contextlib.ExitStack() as ctx:
        E = ctx.enter_context
        sLd = E(nc.semaphore("sLd"))
        sSt = E(nc.semaphore("sSt"))
        sGp = E(nc.semaphore("sGp"))
        sCc = E(nc.semaphore("sCc"))
        sPe = E(nc.semaphore("sPe"))
        sDv = E(nc.semaphore("sDv"))
        sSc = E(nc.semaphore("sSc"))
        cLd, cSt, cGp, cCc, cPe, cDv, cSc = (Ctr() for _ in range(7))

        _block = E(nc.Block())

        class _BlockShim:
            def __init__(self, b):
                self._b = b

            def vector(self, f):
                return self._b.vector(lambda eng: f(_SerEngine(eng, True)))

            def scalar(self, f):
                return self._b.scalar(lambda eng: f(_SerEngine(eng, False)))

            def __getattr__(self, n):
                return getattr(self._b, n)

        block = _BlockShim(_block)

        sCst = E(nc.semaphore("sCst"))
        sWqL = E(nc.semaphore("sWqL"))
        sWsT = E(nc.semaphore("sWsT"))
        sWsP = E(nc.semaphore("sWsP"))
        sXl0 = E(nc.semaphore("sXl0"))
        sXl1 = E(nc.semaphore("sXl1"))
        sWn0 = E(nc.semaphore("sWn0"))
        sWn1 = E(nc.semaphore("sWn1"))
        sGts = E(nc.semaphore("sGts"))
        sScr = E(nc.semaphore("sScr"))
        sScr1 = E(nc.semaphore("sScr1"))
        sCo0 = E(nc.semaphore("sCo0"))
        sCo1 = E(nc.semaphore("sCo1"))
        sXr0 = E(nc.semaphore("sXr0"))
        sXr1 = E(nc.semaphore("sXr1"))
        sBo = E(nc.semaphore("sBo"))
        sWo = E(nc.semaphore("sWo"))
        sL1s = E(nc.semaphore("sL1s"))
        sLbb = E(nc.semaphore("sLbb"))
        sGu0 = E(nc.semaphore("sGu0"))
        sGu1 = E(nc.semaphore("sGu1"))
        sHhS = E(nc.semaphore("sHhS"))
        sPj0 = E(nc.semaphore("sPj0"))
        sPj1 = E(nc.semaphore("sPj1"))
        sPj2 = E(nc.semaphore("sPj2"))
        sPj3 = E(nc.semaphore("sPj3"))
        sLnr = E(nc.semaphore("sLnr"))
        sOsc = E(nc.semaphore("sOsc"))
        sOsc1 = E(nc.semaphore("sOsc1"))
        sJl0 = E(nc.semaphore("sJl0"))
        sJl1 = E(nc.semaphore("sJl1"))
        sCgI = E(nc.semaphore("sCgI"))
        (cCst, cWqL, cWsT, cWsP, cGts, cScr, cScr1, cWo, cL1s, cLbb, cHhS,
         cLnr, cOsc, cOsc1, cCgI) = (Ctr() for _ in range(15))
        cXl = [Ctr(), Ctr()]
        cWn = [Ctr(), Ctr()]
        cCo = [Ctr(), Ctr()]
        cXr = [Ctr(), Ctr()]
        cGu = [Ctr(), Ctr()]
        cPj = [Ctr(), Ctr(), Ctr(), Ctr()]
        cJl = [Ctr(), Ctr()]

        ident = E(nc.sbuf_tensor("identB", [128, 128], BF16))
        consts = E(nc.sbuf_tensor("consts", [128, 64], F32))
        onesb = E(nc.sbuf_tensor("onesb", [128, 1], BF16))
        ones_row = E(nc.sbuf_tensor("ones_row", [1, 128], BF16))
        nc.const_aps.register(F32, HALF_PI, consts[:, 56:57])
        nc.const_aps.register(F32, LNH, consts[:, 57:58])
        nc.const_aps.register(F32, 0.25, consts[:, 58:59])
        nc.const_aps.register(F32, MAGIC, consts[:, 59:60])
        nc.const_aps.register(F32, -MAGIC, consts[:, 60:61])
        ones64 = E(nc.sbuf_tensor("ones64", [128, W], F32))
        summ = E(nc.sbuf_tensor("summ", [128, 4 * 128], F32))
        hinb = E(nc.sbuf_tensor("hinb", [128, 256], F32))
        sgath = E(nc.sbuf_tensor("sgath", [128, NCORES * 32], F32))

        @block.sync
        def _(sync):
            sync.dma_start(out=ident[:, :], in_=ident_in).then_inc(sCst, 16)
            sync.dma_start(out=consts[:, 0:16], in_=b_in_rs).then_inc(sCst, 16)
            sync.dma_start(out=consts[:, 16:24], in_=b_gx_rs).then_inc(sCst, 16)
            sync.dma_start(out=consts[:, 24:32], in_=b_ga_rs).then_inc(sCst, 16)
            sync.dma_start(out=consts[:, 32:40], in_=ar_rs).then_inc(sCst, 16)
            sync.dma_start(out=consts[:, 40:48], in_=ai_rs).then_inc(sCst, 16)
            sync.dma_start(out=consts[:, 48:56], in_=mask_lt).then_inc(sCst, 16)
        consts_ld = cCst.add(16 * 7)

        @block.scalar
        def _(scalar):
            scalar.wait_ge(sCst, consts_ld)
            scalar.activation(consts[:, 32:40], consts[:, 32:40], AF.Exp)
            scalar.activation(consts[:, 32:40], consts[:, 32:40], AF.Ln,
                              bias=1.0).then_inc(sSc, 1)
        cSc.add()

        @block.vector
        def _(vector):
            vector.wait_ge(sSc, cSc.v)
            vector.tensor_scalar_mul(consts[:, 32:40], consts[:, 32:40], -8.0)
            vector.memset(consts[:, 56:57], HALF_PI)
            vector.memset(consts[:, 57:58], LNH)
            vector.memset(consts[:, 58:59], 0.25)
            vector.memset(consts[:, 59:60], MAGIC)
            vector.memset(consts[:, 60:61], -MAGIC)
            vector.memset(onesb[:, :], 1.0)
            vector.memset(ones_row[:, :], 1.0)
            vector.memset(ones64[:, :], 1.0).then_inc(sDv, 1)
        consts_rdy = cDv.add()

        g_xh = nc.sbuf_tensor("xh", [128, DT * TC], F32)
        xh = g_xh.__enter__()
        g_fb = nc.sbuf_tensor("fbuf", [128, 1112], F32)
        fbuf = g_fb.__enter__()
        g_ga = nc.sbuf_tensor("ga", [128, GT * TC], F32)
        ga = g_ga.__enter__()
        g_gx = nc.sbuf_tensor("gx", [128, GT * TC], F32)
        gx = g_gx.__enter__()

        pctx = contextlib.ExitStack()
        p1_qrdy = {}
        if not skip_prepass:
            Ep = pctx.enter_context
            wq_b = Ep(nc.sbuf_tensor("wq_b", [128, D], BF16, side="right"))
            wq_s = Ep(nc.sbuf_tensor("wq_s", [128, 32], F32, side="right"))
            wq_sb = Ep(nc.sbuf_tensor("wq_sb", [128, 32], BF16, side="right"))
            wq_q = Ep(nc.sbuf_tensor("wq_q", [128, D], F32, side="right"))
            wq_hat = Ep(nc.sbuf_tensor("wq_hat", [128, D], BF16, side="right"))
            wq_T = Ep(nc.sbuf_tensor("wq_T", [128, D], BF16, side="right"))
            pq = Ep(nc.psum_tensor("pq", [128, 512], BF16, side="right"))

        def emit_p1_a(pos):
            if skip_prepass:
                return

            @block.gpsimd
            def _(gpsimd, pos=pos):
                if pos > 0:
                    gpsimd.wait_ge(sDv, cDv.v)
                if pos < 16:
                    gpsimd.dma_start(out=wq_b[:, :], in_=wgu_sh[pos]
                                     ).then_inc(sWqL, 16)
                else:
                    src = wpj_sh[pos - 16].rearrange("a p f -> p a f")
                    dst = wq_b.ap().rearrange("p (a f) -> p a f", f=128)
                    gpsimd.dma_start(out=dst, in_=src).then_inc(sWqL, 16)
            cWqL.add(16)

            @block.vector
            def _(vector, pos=pos):
                vector.wait_ge(sWqL, cWqL.v)
                if pos > 0:
                    vector.wait_ge(sPe, cPe.v)
                    if pos <= 16:
                        vector.wait_ge(sWsT, cWsT.v)
                    else:
                        vector.wait_ge(sWsP, cWsP.v)
                b3 = wq_b.ap().rearrange("p (g j) -> p g j", j=W)
                vector.tensor_reduce(wq_s[:, :], b3, AX, OP.add,
                                     apply_absolute_value=True)
                vector.tensor_scalar_mul(wq_sb[:, :], wq_s[:, :], 1.0 / W)
                vector.tensor_scalar_max(wq_q[:, 0:32], wq_sb[:, :], 1e-8)
                vector.reciprocal(wq_s[:, :], wq_q[:, 0:32])
                rb = wq_s.ap()[:, :, None].broadcast_to([128, 32, W])
                h3 = wq_hat.ap().rearrange("p (g j) -> p g j", j=W)
                vector.tensor_tensor(h3, b3, rb, OP.mult)
                vector.tensor_scalar(wq_q[:, :], wq_hat[:, :], MAGIC, MAGIC,
                                     OP.add, OP.subtract)
                vector.tensor_scalar(wq_q[:, :], wq_q[:, :], 1.0, -1.0,
                                     OP.min, OP.max)
                sb3 = wq_sb.ap()[:, :, None].broadcast_to([128, 32, W])
                q3 = wq_q.ap().rearrange("p (g j) -> p g j", j=W)
                vector.tensor_tensor(h3, q3, sb3, OP.mult).then_inc(sDv, 1)
            cDv.add()
            p1_qrdy[pos] = cDv.v

        def emit_p1_b(pos):
            if skip_prepass:
                return
            q_rdy = p1_qrdy[pos]
            for grp in range(4):
                @block.tensor
                def _(tensor, pos=pos, grp=grp, q_rdy=q_rdy):
                    tensor.wait_ge(sDv, q_rdy)
                    tensor.wait_ge(sCst, consts_ld)
                    if not (pos == 0 and grp == 0):
                        tensor.wait_ge(sDv, cDv.v)
                    last = None
                    for j in range(4):
                        k = grp * 4 + j
                        last = tensor.transpose(
                            pq[:, j * 128:(j + 1) * 128],
                            wq_hat[:, k * 128:(k + 1) * 128], ident[:, :])
                    last.then_inc(sPe, 1)
                cPe.add()

                @block.vector
                def _(vector, grp=grp):
                    vector.wait_ge(sPe, cPe.v)
                    vector.tensor_copy(wq_T[:, grp * 512:(grp + 1) * 512],
                                       pq[:, :]).then_inc(sDv, 1)
                cDv.add()

            @block.gpsimd
            def _(gpsimd, pos=pos):
                gpsimd.wait_ge(sDv, cDv.v)
                sem = sWsT if pos < 16 else sWsP
                gpsimd.dma_start(out=wshard[pos * 128:(pos + 1) * 128, :],
                                 in_=wq_T[:, :]).then_inc(sem, 16)
            (cWsT if pos < 16 else cWsP).add(16)

        with contextlib.ExitStack() as actx:
            Ea = actx.enter_context
            xcm = Ea(nc.sbuf_tensor("xcm", [128, DT * TC], BF16))
            xtmb = Ea(nc.sbuf_tensor("xtmb", [128, D], BF16))
            pA = Ea(nc.psum_tensor("pA", [128, 2048], BF16))

            for tt in range(NTT):
                bo = 0
                sXl = (sXl0, sXl1)[tt % 2]
                cXl_ = cXl[tt % 2]

                @block.gpsimd
                def _(gpsimd, tt=tt, bo=bo, sXl=sXl):
                    if tt >= 1:
                        gpsimd.wait_ge(sPe, cPe.v)
                    gpsimd.dma_start(out=xtmb[:, bo:bo + D],
                                     in_=x_in[tt * 128:(tt + 1) * 128, :]
                                     ).then_inc(sXl, 16)
                x_ld = cXl_.add(16)

                for grp in range(4 if A_LEVEL >= 2 else 0):
                    po = (grp % 2) * 1024

                    @block.tensor
                    def _(tensor, tt=tt, grp=grp, bo=bo, po=po, x_ld=x_ld,
                          sXl=sXl):
                        tensor.wait_ge(sXl, x_ld)
                        tensor.wait_ge(sCst, consts_ld)
                        if tt * 4 + grp >= 2:
                            tensor.wait_ge(sDv, cDv.v)
                        last = None
                        for j in range(4):
                            k = grp * 4 + j
                            last = tensor.transpose(
                                pA[:, po + j * 128: po + (j + 1) * 128],
                                xtmb[:, bo + k * 128: bo + (k + 1) * 128],
                                ident[:, :])
                        last.then_inc(sPe, 1)
                    cPe.add()

                    if A_LEVEL >= 3:
                        @block.vector
                        def _(vector, tt=tt, grp=grp, po=po):
                            vector.wait_ge(sPe, cPe.v)
                            dst = xcm.ap().rearrange("p (j t) -> p j t", t=TC)[
                                :, grp * 4:(grp + 1) * 4, tt * 128:(tt + 1) * 128]
                            src = pA.ap()[:, po:po + 512].rearrange(
                                "p (j t) -> p j t", t=128)
                            vector.tensor_copy(dst, src).then_inc(sDv, 1)
                        cDv.add()
                    else:
                        @block.vector
                        def _(vector):
                            vector.wait_ge(sPe, cPe.v)
                            nc.vector.engine_nop().then_inc(sDv, 1)
                        cDv.add()

            if stop == 2:
                @block.sync
                def _(sync):
                    sync.wait_ge(sDv, cDv.v)
                actx.close()
                g_gx.__exit__(None, None, None)
                g_ga.__exit__(None, None, None)
                g_xh.__exit__(None, None, None)
                return nc

            with contextlib.ExitStack() as bctx:
                Eb = bctx.enter_context
                winj = Eb(nc.sbuf_tensor("winj", [128, 2 * D], BF16))
                pB = Eb(nc.psum_tensor("pB", [128, 1024], F32))

                w_in_r = w_in.rearrange("(k p) (j f) -> p k j f", p=128, f=128)
                x_cm_rdy = cDv.v
                n = 0
                for j in range(DT):
                    wo = (j % 2) * D
                    sWn = (sWn0, sWn1)[j % 2]

                    @block.gpsimd
                    def _(gpsimd, j=j, wo=wo, sWn=sWn):
                        if j >= 2:
                            gpsimd.wait_ge(sPe, cPe.v)
                        dst = winj.ap()[:, wo:wo + D].rearrange(
                            "p (k f) -> p k f", f=128)
                        gpsimd.dma_start(out=dst, in_=w_in_r[:, :, j, :]
                                         ).then_inc(sWn, 16)
                    wj_ld = cWn[j % 2].add(16)

                    for ts_ in range(2):
                        po = (n % 2) * 512

                        @block.tensor
                        def _(tensor, j=j, ts_=ts_, po=po, n=n, wo=wo,
                              wj_ld=wj_ld, sWn=sWn):
                            tensor.wait_ge(sWn, wj_ld)
                            tensor.wait_ge(sDv, x_cm_rdy)
                            if n >= 2:
                                tensor.wait_ge(sSc, cSc.v - 1)
                            last = None
                            for k in range(DT):
                                last = tensor.matmul(
                                    pB[:, po:po + 512],
                                    winj[:, wo + k * 128: wo + (k + 1) * 128],
                                    xcm[:, k * TC + ts_ * 512:
                                        k * TC + ts_ * 512 + 512],
                                    start=(k == 0), stop=(k == DT - 1))
                            last.then_inc(sPe, 1)
                        cPe.add()

                        @block.scalar
                        def _(scalar, j=j, ts_=ts_, po=po):
                            scalar.wait_ge(sPe, cPe.v)
                            scalar.activation(
                                xh[:, j * TC + ts_ * 512: j * TC + ts_ * 512 + 512],
                                pB[:, po:po + 512], AF.Identity,
                                bias=consts[:, j:j + 1]).then_inc(sSc, 1)
                        cSc.add()
                        n += 1
                    if j > 0:
                        emit_p1_b(j - 1)
                    emit_p1_a(j)
                xh_rdy_sc = cSc.v

        if stop == 3:
            @block.sync
            def _(sync):
                sync.wait_ge(sSc, xh_rdy_sc)
            g_gx.__exit__(None, None, None)
            g_ga.__exit__(None, None, None)
            g_xh.__exit__(None, None, None)
            return nc

        if debug:
            @block.sync
            def _(sync):
                sync.wait_ge(sSc, xh_rdy_sc)
                sync.dma_start(out=dbg["dbg_xh"], in_=xh[:, :]).then_inc(sSt, 16)
            cSt.add(16)

        with contextlib.ExitStack() as cctx:
            Ec = cctx.enter_context
            xhb = Ec(nc.sbuf_tensor("xhb", [128, DT * TC], BF16))
            wgb = Ec(nc.sbuf_tensor("wgb", [128, GT * 4 * 128], BF16))
            pC = Ec(nc.psum_tensor("pC", [128, 1024], F32))

            @block.gpsimd
            def _(gpsimd):
                gpsimd.wait_ge(sSc, xh_rdy_sc)
                for r in range(DT):
                    gpsimd.dma_start(out=xhb[:, r * TC:(r + 1) * TC],
                                     in_=xh[:, r * TC:(r + 1) * TC]
                                     ).then_inc(sGts, 16)
                for g in range(GT):
                    for k in range(2):
                        gpsimd.dma_start(
                            out=wgb[:, (g * 4 + k) * 128:(g * 4 + k + 1) * 128],
                            in_=w_gx[g, k]).then_inc(sGts, 16)
                        gpsimd.dma_start(
                            out=wgb[:, (g * 4 + 2 + k) * 128:(g * 4 + 3 + k) * 128],
                            in_=w_ga[g, k]).then_inc(sGts, 16)
            gates_ld = cGts.add(16 * (DT + GT * 4))

            n = 0
            for g in range(GT):
                for half in range(2):
                    for ts_ in range(2):
                        po = (n % 2) * 512

                        @block.tensor
                        def _(tensor, g=g, half=half, ts_=ts_, po=po, n=n):
                            tensor.wait_ge(sGts, gates_ld)
                            if n >= 2:
                                tensor.wait_ge(sSc, cSc.v - 1)
                            last = None
                            for k in range(2):
                                lb = (g * 4 + half * 2 + k) * 128
                                rbase = (2 * g + k) * TC + ts_ * 512
                                last = tensor.matmul(
                                    pC[:, po:po + 512],
                                    wgb[:, lb:lb + 128],
                                    xhb[:, rbase:rbase + 512],
                                    start=(k == 0), stop=(k == 1))
                            last.then_inc(sPe, 1)
                        cPe.add()

                        @block.scalar
                        def _(scalar, g=g, half=half, ts_=ts_, po=po):
                            scalar.wait_ge(sPe, cPe.v)
                            dstbuf = gx if half == 0 else ga
                            bcol = (16 if half == 0 else 24) + g
                            scalar.activation(
                                dstbuf[:, g * TC + ts_ * 512:
                                       g * TC + ts_ * 512 + 512],
                                pC[:, po:po + 512], AF.Sigmoid,
                                bias=consts[:, bcol:bcol + 1]).then_inc(sSc, 1)
                        cSc.add()
                        n += 1
                        if n % 4 == 0:
                            m = n // 4 - 1
                            emit_p1_b(15 + m)
                            emit_p1_a(16 + m)
            gates_rdy = cSc.v

        if stop == 4:
            @block.sync
            def _(sync):
                sync.wait_ge(sSc, gates_rdy)
            g_gx.__exit__(None, None, None)
            g_ga.__exit__(None, None, None)
            g_xh.__exit__(None, None, None)
            return nc

        if debug:
            @block.sync
            def _(sync):
                sync.wait_ge(sSc, gates_rdy)
                sync.dma_start(out=dbg["dbg_ga"], in_=ga[:, :]).then_inc(sSt, 16)
                sync.dma_start(out=dbg["dbg_gx"], in_=gx[:, :]).then_inc(sSt, 16)
            cSt.add(32)

        if not skip_prepass:
            emit_p1_b(23)
        pctx.close()

        def emit_wq_ag():
            if skip_prepass:
                return cCc.v, cCc.v
            if skip_ag:
                @block.gpsimd
                def _(gpsimd):
                    gpsimd.wait_ge(sWsT, cWsT.v)
                    gpsimd.wait_ge(sWsP, cWsP.v)
                    gpsimd.nop().then_inc(sCc, 2)
            else:
                @block.gpsimd
                def _(gpsimd):
                    gpsimd.wait_ge(sWsT, 16 * 16)
                    gpsimd.collective_compute(
                        "AllGather", OP.bypass,
                        replica_groups=[list(range(NCORES))],
                        ins=[wshard[0:16 * 128, :]], outs=[wgath1],
                    ).then_inc(sCc, 1)
                    gpsimd.wait_ge(sWsP, cWsP.v)
                    gpsimd.collective_compute(
                        "AllGather", OP.bypass,
                        replica_groups=[list(range(NCORES))],
                        ins=[wshard[16 * 128:, :]], outs=[wgath2],
                    ).then_inc(sCc, 1)
            return cCc.add(), cCc.add()

        wq_ag1, wq_ag = emit_wq_ag()

        if stop == 1:
            @block.sync
            def _(sync):
                sync.wait_ge(sCc, wq_ag)
            return nc

        @block.vector
        def _(vector):
            vector.wait_ge(sSc, gates_rdy)
            vector.wait_ge(sDv, consts_rdy)
            nc.vector.engine_nop().then_inc(sDv, 1)
        xc_rdy = cDv.add()

        NSLOT = 16
        g_pool = nc.sbuf_tensor("pool", [128, NSLOT * 1024], F32)
        poolt = g_pool.__enter__()

        def _mkP(par):
            def sl(s):
                if s in (16, 17):
                    return s - 16
                if par and s in (2, 3, 4, 5):
                    return s + 10
                return s
            Pf = lambda s: poolt[:, sl(s) * 1024:(sl(s) + 1) * 1024]
            P3f = lambda s, t=CH: poolt.ap()[:, sl(s) * 1024:
                                             (sl(s) + 1) * 1024].rearrange(
                "p (r t) -> p r t", t=t)
            return Pf, P3f

        scr_mark = []
        for c in range(NCH):
            t0 = c * CH
            P_, P3 = _mkP(c % 2)

            def gsl3(buf, off=0, t0=t0):
                return buf.ap().rearrange("p (r t) -> p r t", t=TC)[
                    :, off:off + GT, t0:t0 + CH]

            @block.scalar
            def _(scalar, c=c, t0=t0):
                scalar.wait_ge(sSc, gates_rdy)
                if c == 0:
                    scalar.wait_ge(sDv, xc_rdy)
                if c > 0:
                    scalar.wait_ge(sDv, cDv.v)
                rs = _R(scalar, "lrli")
                last = None
                for r in range(GT):
                    sl = ga[:, r * TC + t0: r * TC + t0 + CH]
                    rs.mul(P_(0)[:, r * CH:(r + 1) * CH], sl,
                           consts[:, 32 + r:33 + r])
                    last = rs.mul(P_(1)[:, r * CH:(r + 1) * CH], sl,
                                  consts[:, 40 + r:41 + r])
                last.then_inc(sSc, 1)
            cSc.add()

            @block.vector
            def _(vector, c=c, t0=t0,
                  sm=(scr_mark[c - 2] if c >= 2 else 0)):
                vector.wait_ge(sSc, cSc.v)
                if c == 0:
                    vector.wait_ge(sWsT, cWsT.v)
                    vector.wait_ge(sWsP, cWsP.v)
                if c >= 2:
                    vector.wait_ge((sScr, sScr1)[c % 2], sm)
                rv = _R(vector, "tts1")
                last = None
                for r in range(GT):
                    for w in range(2):
                        a = r * CH + w * W
                        rv.tensor_tensor_scan(
                            P_(2)[:, a:a + W], ones64[:, :], P_(0)[:, a:a + W],
                            0.0, OP.mult, OP.add)
                        last = rv.tensor_tensor_scan(
                            P_(3)[:, a:a + W], ones64[:, :], P_(1)[:, a:a + W],
                            0.0, OP.mult, OP.add)
                vector.drain()
                last.then_inc(sDv, 1)
            cDv.add()

            @block.scalar
            def _(scalar, c=c):
                scalar.wait_ge(sDv, cDv.v)
                rs = _R(scalar, "luts")
                rs.activation(P_(4), P_(0), AF.Exp, scale=2.0)
                rs.activation(P_(5), P_(0), AF.Exp, scale=4.0)
                rs.activation(P_(6), P_(2), AF.Exp)
                rs.activation(P_(7), P_(2), AF.Exp, scale=-1.0)
                last = rs.activation(P_(8), P_(1), AF.Sin, scale=2.0,
                                     bias=HALF_PI)
                last.then_inc(sSc, 1)
            cSc.add()

            @block.vector
            def _(vector, c=c):
                vector.wait_ge(sSc, cSc.v)
                rv = _R(vector, "rr")
                rv.tensor_scalar(P_(0), P_(3), INV_2PI, 0.25, OP.mult, OP.add)
                rv.tensor_scalar(P_(9), P_(3), INV_2PI, None, OP.mult)
                rv.tensor_scalar(P_(0), P_(0), MAGIC, MAGIC, OP.add, OP.subtract)
                rv.tensor_scalar(P_(9), P_(9), MAGIC, MAGIC, OP.add, OP.subtract)
                rv.scalar_tensor_tensor(P_(0), P_(0), -TWO_PI, P_(3),
                                        OP.mult, OP.add)
                rv.scalar_tensor_tensor(P_(2), P_(9), -TWO_PI, P_(3),
                                        OP.mult, OP.add)
                rv.tensor_scalar(P_(0), P_(0), HALF_PI - 1e-6,
                                 -3.0 * HALF_PI + 1e-6, OP.min, OP.max)
                rv.tensor_scalar(P_(2), P_(2), TWO_PI / 2 - 1e-6,
                                 -TWO_PI / 2 + 1e-6, OP.min, OP.max)
                vector.tensor_mul(P_(9), P_(4), P_(8))
                vector.scalar_tensor_tensor(P_(4), P_(9), -2.0, P_(5),
                                            OP.mult, OP.add)
                vector.tensor_scalar(P_(4), P_(4), 1.0, TINY, OP.add, OP.max
                                     ).then_inc(sDv, 1)
            cDv.add()

            @block.scalar
            def _(scalar, c=c):
                scalar.wait_ge(sDv, cDv.v)
                scalar.activation(P_(5), P_(4), AF.Ln)
                scalar.activation(P_(5), P_(5), AF.Exp, scale=0.5
                                  ).then_inc(sSc, 1)
            cSc.add()

            @block.vector
            def _(vector, c=c):
                vector.wait_ge(sSc, cSc.v)
                rv = _R(vector, "w1w2")
                rv.tensor_sub(P_(8), P_(5), P_(9))
                rv.tensor_add(P_(9), P_(5), P_(9))
                rv.tensor_scalar(P_(8), P_(8), 1.0, TINY, OP.add, OP.max)
                vector.tensor_scalar(P_(9), P_(9), -1.0, TINY, OP.add, OP.max
                                     ).then_inc(sDv, 1)
            cDv.add()

            @block.scalar
            def _(scalar, c=c):
                scalar.wait_ge(sDv, cDv.v)
                scalar.activation(P_(10), P_(8), AF.Ln)
                scalar.activation(P_(10), P_(10), AF.Exp, scale=0.5, bias=LNH)
                scalar.activation(P_(11), P_(9), AF.Ln)
                scalar.activation(P_(11), P_(11), AF.Exp, scale=0.5, bias=LNH)
                scalar.activation(P_(16), P_(0), AF.Sin, bias=HALF_PI)
                scalar.activation(P_(17), P_(2), AF.Sin).then_inc(sSc, 1)
            cSc.add()

            @block.vector
            def _(vector, c=c, t0=t0):
                vector.wait_ge(sSc, cSc.v)
                rv = _R(vector, "big")
                rv.tensor_mul(P_(2), P_(6), P_(16))
                rv.tensor_mul(P_(3), P_(6), P_(17))
                rv.tensor_mul(P_(16), P_(7), P_(16))
                rv.tensor_mul(P_(17), P_(7), P_(17))
                rv.tensor_mul(P_(6), P_(10), P_(16))
                rv.tensor_mul(P_(7), P_(11), P_(17))
                rv.tensor_mul(P_(9), P_(10), P_(17))
                rv.tensor_mul(P_(8), P_(11), P_(16))
                rv.tensor_sub(P_(6), P_(6), P_(7))
                rv.tensor_add(P_(7), P_(9), P_(8))
                rv.tensor_tensor(P3(6), gsl3(gx), P3(6), OP.mult)
                rv.tensor_tensor(P3(7), gsl3(gx), P3(7), OP.mult)
                rv.tensor_tensor(P3(10), gsl3(xh), P3(6), OP.mult)
                rv.tensor_tensor(P3(16), gsl3(xh, 8), P3(6), OP.mult)
                rv.tensor_tensor(P3(11), gsl3(xh, 8), P3(7), OP.mult)
                rv.tensor_tensor(P3(17), gsl3(xh), P3(7), OP.mult)
                rv.tensor_add(P_(10), P_(10), P_(11))
                rv.tensor_sub(P_(9), P_(16), P_(17))
                vector.drain()
                rv2 = _R(vector, "tts2")
                for r in range(GT):
                    for w in range(2):
                        a = r * CH + w * W
                        rv2.tensor_tensor_scan(
                            P_(4)[:, a:a + W], ones64[:, :], P_(10)[:, a:a + W],
                            0.0, OP.mult, OP.add)
                        rv2.tensor_tensor_scan(
                            P_(5)[:, a:a + W], ones64[:, :], P_(9)[:, a:a + W],
                            0.0, OP.mult, OP.add)
                vector.drain()
                rv3 = _R(vector, "ext")
                last = None
                for src, col in ((2, 0), (3, 1), (4, 2), (5, 3)):
                    sap = P3(src)[:, :, W - 1::W]
                    dap = summ.ap()[:, col * 128:(col + 1) * 128].rearrange(
                        "p (r w) -> p r w", w=16)[:, :, 2 * c:2 * c + 2]
                    last = rv3.tensor_copy(dap, sap)
                vector.drain()
                last.then_inc(sDv, 1)
            cDv.add()

            @block.sync
            def _(sync, c=c):
                sync.wait_ge(sDv, cDv.v)
                for i, s in enumerate((4, 5, 2, 3)):
                    base = (c * 4 + i) * 128
                    sync.dma_start(out=scanscr[base:base + 128, :],
                                   in_=P_(s)).then_inc((sScr, sScr1)[c % 2], 16)
            (cScr, cScr1)[c % 2].add(64)
            scr_mark.append((cScr, cScr1)[c % 2].v)
        scan_dv = cDv.v
        g_pool.__exit__(None, None, None)
        g_gx.__exit__(None, None, None)
        g_ga.__exit__(None, None, None)
        _FO = {0: (0, 512), 1: (512, 256), 2: (768, 40), 3: (808, 32),
               4: (840, 16), 5: (856, 256)}
        P_ = lambda s: fbuf[:, _FO[s][0]:_FO[s][0] + _FO[s][1]]

        if stop == 5:
            @block.sync
            def _(sync):
                sync.wait_ge(sDv, scan_dv)
                sync.wait_ge(sScr, cScr.v)
                sync.wait_ge(sScr1, cScr1.v)
            g_fb.__exit__(None, None, None)
            g_xh.__exit__(None, None, None)
            return nc

        g_ln1cm = nc.sbuf_tensor("ln1cm", [128, DT * TC], BF16, side="right")
        ln1cm = g_ln1cm.__enter__()
        g_wob = nc.sbuf_tensor("woutb", [128, DT * D], BF16, side="right")
        woutb = g_wob.__enter__()
        g_bob = nc.sbuf_tensor("b_out_r", [1, D], BF16, side="right")
        b_out_r = g_bob.__enter__()
        g_xres = nc.sbuf_tensor("xres", [128, 2 * D], F32, side="right")
        xres = g_xres.__enter__()

        @block.gpsimd
        def _(gpsimd):
            gpsimd.wait_ge(sDv, scan_dv)
            gpsimd.wait_ge(sScr, cScr.v)
            gpsimd.wait_ge(sScr1, cScr1.v)
            for k in range(DT):
                gpsimd.dma_start(out=woutb[:, k * D:(k + 1) * D],
                                 in_=w_out[k * 128:(k + 1) * 128, :]
                                 ).then_inc(sWo, 16)
            gpsimd.dma_start(out=b_out_r[:, :],
                             in_=b_out_bc[0:1, :]).then_inc(sBo, 16)
            gpsimd.dma_start(out=xres[:, 0:D],
                             in_=x_in[0:128, :]).then_inc(sOsc, 16)
            gpsimd.dma_start(out=xres[:, D:2 * D],
                             in_=x_in[128:256, :]).then_inc(sOsc1, 16)
        wout_ld = cWo.add(16 * DT)
        cOsc.add(16)
        cOsc1.add(16)

        def wsl(buf, base, w):
            return buf.ap()[:, base:base + 128].rearrange(
                "p (r w) -> p r w", w=16)[:, :, w:w + 1]

        @block.vector
        def _(vector):
            vector.wait_ge(sDv, scan_dv)
            vector.wait_ge(sScr, cScr.v)
            vector.wait_ge(sScr1, cScr1.v)
            vector.tensor_mul(P_(0)[:, 0:128], summ[:, 0:128], summ[:, 256:384])
            vector.tensor_mul(P_(0)[:, 128:256], summ[:, 128:256], summ[:, 384:512])
            vector.tensor_mul(P_(0)[:, 256:384], summ[:, 0:128], summ[:, 384:512])
            vector.tensor_mul(P_(0)[:, 384:512], summ[:, 128:256], summ[:, 256:384])
            vector.tensor_sub(summ[:, 256:384], P_(0)[:, 0:128], P_(0)[:, 128:256])
            vector.tensor_add(summ[:, 384:512], P_(0)[:, 256:384], P_(0)[:, 384:512])
            vector.memset(hinb[:, :], 0.0)
            vector.memset(P_(1)[:, 0:128], 1.0)
            vector.memset(P_(1)[:, 128:256], 0.0)
            t0_ = P_(2)[:, 0:8].rearrange("p (r w) -> p r w", w=1)
            t1_ = P_(2)[:, 8:16].rearrange("p (r w) -> p r w", w=1)
            for w in range(1, NW):
                ArP, AiP = wsl(summ, 0, w - 1), wsl(summ, 128, w - 1)
                BrP, BiP = wsl(summ, 256, w - 1), wsl(summ, 384, w - 1)
                HrP, HiP = wsl(hinb, 0, w - 1), wsl(hinb, 128, w - 1)
                Hr, Hi = wsl(hinb, 0, w), wsl(hinb, 128, w)
                cArP, cAiP = wsl(fbuf, 512, w - 1), wsl(fbuf, 640, w - 1)
                cAr, cAi = wsl(fbuf, 512, w), wsl(fbuf, 640, w)
                vector.tensor_tensor(t0_, ArP, HrP, OP.mult)
                vector.tensor_tensor(t1_, AiP, HiP, OP.mult)
                vector.tensor_tensor(t0_, t0_, t1_, OP.subtract)
                vector.tensor_tensor(Hr, t0_, BrP, OP.add)
                vector.tensor_tensor(t0_, ArP, HiP, OP.mult)
                vector.tensor_tensor(t1_, AiP, HrP, OP.mult)
                vector.tensor_tensor(t0_, t0_, t1_, OP.add)
                vector.tensor_tensor(Hi, t0_, BiP, OP.add)
                vector.tensor_tensor(t0_, cArP, ArP, OP.mult)
                vector.tensor_tensor(t1_, cAiP, AiP, OP.mult)
                vector.tensor_tensor(cAr, t0_, t1_, OP.subtract)
                vector.tensor_tensor(t0_, cArP, AiP, OP.mult)
                vector.tensor_tensor(t1_, cAiP, ArP, OP.mult)
                vector.tensor_tensor(cAi, t0_, t1_, OP.add)
            ArL, AiL = wsl(summ, 0, NW - 1), wsl(summ, 128, NW - 1)
            BrL, BiL = wsl(summ, 256, NW - 1), wsl(summ, 384, NW - 1)
            HrL, HiL = wsl(hinb, 0, NW - 1), wsl(hinb, 128, NW - 1)
            cArL, cAiL = wsl(fbuf, 512, NW - 1), wsl(fbuf, 640, NW - 1)
            pk = lambda a, b: P_(3)[:, a:b].rearrange("p (r w) -> p r w", w=1)
            vector.tensor_tensor(t0_, cArL, ArL, OP.mult)
            vector.tensor_tensor(t1_, cAiL, AiL, OP.mult)
            vector.tensor_tensor(pk(0, 8), t0_, t1_, OP.subtract)
            vector.tensor_tensor(t0_, cArL, AiL, OP.mult)
            vector.tensor_tensor(t1_, cAiL, ArL, OP.mult)
            vector.tensor_tensor(pk(8, 16), t0_, t1_, OP.add)
            vector.tensor_tensor(t0_, ArL, HrL, OP.mult)
            vector.tensor_tensor(t1_, AiL, HiL, OP.mult)
            vector.tensor_tensor(t0_, t0_, t1_, OP.subtract)
            vector.tensor_tensor(pk(16, 24), t0_, BrL, OP.add)
            vector.tensor_tensor(t0_, ArL, HiL, OP.mult)
            vector.tensor_tensor(t1_, AiL, HrL, OP.mult)
            vector.tensor_tensor(t0_, t0_, t1_, OP.add)
            vector.tensor_tensor(pk(24, 32), t0_, BiL, OP.add).then_inc(sDv, 1)
        summ_rdy = cDv.add()

        @block.gpsimd
        def _(gpsimd):
            gpsimd.wait_ge(sDv, summ_rdy)
            gpsimd.dma_start(out=cc_in, in_=P_(3)[:, 0:32]).then_inc(sCgI, 16)
            gpsimd.wait_ge(sCgI, cCgI.add(16))
            gpsimd.collective_compute(
                "AllGather", OP.bypass,
                replica_groups=[list(range(NCORES))],
                ins=[cc_in], outs=[cc_out],
            ).then_inc(sCc, 1)
            gpsimd.wait_ge(sCc, cCc.add())
            last = None
            for j in range(NCORES):
                last = gpsimd.dma_start(out=sgath[:, 32 * j:32 * (j + 1)],
                                        in_=cc_out[128 * j:128 * (j + 1), :])
                last.then_inc(sCgI, 16)
            cCgI.add(16 * NCORES)
        carry_ld = cCgI.v

        @block.vector
        def _(vector):
            vector.wait_ge(sCgI, carry_ld)
            vector.memset(P_(4)[:, 0:16], 0.0)
            hr = P_(4)[:, 0:8].rearrange("p (r w) -> p r w", w=1)
            hi = P_(4)[:, 8:16].rearrange("p (r w) -> p r w", w=1)
            t0_ = P_(2)[:, 0:8].rearrange("p (r w) -> p r w", w=1)
            t1_ = P_(2)[:, 8:16].rearrange("p (r w) -> p r w", w=1)
            t2_ = P_(2)[:, 16:24].rearrange("p (r w) -> p r w", w=1)
            t3_ = P_(2)[:, 24:32].rearrange("p (r w) -> p r w", w=1)
            t4_ = P_(2)[:, 32:40].rearrange("p (r w) -> p r w", w=1)
            for j in range(NCORES):
                g = lambda a, b: sgath.ap()[:, 32 * j + a:32 * j + b].rearrange(
                    "p (r w) -> p r w", w=1)
                m = consts[:, 48 + j:49 + j]
                vector.tensor_scalar(t2_, g(0, 8), -1.0, None, OP.add)
                vector.tensor_scalar(t2_, t2_, m, 1.0, OP.mult, OP.add)
                vector.tensor_scalar(t3_, g(8, 16), m, None, OP.mult)
                vector.tensor_tensor(t0_, t2_, hr, OP.mult)
                vector.tensor_tensor(t1_, t3_, hi, OP.mult)
                vector.tensor_tensor(t0_, t0_, t1_, OP.subtract)
                vector.tensor_tensor(t1_, t2_, hi, OP.mult)
                vector.tensor_tensor(t4_, t3_, hr, OP.mult)
                vector.tensor_tensor(hi, t1_, t4_, OP.add)
                vector.tensor_scalar(t1_, g(16, 24), m, None, OP.mult)
                vector.tensor_tensor(hr, t0_, t1_, OP.add)
                vector.tensor_scalar(t1_, g(24, 32), m, None, OP.mult)
                vector.tensor_tensor(hi, hi, t1_, OP.add)
            hcr = P_(4)[:, 0:8][:, :, None].broadcast_to([128, 8, 16])
            hci = P_(4)[:, 8:16][:, :, None].broadcast_to([128, 8, 16])
            cAr = P_(1)[:, 0:128].rearrange("p (r w) -> p r w", w=16)
            cAi = P_(1)[:, 128:256].rearrange("p (r w) -> p r w", w=16)
            hR = hinb.ap()[:, 0:128].rearrange("p (r w) -> p r w", w=16)
            hI = hinb.ap()[:, 128:256].rearrange("p (r w) -> p r w", w=16)
            p5a = P_(5)[:, 0:128].rearrange("p (r w) -> p r w", w=16)
            p5b = P_(5)[:, 128:256].rearrange("p (r w) -> p r w", w=16)
            vector.tensor_tensor(p5a, cAr, hcr, OP.mult)
            vector.tensor_tensor(p5b, cAi, hci, OP.mult)
            vector.tensor_tensor(p5a, p5a, p5b, OP.subtract)
            vector.tensor_tensor(hR, hR, p5a, OP.add)
            vector.tensor_tensor(p5a, cAr, hci, OP.mult)
            vector.tensor_tensor(p5b, cAi, hcr, OP.mult)
            vector.tensor_tensor(p5a, p5a, p5b, OP.add)
            vector.tensor_tensor(hI, hI, p5a, OP.add).then_inc(sDv, 1)
        hin_rdy = cDv.add()

        if stop == 6:
            @block.sync
            def _(sync):
                sync.wait_ge(sDv, hin_rdy)
            g_fb.__exit__(None, None, None)
            g_xh.__exit__(None, None, None)
            return nc

        if debug:
            @block.sync
            def _(sync):
                sync.wait_ge(sDv, hin_rdy)
                sync.dma_start(out=dbg["dbg_hin"], in_=hinb[:, :]).then_inc(sSt, 16)
                sync.dma_start(out=dbg["dbg_summ"], in_=summ[:, :]).then_inc(sSt, 16)
                sync.dma_start(out=dbg["dbg_scr"], in_=scanscr).then_inc(sSt, 16)
            cSt.add(48)

        g_fb.__exit__(None, None, None)
        g_xh.__exit__(None, None, None)

        g_hb = nc.sbuf_tensor("hb", [128, DT * TC], BF16, side="right")
        hb = g_hb.__enter__()
        g_corr = nc.sbuf_tensor("corr", [128, 10 * 1024], F32, side="right")
        corr = g_corr.__enter__()
        C_ = lambda s: corr[:, s * 1024:(s + 1) * 1024]
        C3 = lambda s, t=CH: corr.ap()[:, s * 1024:(s + 1) * 1024].rearrange(
            "p (r t) -> p r t", t=t)

        g_mark = []
        for c in range(NCH):
            b = 2 + (c % 2) * 4
            sCo = (sCo0, sCo1)[c % 2]

            @block.sync
            def _(sync, c=c, b=b, sCo=sCo):
                sync.wait_ge(sDv, hin_rdy)
                if c == 0:
                    sync.wait_ge(sWsT, cWsT.v)
                    sync.wait_ge(sWsP, cWsP.v)
                if c >= 2:
                    sync.wait_ge(sDv, cDv.v)
                for i in range(4):
                    base = (c * 4 + i) * 128
                    sync.dma_start(out=C_(b + i), in_=scanscr[base:base + 128, :]
                                   ).then_inc(sCo, 16)
            co_ld = cCo[c % 2].add(64)

            @block.vector
            def _(vector, c=c, b=b, sCo=sCo, co_ld=co_ld):
                vector.wait_ge(sCo, co_ld)
                Sr, Si, cP, sP = b, b + 1, b + 2, b + 3
                r4 = lambda s: C3(s).rearrange("p r (w t) -> p r w t", t=W)
                hR = hinb.ap()[:, 0:128].rearrange("p (r w) -> p r w", w=16)[
                    :, :, 2 * c:2 * c + 2][:, :, :, None].broadcast_to(
                    [128, 8, 2, W])
                hI = hinb.ap()[:, 128:256].rearrange("p (r w) -> p r w", w=16)[
                    :, :, 2 * c:2 * c + 2][:, :, :, None].broadcast_to(
                    [128, 8, 2, W])
                vector.tensor_tensor(r4(Sr), r4(Sr), hR, OP.add)
                vector.tensor_tensor(r4(Si), r4(Si), hI, OP.add)
                vector.tensor_mul(C_(0), C_(cP), C_(Sr))
                vector.tensor_mul(C_(1), C_(sP), C_(Si))
                hre = hb.ap().rearrange("p (r t) -> p r t", t=TC)[
                    :, 0:8, c * CH:(c + 1) * CH]
                vector.tensor_tensor(hre, C3(0), C3(1), OP.subtract)
                vector.tensor_mul(C_(0), C_(cP), C_(Si))
                vector.tensor_mul(C_(1), C_(sP), C_(Sr))
                him = hb.ap().rearrange("p (r t) -> p r t", t=TC)[
                    :, 8:16, c * CH:(c + 1) * CH]
                vector.tensor_tensor(him, C3(0), C3(1), OP.add).then_inc(sDv, 1)
            g_mark.append(cDv.add())
        h_rdy = cDv.v
        g_corr.__exit__(None, None, None)

        if stop == 7:
            @block.sync
            def _(sync):
                sync.wait_ge(sDv, h_rdy)
            return nc

        if debug:
            @block.sync
            def _(sync):
                sync.wait_ge(sDv, h_rdy)
                sync.dma_start(out=dbg["dbg_h"], in_=hb[:, :]).then_inc(sSt, 16)
            cSt.add(16)

        with contextlib.ExitStack() as hctx:
            Eh = hctx.enter_context
            r1b = Eh(nc.sbuf_tensor("r1b", [128, 2 * D], F32))
            ln1b = Eh(nc.sbuf_tensor("ln1b", [128, D], F32))
            ln1bb = Eh(nc.sbuf_tensor("ln1bb", [128, 2 * D], BF16))
            sq_s = Eh(nc.sbuf_tensor("sq_s", [128, D], BF16))
            stat = Eh(nc.sbuf_tensor("stat", [128, 8], F32))
            pH = Eh(nc.psum_tensor("pH", [128, 1024], F32))
            pT = Eh(nc.psum_tensor("pT", [128, 512], BF16))

            cast_mark = []
            tp_mark = {}
            tp_ev = []

            def emit_ln1_tp(tt):
                for grp in range(4):
                    @block.tensor
                    def _(tensor, tt=tt, grp=grp,
                          ev=(tp_ev[-1] if tp_ev else None)):
                        tensor.wait_ge(sLbb, cast_mark[tt])
                        if ev is not None:
                            tensor.wait_ge(sDv, ev)
                        last = None
                        for j in range(4):
                            k = grp * 4 + j
                            last = tensor.transpose(
                                pT[:, j * 128:(j + 1) * 128],
                                ln1bb[:, (tt % 2) * D + k * 128:
                                      (tt % 2) * D + (k + 1) * 128],
                                ident[:, :])
                        last.then_inc(sPe, 1)
                    cPe.add()

                    @block.vector
                    def _(vector, tt=tt, grp=grp):
                        vector.wait_ge(sPe, cPe.v)
                        dst = ln1cm.ap().rearrange("p (j t) -> p j t", t=TC)[
                            :, grp * 4:(grp + 1) * 4, tt * 128:(tt + 1) * 128]
                        src = pT.ap().rearrange("p (j t) -> p j t", t=128)
                        vector.tensor_copy(dst, src).then_inc(sDv, 1)
                    tp_ev.append(cDv.add())
                tp_mark[tt] = cPe.v

            nH = 0
            for tt in range(NTT):
                bo = (tt % 2) * D

                sXr = (sXr0, sXr1)[tt % 2]

                if tt >= 2:
                    @block.sync
                    def _(sync, tt=tt, bo=bo, sXr=sXr):
                        sync.wait_ge(sDv, cDv.v)
                        sync.dma_start(out=xres[:, bo:bo + D],
                                       in_=x_in[tt * 128:(tt + 1) * 128, :]
                                       ).then_inc(sXr, 16)
                    cXr[tt % 2].add(16)
                xr_ld = cXr[tt % 2].v

                for cs in range(4):
                    po = (nH % 2) * 512

                    @block.tensor
                    def _(tensor, tt=tt, cs=cs, po=po, nH=nH):
                        tensor.wait_ge(sDv, g_mark[tt])
                        tensor.wait_ge(sWo, wout_ld)
                        if tt == 0 and cs == 0:
                            tensor.wait_ge(sBo, 16)
                        if nH >= 2:
                            tensor.wait_ge(sDv, cDv.v - 1)
                        tensor.matmul(pH[:, po:po + 512], ones_row[0:1, :],
                                      b_out_r[0:1, cs * 512:(cs + 1) * 512],
                                      start=True, stop=False)
                        last = None
                        for k in range(DT):
                            last = tensor.matmul(
                                pH[:, po:po + 512],
                                hb[:, k * TC + tt * 128: k * TC + (tt + 1) * 128],
                                woutb[:, k * D + cs * 512: k * D + (cs + 1) * 512],
                                start=False, stop=(k == DT - 1))
                        last.then_inc(sPe, 1)
                    cPe.add()

                    @block.vector
                    def _(vector, cs=cs, po=po, bo=bo, tt=tt, xr_ld=xr_ld,
                          sXr=sXr):
                        vector.wait_ge(sPe, cPe.v)
                        if cs == 0:
                            if tt < 2:
                                vector.wait_ge((sOsc, sOsc1)[tt], 16)
                            else:
                                vector.wait_ge(sXr, xr_ld)
                        vector.scalar_tensor_tensor(
                            r1b[:, bo + cs * 512: bo + (cs + 1) * 512],
                            pH[:, po:po + 512], 1.0,
                            xres[:, bo + cs * 512: bo + (cs + 1) * 512],
                            OP.mult, OP.add).then_inc(sDv, 1)
                    cDv.add()
                    nH += 1

                @block.vector
                def _(vector, bo=bo):
                    vector.wait_ge(sDv, cDv.v)
                    vector.tensor_reduce(stat[:, 0:1], r1b[:, bo:bo + D], AX,
                                         OP.add).then_inc(sDv, 1)
                cDv.add()

                @block.scalar
                def _(scalar, bo=bo):
                    scalar.wait_ge(sDv, cDv.v)
                    scalar.activation(sq_s[:, :], r1b[:, bo:bo + D], AF.Square,
                                      accum_out=stat[:, 1:2]).then_inc(sSc, 1)
                cSc.add()

                @block.vector
                def _(vector):
                    vector.wait_ge(sSc, cSc.v)
                    vector.tensor_scalar_mul(stat[:, 2:3], stat[:, 0:1], 1.0 / D)
                    vector.tensor_scalar_mul(stat[:, 3:4], stat[:, 1:2], 1.0 / D)
                    vector.tensor_mul(stat[:, 4:5], stat[:, 2:3], stat[:, 2:3])
                    vector.tensor_sub(stat[:, 3:4], stat[:, 3:4], stat[:, 4:5])
                    vector.tensor_scalar_add(stat[:, 3:4], stat[:, 3:4], LN_EPS
                                             ).then_inc(sDv, 1)
                cDv.add()

                @block.scalar
                def _(scalar):
                    scalar.wait_ge(sDv, cDv.v)
                    scalar.activation(stat[:, 5:6], stat[:, 3:4], AF.Ln)
                    scalar.activation(stat[:, 5:6], stat[:, 5:6], AF.Exp,
                                      scale=-0.5).then_inc(sSc, 1)
                cSc.add()

                @block.vector
                def _(vector, bo=bo):
                    vector.wait_ge(sSc, cSc.v)
                    vector.wait_ge(sL1s, cL1s.v)
                    vector.wait_ge(sLbb, cLbb.v)
                    vector.tensor_scalar(ln1b[:, :], r1b[:, bo:bo + D],
                                         stat[:, 2:3], stat[:, 5:6],
                                         OP.subtract, OP.mult).then_inc(sDv, 1)
                cDv.add()
                ln1_rdy = cDv.v

                @block.sync
                def _(sync, tt=tt, ln1_rdy=ln1_rdy):
                    sync.wait_ge(sDv, ln1_rdy)
                    sync.dma_start(out=ln1scr[tt * 128:(tt + 1) * 128, :],
                                   in_=ln1b[:, :]).then_inc(sL1s, 16)
                    if debug:
                        sync.dma_start(out=dbg["dbg_ln1"][tt * 128:(tt + 1) * 128, :],
                                       in_=ln1b[:, :]).then_inc(sSt, 16)
                cL1s.add(16)
                if debug:
                    cSt.add(16)

                @block.gpsimd
                def _(gpsimd, tt=tt, ln1_rdy=ln1_rdy,
                      tm=(tp_mark[tt - 2] if tt >= 2 else 0)):
                    gpsimd.wait_ge(sDv, ln1_rdy)
                    if tt >= 2:
                        gpsimd.wait_ge(sPe, tm)
                    gpsimd.dma_start(out=ln1bb[:, (tt % 2) * D:(tt % 2 + 1) * D],
                                     in_=ln1b[:, :]).then_inc(sLbb, 16)
                cast_mark.append(cLbb.add(16))

                if tt > 0:
                    emit_ln1_tp(tt - 1)
            emit_ln1_tp(NTT - 1)
            ln1_all = cDv.v

        if stop == 8:
            @block.sync
            def _(sync):
                sync.wait_ge(sDv, ln1_all)
                sync.wait_ge(sLd, cLd.v)
            return nc

        g_hb.__exit__(None, None, None)
        g_xres.__exit__(None, None, None)
        g_bob.__exit__(None, None, None)
        g_wob.__exit__(None, None, None)
        with contextlib.ExitStack() as ictx:
            Ei = ictx.enter_context
            hhres = Ei(nc.sbuf_tensor("hhres", [128, NPAIR * TC], BF16))
            silu_s = Ei(nc.sbuf_tensor("silu_s", [128, TC], F32))
            sqh = Ei(nc.sbuf_tensor("sqh", [128, 2 * TC], BF16))
            ssrow = Ei(nc.sbuf_tensor("ssrow", [128, TC], F32))
            sscol = Ei(nc.sbuf_tensor("sscol", [128, 16], F32))
            ssb = Ei(nc.sbuf_tensor("ssb", [128, TC], BF16))
            gwctx = contextlib.ExitStack()
            gw = gwctx.enter_context(nc.sbuf_tensor("gw", [128, 2 * D], BF16))
            uw = gwctx.enter_context(nc.sbuf_tensor("uw", [128, 2 * D], BF16))
            gctx = contextlib.ExitStack()
            Eg = gctx.enter_context
            pG = Eg(nc.psum_tensor("pG", [128, 1024], F32))
            pU = Eg(nc.psum_tensor("pU", [128, 1024], F32))
            pS = Eg(nc.psum_tensor("pS", [1, 1024], F32))
            pR = Eg(nc.psum_tensor("pR", [128, 128], BF16))

            mm_mark = {}
            sq_mark = {}
            for i in range(NPAIR):
                bo = (i % 2) * D
                ho = (i % 2) * TC
                sGu = (sGu0, sGu1)[i % 2]

                @block.sync
                def _(sync, i=i, bo=bo, sGu=sGu,
                      mw=(mm_mark[i - 2] if i >= 2 else 0)):
                    sync.wait_ge(sCc, wq_ag1)
                    if i >= 2:
                        sync.wait_ge(sPe, mw)
                    sync.dma_start(out=gw[:, bo:bo + D],
                                   in_=wg_tile(i % 8, 2 * (i // 8))
                                   ).then_inc(sGu, 16)
                    sync.dma_start(out=uw[:, bo:bo + D],
                                   in_=wg_tile(i % 8, 2 * (i // 8) + 1)
                                   ).then_inc(sGu, 16)
                gu_ld = cGu[i % 2].add(32)

                for ts_ in range(2):
                    po = ts_ * 512

                    @block.tensor
                    def _(tensor, i=i, ts_=ts_, po=po, bo=bo, gu_ld=gu_ld,
                          sGu=sGu):
                        tensor.wait_ge(sGu, gu_ld)
                        tensor.wait_ge(sDv, ln1_all)
                        if i > 0:
                            tensor.wait_ge(sDv, cDv.v - 1)
                        lastg = None
                        for k in range(DT):
                            tensor.matmul(
                                pG[:, po:po + 512],
                                gw[:, bo + k * 128: bo + (k + 1) * 128],
                                ln1cm[:, k * TC + ts_ * 512:
                                      k * TC + ts_ * 512 + 512],
                                start=(k == 0), stop=(k == DT - 1))
                        for k in range(DT):
                            lastg = tensor.matmul(
                                pU[:, po:po + 512],
                                uw[:, bo + k * 128: bo + (k + 1) * 128],
                                ln1cm[:, k * TC + ts_ * 512:
                                      k * TC + ts_ * 512 + 512],
                                start=(k == 0), stop=(k == DT - 1))
                        lastg.then_inc(sPe, 1)
                    cPe.add()

                    @block.scalar
                    def _(scalar, ts_=ts_, po=po):
                        scalar.wait_ge(sPe, cPe.v)
                        scalar.activation(silu_s[:, po:po + 512],
                                          pG[:, po:po + 512], AF.Silu
                                          ).then_inc(sSc, 1)
                    cSc.add()

                    @block.vector
                    def _(vector, i=i, ts_=ts_, po=po):
                        vector.wait_ge(sSc, cSc.v)
                        if i == 0 and ts_ == 0:
                            vector.wait_ge(sL1s, cL1s.v)
                            vector.wait_ge(sLbb, cLbb.v)
                            if debug:
                                vector.wait_ge(sSt, cSt.v)
                        vector.tensor_mul(hhres[:, i * TC + po: i * TC + po + 512],
                                          silu_s[:, po:po + 512],
                                          pU[:, po:po + 512]).then_inc(sDv, 1)
                    cDv.add()

                    @block.scalar
                    def _(scalar, i=i, ts_=ts_, po=po):
                        scalar.wait_ge(sDv, cDv.v)
                        scalar.activation(sqh[:, ((i % 2) * 2 + ts_) * 512:
                                              ((i % 2) * 2 + ts_) * 512 + 512],
                                          hhres[:, i * TC + po: i * TC + po + 512],
                                          AF.Square).then_inc(sSc, 1)
                    cSc.add()
                mm_mark[i] = cPe.v
                sq_mark[i] = cSc.v

                if i > 0:
                    @block.tensor
                    def _(tensor, i=i, sw=sq_mark[i - 1]):
                        tensor.wait_ge(sSc, sw)
                        for ts_ in range(2):
                            tensor.matmul(pS[:, ts_ * 512:(ts_ + 1) * 512],
                                          onesb[:, :],
                                          sqh[:, (((i - 1) % 2) * 2 + ts_) * 512:
                                              (((i - 1) % 2) * 2 + ts_) * 512 + 512],
                                          start=(i - 1 == 0), stop=False,
                                          skip_group_check=True).then_inc(sPe, 1)
                            cPe.add()

            @block.tensor
            def _(tensor):
                tensor.wait_ge(sSc, cSc.v)
                last = None
                for ts_ in range(2):
                    last = tensor.matmul(
                        pS[:, ts_ * 512:(ts_ + 1) * 512], onesb[:, :],
                        sqh[:, (((NPAIR - 1) % 2) * 2 + ts_) * 512:
                            (((NPAIR - 1) % 2) * 2 + ts_) * 512 + 512],
                        start=False, stop=True, skip_group_check=True)
                last.then_inc(sPe, 1)
            cPe.add()
            gu_done_pe = cPe.v

            @block.vector
            def _(vector):
                vector.wait_ge(sPe, gu_done_pe)
                vector.memset(ssrow[:, :], 0.0)
                vector.tensor_copy(ssrow[0:1, :], pS[:, :])
                vector.tensor_copy(ssb[:, :], ssrow[:, :]).then_inc(sDv, 1)
            cDv.add()
            ss_rdy = cDv.v

            for tt in range(NTT):
                @block.tensor
                def _(tensor, tt=tt, ss_rdy=ss_rdy):
                    tensor.wait_ge(sDv, ss_rdy)
                    if tt > 0:
                        tensor.wait_ge(sDv, cDv.v)
                    tensor.transpose(pR[:, :], ssb[:, tt * 128:(tt + 1) * 128],
                                     ident[:, :]).then_inc(sPe, 1)
                cPe.add()

                @block.vector
                def _(vector, tt=tt):
                    vector.wait_ge(sPe, cPe.v)
                    vector.tensor_copy(sscol[:, tt:tt + 1], pR[:, 0:1]
                                       ).then_inc(sDv, 1)
                cDv.add()

            @block.vector
            def _(vector):
                vector.wait_ge(sDv, cDv.v)
                vector.tensor_scalar(sscol[:, 0:8], sscol[:, 0:8], 1.0 / H,
                                     RMS_EPS, OP.mult, OP.add).then_inc(sDv, 1)
            cDv.add()

            @block.scalar
            def _(scalar):
                scalar.wait_ge(sDv, cDv.v)
                scalar.activation(sscol[:, 8:16], sscol[:, 0:8], AF.Ln)
                scalar.activation(sscol[:, 8:16], sscol[:, 8:16], AF.Exp,
                                  scale=-0.5).then_inc(sSc, 1)
            cSc.add()
            inv_rdy = cSc.v

            if stop == 9:
                @block.sync
                def _(sync):
                    sync.wait_ge(sSc, inv_rdy)
                return nc

            if debug:
                @block.sync
                def _(sync):
                    sync.wait_ge(sSc, inv_rdy)
                    sync.dma_start(out=dbg["dbg_inv"], in_=sscol[:, :]
                                   ).then_inc(sSt, 16)
                cSt.add(16)

            gctx.close()
            gwctx.close()
            g_ln1cm.__exit__(None, None, None)
            outpre = Ei(nc.sbuf_tensor("outpre", [128, NTT * D], BF16))
            sq2 = Ei(nc.sbuf_tensor("sq2", [128, 512], BF16))
            st2 = Ei(nc.sbuf_tensor("st2", [128, 48], F32))
            pwlctx = contextlib.ExitStack()
            Epw = pwlctx.enter_context
            pwb = Epw(nc.sbuf_tensor("pwb", [128, 4 * 512], BF16))
            lnrb = Epw(nc.sbuf_tensor("lnrb", [128, NTT * 512], F32))
            pP = Ei(nc.psum_tensor("pP", [128, 4096], F32))

            for cs in range(4):
                @block.scalar
                def _(scalar, cs=cs):
                    if cs == 0:
                        scalar.wait_ge(sL1s, cL1s.v)
                        scalar.wait_ge(sPe, gu_done_pe)
                    else:
                        scalar.wait_ge(sDv, cDv.v)
                    for tt in range(NTT):
                        scalar.dma_start(
                            out=lnrb[:, tt * 512:(tt + 1) * 512],
                            in_=ln1scr[tt * 128:(tt + 1) * 128,
                                       cs * 512:(cs + 1) * 512]
                        ).then_inc(sLnr, 16)
                lnr_all = cLnr.add(16 * NTT)

                for k in range(NPK):
                    po = (k % 4) * 512
                    sPj = (sPj0, sPj1, sPj2, sPj3)[k % 4]

                    @block.sync
                    def _(sync, cs=cs, k=k, po=po, sPj=sPj):
                        if cs == 0 and k == 0:
                            sync.wait_ge(sCc, wq_ag)
                            sync.wait_ge(sL1s, cL1s.v)
                            sync.wait_ge(sPe, gu_done_pe)
                        if k >= 4 or cs > 0:
                            sync.wait_ge(sPe, cPe.v - 3)
                        src = wg_tile(k % 8, 16 + k // 8)
                        sync.dma_start(out=pwb[:, po:po + 512],
                                       in_=src[:, cs * 512:(cs + 1) * 512]
                                       ).then_inc(sPj, 16)
                    pj_ld = cPj[k % 4].add(16)

                    @block.tensor
                    def _(tensor, cs=cs, k=k, po=po, pj_ld=pj_ld,
                          sPj=sPj):
                        tensor.wait_ge(sPj, pj_ld)
                        if k == 0 and cs == 0:
                            tensor.wait_ge(sSc, inv_rdy)
                        if k == 0 and cs > 0:
                            tensor.wait_ge(sDv, cDv.v)
                        last = None
                        for tt in range(NTT):
                            last = tensor.matmul(
                                pP[:, tt * 512:(tt + 1) * 512],
                                hhres[:, k * TC + tt * 128: k * TC + (tt + 1) * 128],
                                pwb[:, po:po + 512],
                                start=(k == 0), stop=(k == NPK - 1),
                                skip_group_check=True)
                        last.then_inc(sPe, 1)
                    cPe.add()

                for tt in range(NTT):
                    osl = outpre[:, tt * D + cs * 512: tt * D + (cs + 1) * 512]

                    @block.vector
                    def _(vector, cs=cs, tt=tt, osl=osl, lnr_all=lnr_all):
                        vector.wait_ge(sLnr, lnr_all)
                        vector.wait_ge(sPe, cPe.v)
                        vector.wait_ge(sSc, inv_rdy)
                        vector.scalar_tensor_tensor(
                            osl, pP[:, tt * 512:(tt + 1) * 512],
                            sscol[:, 8 + tt:9 + tt],
                            lnrb[:, tt * 512:(tt + 1) * 512],
                            OP.mult, OP.add).then_inc(sDv, 1)
                    stt_mark = cDv.add()

                    @block.vector
                    def _(vector, cs=cs, tt=tt, osl=osl):
                        if cs == 0:
                            vector.tensor_reduce(st2[:, tt:tt + 1], osl, AX,
                                                 OP.add).then_inc(sDv, 1)
                        else:
                            vector.tensor_reduce(st2[:, 16:17], osl, AX,
                                                 OP.add)
                            vector.tensor_add(st2[:, tt:tt + 1],
                                              st2[:, tt:tt + 1],
                                              st2[:, 16:17]).then_inc(sDv, 1)
                    cDv.add()

                    @block.scalar
                    def _(scalar, cs=cs, tt=tt, osl=osl, m=stt_mark):
                        scalar.wait_ge(sDv, m)
                        ac = st2[:, 8 + tt:9 + tt] if cs == 0 else st2[:, 17:18]
                        scalar.activation(sq2[:, :], osl, AF.Square,
                                          accum_out=ac).then_inc(sSc, 1)
                    sq_mk = cSc.add()

                    if cs > 0:
                        @block.vector
                        def _(vector, tt=tt, m=sq_mk):
                            vector.wait_ge(sSc, m)
                            vector.tensor_add(st2[:, 8 + tt:9 + tt],
                                              st2[:, 8 + tt:9 + tt],
                                              st2[:, 17:18]).then_inc(sDv, 1)
                        cDv.add()
            proj_done = cDv.v

            if stop == 10:
                @block.sync
                def _(sync):
                    sync.wait_ge(sDv, proj_done)
                return nc

            pwlctx.close()
            ln2b = Ei(nc.sbuf_tensor("ln2b", [128, 2 * D], F32))

            @block.vector
            def _(vector):
                vector.wait_ge(sSc, cSc.v)
                vector.tensor_scalar_mul(st2[:, 24:32], st2[:, 0:8], 1.0 / D)
                vector.tensor_scalar_mul(st2[:, 32:40], st2[:, 8:16], 1.0 / D)
                vector.tensor_mul(st2[:, 16:24], st2[:, 24:32], st2[:, 24:32])
                vector.tensor_sub(st2[:, 32:40], st2[:, 32:40], st2[:, 16:24])
                vector.tensor_scalar_add(st2[:, 32:40], st2[:, 32:40], LN_EPS
                                         ).then_inc(sDv, 1)
            cDv.add()

            @block.scalar
            def _(scalar):
                scalar.wait_ge(sDv, cDv.v)
                scalar.activation(st2[:, 40:48], st2[:, 32:40], AF.Ln)
                scalar.activation(st2[:, 40:48], st2[:, 40:48], AF.Exp,
                                  scale=-0.5).then_inc(sSc, 1)
            cSc.add()
            j_inv = cSc.v

            for tt in range(NTT):
                bo = (tt % 2) * D
                src = outpre[:, tt * D:(tt + 1) * D]

                @block.vector
                def _(vector, tt=tt, bo=bo, src=src):
                    vector.wait_ge(sSc, j_inv)
                    if tt >= 2:
                        vector.wait_ge((sJl0, sJl1)[tt % 2], cJl[tt % 2].v)
                    vector.tensor_scalar(ln2b[:, bo:bo + D], src,
                                         st2[:, 24 + tt:25 + tt],
                                         st2[:, 40 + tt:41 + tt],
                                         OP.subtract, OP.mult).then_inc(sDv, 1)
                cDv.add()

                @block.sync
                def _(sync, tt=tt, bo=bo):
                    sync.wait_ge(sDv, cDv.v)
                    sync.dma_start(out=out_t[tt * 128:(tt + 1) * 128, :],
                                   in_=ln2b[:, bo:bo + D]
                                   ).then_inc((sJl0, sJl1)[tt % 2], 16)
                cJl[tt % 2].add(16)

            @block.sync
            def _(sync):
                sync.wait_ge(sJl0, cJl[0].v)
                sync.wait_ge(sJl1, cJl[1].v)
                sync.wait_ge(sSt, cSt.v)

    return nc


def _stage_inputs(inputs):
    f32 = np.float32
    x = np.ascontiguousarray(np.asarray(inputs["L0_attn_x"], f32).reshape(8192, D))
    W_in = np.ascontiguousarray(np.asarray(inputs["L0_attn_W_in"], f32))
    b_in = np.asarray(inputs["L0_attn_b_in"], f32)
    W_gx = np.ascontiguousarray(np.asarray(inputs["L0_attn_W_gx"], f32)
                                .reshape(GT, 2, 128, 128))
    W_ga = np.ascontiguousarray(np.asarray(inputs["L0_attn_W_ga"], f32)
                                .reshape(GT, 2, 128, 128))
    b_gx = np.asarray(inputs["L0_attn_b_gx"], f32)
    b_ga = np.asarray(inputs["L0_attn_b_ga"], f32)
    a_real = np.asarray(inputs["L0_attn_a_real"], f32)
    a_imag = np.asarray(inputs["L0_attn_a_imag"], f32)
    W_out = np.ascontiguousarray(np.asarray(inputs["L0_attn_W_out"], f32))
    b_out = np.asarray(inputs["L0_attn_b_out"], f32)
    wgu = np.asarray(inputs["L0_ffn_w_gate_up"], f32)
    wpj = np.asarray(inputs["L0_ffn_w_proj"], f32)

    common = dict(
        w_in=W_in,
        b_in_rs=np.ascontiguousarray(b_in.reshape(DT, 128).T),
        w_gx=W_gx, w_ga=W_ga,
        b_gx_rs=np.ascontiguousarray(b_gx.T),
        b_ga_rs=np.ascontiguousarray(b_ga.T),
        ar_rs=np.ascontiguousarray(a_real.reshape(GT, 128).T),
        ai_rs=np.ascontiguousarray(a_imag.reshape(GT, 128).T),
        w_out=W_out,
        b_out_bc=np.ascontiguousarray(np.broadcast_to(b_out, (128, D))),
        ident=np.eye(128, dtype=ml_dtypes.bfloat16),
    )

    in_maps = []
    for c in range(NCORES):
        gu_rows = []
        for j in range(8):
            i = 8 * j + c
            gu_rows.append(wgu[128 * i:128 * (i + 1)])
            gu_rows.append(wgu[H + 128 * i:H + 128 * (i + 1)])
        wgu_shard = np.ascontiguousarray(np.stack(gu_rows))
        pj_blocks = []
        for m in range(8):
            k = 8 * m + c
            pj_blocks.append(wpj[:, 128 * k:128 * (k + 1)].reshape(16, 128, 128))
        wpj_shard = np.ascontiguousarray(np.stack(pj_blocks))
        m = dict(common)
        m.update(
            x=np.ascontiguousarray(x[TC * c:TC * (c + 1)]),
            wgu_sh=wgu_shard, wpj_sh=wpj_shard,
            mask_lt=np.ascontiguousarray(np.broadcast_to(
                (np.arange(NCORES) < c).astype(f32), (128, NCORES))),
        )
        in_maps.append(m)
    return in_maps


def kernel(**inputs):
    in_maps = _stage_inputs(inputs)
    nc = build_nc(debug=_DEBUG)
    res = run_bass_kernel_spmd(nc, in_maps, core_ids=list(range(NCORES)))
    out = np.concatenate([np.asarray(r["out"]) for r in res.results], axis=0)
    return out.reshape(1, 8192, D)


# version 16
# speedup vs baseline: 1.0519x; 1.0027x over previous
import contextlib
import numpy as np
import ml_dtypes

import concourse.bass as bass
import concourse.mybir as mybir
from concourse.bass_utils import run_bass_kernel_spmd

F32 = mybir.dt.float32
BF16 = mybir.dt.bfloat16
AF = mybir.ActivationFunctionType
OP = mybir.AluOpType
AX = mybir.AxisListType.X

NCORES = 8
TC = 1024
NTT = 8
D = 2048
DT = 16
GT = 8
H = 8192
NPAIR = 64
NPK = 64
W = 64
CH = 128
NCH = 8
NW = 16

MAGIC = 12582912.0
TWO_PI = 6.283185307179586
INV_2PI = 0.15915494309189535
HALF_PI = 1.5707963267948966
LNH = -0.34657359027997264
TINY = 1e-35
LN_EPS = 1e-5
RMS_EPS = 1.1920929e-07

_DEBUG = False
BARRIER_AG = False
RACE_DETECT = False
RAW_GROUPS = {"lrli", "tts1", "rr", "w1w2", "big", "tts2", "ext"}
A_LEVEL = 3


class _SerEngine:

    _PASS = {"wait_ge", "drain", "sem_inc", "nop", "engine_nop", "wait_op"}
    _HAZARD = {"tensor_reduce", "reduce_sum", "reduce_max", "reciprocal"}

    def __init__(self, eng, always_drain):
        self._e = eng
        self._always = always_drain

    def __getattr__(self, n):
        f = getattr(self._e, n)
        if n in self._PASS or not callable(f):
            return f
        def wrap(*a, **k):
            r = f(*a, **k)
            if RACE_DETECT or self._always:
                self._e.drain()
            return r
        return wrap


class Ctr:
    def __init__(self):
        self.v = 0

    def add(self, n=1):
        self.v += n
        return self.v


def build_nc(debug=False, stop=None, skip_prepass=False, skip_ag=False):
    def _R(eng, grp="x"):
        return eng._e if (grp in RAW_GROUPS and not RACE_DETECT) else eng

    nc = bass.Bass("TRN2", target_bir_lowering=False, debug=False,
                   detect_race_conditions=RACE_DETECT)

    x_in = nc.dram_tensor("x", [TC, D], F32, kind="ExternalInput").ap()
    w_in = nc.dram_tensor("w_in", [DT, 128, DT, 128], F32,
                          kind="ExternalInput").ap()
    b_in_rs = nc.dram_tensor("b_in_rs", [128, DT], F32, kind="ExternalInput").ap()
    w_gx = nc.dram_tensor("w_gx", [GT, 2, 128, 128], F32, kind="ExternalInput").ap()
    w_ga = nc.dram_tensor("w_ga", [GT, 2, 128, 128], F32, kind="ExternalInput").ap()
    b_gx_rs = nc.dram_tensor("b_gx_rs", [128, GT], F32, kind="ExternalInput").ap()
    b_ga_rs = nc.dram_tensor("b_ga_rs", [128, GT], F32, kind="ExternalInput").ap()
    ar_rs = nc.dram_tensor("ar_rs", [128, GT], F32, kind="ExternalInput").ap()
    ai_rs = nc.dram_tensor("ai_rs", [128, GT], F32, kind="ExternalInput").ap()
    w_out = nc.dram_tensor("w_out", [D, D], F32, kind="ExternalInput").ap()
    b_out_bc = nc.dram_tensor("b_out_bc", [128, D], F32, kind="ExternalInput").ap()
    wgu_sh = nc.dram_tensor("wgu_sh", [16, 128, D], F32, kind="ExternalInput").ap()
    wpj_sh = nc.dram_tensor("wpj_sh", [8, 16, 128, 128], F32,
                            kind="ExternalInput").ap()
    mask_lt = nc.dram_tensor("mask_lt", [128, NCORES], F32,
                             kind="ExternalInput").ap()
    ident_in = nc.dram_tensor("ident", [128, 128], BF16, kind="ExternalInput").ap()

    out_t = nc.dram_tensor("out", [TC, D], F32, kind="ExternalOutput").ap()

    dbg = {}
    if debug:
        for name, shape, dt_ in [
            ("dbg_xh", [128, DT * TC], F32),
            ("dbg_ga", [128, GT * TC], F32),
            ("dbg_gx", [128, GT * TC], F32),
            ("dbg_h", [128, DT * TC], BF16),
            ("dbg_ln1", [NTT * 128, D], F32),
            ("dbg_inv", [128, 16], F32),
            ("dbg_wq", [24 * 128, D], BF16),
            ("dbg_hin", [128, 256], F32),
            ("dbg_scr", [NCH * 4 * 128, 1024], F32),
            ("dbg_summ", [128, 512], F32),
        ]:
            dbg[name] = nc.dram_tensor(name, shape, dt_, kind="ExternalOutput").ap()

    wshard = nc.dram_tensor("wshard", [24 * 128, D], BF16).ap()
    wgath1 = nc.dram_tensor("wgath1", [NCORES * 16 * 128, D], BF16).ap()
    wgath2 = nc.dram_tensor("wgath2", [NCORES * 8 * 128, D], BF16).ap()
    scanscr = nc.dram_tensor("scanscr", [NCH * 4 * 128, 1024], F32).ap()
    cc_in = nc.dram_tensor("cc_in", [128, 32], F32).ap()
    cc_out = nc.dram_tensor("cc_out", [NCORES * 128, 32], F32).ap()
    ln1scr = nc.dram_tensor("ln1scr", [NTT * 128, D], F32).ap()
    hhscr = nc.dram_tensor("hhscr", [NPK * 128, TC], BF16).ap()
    outscr = nc.dram_tensor("outscr", [NTT * 128, D], F32).ap()

    def wg_tile(owner, pos):
        if pos < 16:
            base = (owner * 16 + pos) * 128
            return wgath1[base:base + 128, :]
        base = (owner * 8 + pos - 16) * 128
        return wgath2[base:base + 128, :]

    with contextlib.ExitStack() as ctx:
        E = ctx.enter_context
        sLd = E(nc.semaphore("sLd"))
        sSt = E(nc.semaphore("sSt"))
        sGp = E(nc.semaphore("sGp"))
        sCc = E(nc.semaphore("sCc"))
        sPe = E(nc.semaphore("sPe"))
        sDv = E(nc.semaphore("sDv"))
        sSc = E(nc.semaphore("sSc"))
        cLd, cSt, cGp, cCc, cPe, cDv, cSc = (Ctr() for _ in range(7))

        _block = E(nc.Block())

        class _BlockShim:
            def __init__(self, b):
                self._b = b

            def vector(self, f):
                return self._b.vector(lambda eng: f(_SerEngine(eng, True)))

            def scalar(self, f):
                return self._b.scalar(lambda eng: f(_SerEngine(eng, False)))

            def __getattr__(self, n):
                return getattr(self._b, n)

        block = _BlockShim(_block)

        sCst = E(nc.semaphore("sCst"))
        sWqL = E(nc.semaphore("sWqL"))
        sWsT = E(nc.semaphore("sWsT"))
        sWsP = E(nc.semaphore("sWsP"))
        sXl0 = E(nc.semaphore("sXl0"))
        sXl1 = E(nc.semaphore("sXl1"))
        sWn0 = E(nc.semaphore("sWn0"))
        sWn1 = E(nc.semaphore("sWn1"))
        sGts = E(nc.semaphore("sGts"))
        sScr = E(nc.semaphore("sScr"))
        sScr1 = E(nc.semaphore("sScr1"))
        sCo0 = E(nc.semaphore("sCo0"))
        sCo1 = E(nc.semaphore("sCo1"))
        sXr0 = E(nc.semaphore("sXr0"))
        sXr1 = E(nc.semaphore("sXr1"))
        sBo = E(nc.semaphore("sBo"))
        sWo = E(nc.semaphore("sWo"))
        sL1s = E(nc.semaphore("sL1s"))
        sLbb = E(nc.semaphore("sLbb"))
        sGu0 = E(nc.semaphore("sGu0"))
        sGu1 = E(nc.semaphore("sGu1"))
        sHhS = E(nc.semaphore("sHhS"))
        sPj0 = E(nc.semaphore("sPj0"))
        sPj1 = E(nc.semaphore("sPj1"))
        sPj2 = E(nc.semaphore("sPj2"))
        sPj3 = E(nc.semaphore("sPj3"))
        sLnr = E(nc.semaphore("sLnr"))
        sOsc = E(nc.semaphore("sOsc"))
        sOsc1 = E(nc.semaphore("sOsc1"))
        sJl0 = E(nc.semaphore("sJl0"))
        sJl1 = E(nc.semaphore("sJl1"))
        sCgI = E(nc.semaphore("sCgI"))
        (cCst, cWqL, cWsT, cWsP, cGts, cScr, cScr1, cWo, cL1s, cLbb, cHhS,
         cLnr, cOsc, cOsc1, cCgI) = (Ctr() for _ in range(15))
        cXl = [Ctr(), Ctr()]
        cWn = [Ctr(), Ctr()]
        cCo = [Ctr(), Ctr()]
        cXr = [Ctr(), Ctr()]
        cGu = [Ctr(), Ctr()]
        cPj = [Ctr(), Ctr(), Ctr(), Ctr()]
        cJl = [Ctr(), Ctr()]

        ident = E(nc.sbuf_tensor("identB", [128, 128], BF16))
        consts = E(nc.sbuf_tensor("consts", [128, 64], F32))
        onesb = E(nc.sbuf_tensor("onesb", [128, 1], BF16))
        ones_row = E(nc.sbuf_tensor("ones_row", [1, 128], BF16))
        nc.const_aps.register(F32, HALF_PI, consts[:, 56:57])
        nc.const_aps.register(F32, LNH, consts[:, 57:58])
        nc.const_aps.register(F32, 0.25, consts[:, 58:59])
        nc.const_aps.register(F32, MAGIC, consts[:, 59:60])
        nc.const_aps.register(F32, -MAGIC, consts[:, 60:61])
        ones64 = E(nc.sbuf_tensor("ones64", [128, W], F32))
        summ = E(nc.sbuf_tensor("summ", [128, 4 * 128], F32))
        hinb = E(nc.sbuf_tensor("hinb", [128, 256], F32))
        sgath = E(nc.sbuf_tensor("sgath", [128, NCORES * 32], F32))

        @block.sync
        def _(sync):
            sync.dma_start(out=ident[:, :], in_=ident_in).then_inc(sCst, 16)
            sync.dma_start(out=consts[:, 0:16], in_=b_in_rs).then_inc(sCst, 16)
            sync.dma_start(out=consts[:, 16:24], in_=b_gx_rs).then_inc(sCst, 16)
            sync.dma_start(out=consts[:, 24:32], in_=b_ga_rs).then_inc(sCst, 16)
            sync.dma_start(out=consts[:, 32:40], in_=ar_rs).then_inc(sCst, 16)
            sync.dma_start(out=consts[:, 40:48], in_=ai_rs).then_inc(sCst, 16)
            sync.dma_start(out=consts[:, 48:56], in_=mask_lt).then_inc(sCst, 16)
        consts_ld = cCst.add(16 * 7)

        @block.scalar
        def _(scalar):
            scalar.wait_ge(sCst, consts_ld)
            scalar.activation(consts[:, 32:40], consts[:, 32:40], AF.Exp)
            scalar.activation(consts[:, 32:40], consts[:, 32:40], AF.Ln,
                              bias=1.0).then_inc(sSc, 1)
        cSc.add()

        @block.vector
        def _(vector):
            vector.wait_ge(sSc, cSc.v)
            vector.tensor_scalar_mul(consts[:, 32:40], consts[:, 32:40], -8.0)
            vector.memset(consts[:, 56:57], HALF_PI)
            vector.memset(consts[:, 57:58], LNH)
            vector.memset(consts[:, 58:59], 0.25)
            vector.memset(consts[:, 59:60], MAGIC)
            vector.memset(consts[:, 60:61], -MAGIC)
            vector.memset(onesb[:, :], 1.0)
            vector.memset(ones_row[:, :], 1.0)
            vector.memset(ones64[:, :], 1.0).then_inc(sDv, 1)
        consts_rdy = cDv.add()

        g_xh = nc.sbuf_tensor("xh", [128, DT * TC], F32)
        xh = g_xh.__enter__()
        g_fb = nc.sbuf_tensor("fbuf", [128, 1112], F32)
        fbuf = g_fb.__enter__()
        g_ga = nc.sbuf_tensor("ga", [128, GT * TC], F32)
        ga = g_ga.__enter__()
        g_gx = nc.sbuf_tensor("gx", [128, GT * TC], F32)
        gx = g_gx.__enter__()

        pctx = contextlib.ExitStack()
        p1_qrdy = {}
        if not skip_prepass:
            Ep = pctx.enter_context
            wq_b = Ep(nc.sbuf_tensor("wq_b", [128, D], BF16, side="right"))
            wq_s = Ep(nc.sbuf_tensor("wq_s", [128, 32], F32, side="right"))
            wq_sb = Ep(nc.sbuf_tensor("wq_sb", [128, 32], BF16, side="right"))
            wq_q = Ep(nc.sbuf_tensor("wq_q", [128, D], F32, side="right"))
            wq_hat = Ep(nc.sbuf_tensor("wq_hat", [128, D], BF16, side="right"))
            wq_T = Ep(nc.sbuf_tensor("wq_T", [128, D], BF16, side="right"))
            pq = Ep(nc.psum_tensor("pq", [128, 512], BF16, side="right"))

        def emit_p1_a(pos):
            if skip_prepass:
                return

            @block.gpsimd
            def _(gpsimd, pos=pos):
                if pos > 0:
                    gpsimd.wait_ge(sDv, cDv.v)
                if pos < 16:
                    gpsimd.dma_start(out=wq_b[:, :], in_=wgu_sh[pos]
                                     ).then_inc(sWqL, 16)
                else:
                    src = wpj_sh[pos - 16].rearrange("a p f -> p a f")
                    dst = wq_b.ap().rearrange("p (a f) -> p a f", f=128)
                    gpsimd.dma_start(out=dst, in_=src).then_inc(sWqL, 16)
            cWqL.add(16)

            @block.vector
            def _(vector, pos=pos):
                vector.wait_ge(sWqL, cWqL.v)
                if pos > 0:
                    vector.wait_ge(sPe, cPe.v)
                    if pos <= 16:
                        vector.wait_ge(sWsT, cWsT.v)
                    else:
                        vector.wait_ge(sWsP, cWsP.v)
                b3 = wq_b.ap().rearrange("p (g j) -> p g j", j=W)
                vector.tensor_reduce(wq_s[:, :], b3, AX, OP.add,
                                     apply_absolute_value=True)
                vector.tensor_scalar_mul(wq_sb[:, :], wq_s[:, :], 1.0 / W)
                vector.tensor_scalar_max(wq_q[:, 0:32], wq_sb[:, :], 1e-8)
                vector.reciprocal(wq_s[:, :], wq_q[:, 0:32])
                rb = wq_s.ap()[:, :, None].broadcast_to([128, 32, W])
                h3 = wq_hat.ap().rearrange("p (g j) -> p g j", j=W)
                vector.tensor_tensor(h3, b3, rb, OP.mult)
                vector.tensor_scalar(wq_q[:, :], wq_hat[:, :], MAGIC, MAGIC,
                                     OP.add, OP.subtract)
                vector.tensor_scalar(wq_q[:, :], wq_q[:, :], 1.0, -1.0,
                                     OP.min, OP.max)
                sb3 = wq_sb.ap()[:, :, None].broadcast_to([128, 32, W])
                q3 = wq_q.ap().rearrange("p (g j) -> p g j", j=W)
                vector.tensor_tensor(h3, q3, sb3, OP.mult).then_inc(sDv, 1)
            cDv.add()
            p1_qrdy[pos] = cDv.v

        def emit_p1_b(pos):
            if skip_prepass:
                return
            q_rdy = p1_qrdy[pos]
            for grp in range(4):
                @block.tensor
                def _(tensor, pos=pos, grp=grp, q_rdy=q_rdy):
                    tensor.wait_ge(sDv, q_rdy)
                    tensor.wait_ge(sCst, consts_ld)
                    if not (pos == 0 and grp == 0):
                        tensor.wait_ge(sDv, cDv.v)
                    last = None
                    for j in range(4):
                        k = grp * 4 + j
                        last = tensor.transpose(
                            pq[:, j * 128:(j + 1) * 128],
                            wq_hat[:, k * 128:(k + 1) * 128], ident[:, :])
                    last.then_inc(sPe, 1)
                cPe.add()

                @block.vector
                def _(vector, grp=grp):
                    vector.wait_ge(sPe, cPe.v)
                    vector.tensor_copy(wq_T[:, grp * 512:(grp + 1) * 512],
                                       pq[:, :]).then_inc(sDv, 1)
                cDv.add()

            @block.gpsimd
            def _(gpsimd, pos=pos):
                gpsimd.wait_ge(sDv, cDv.v)
                sem = sWsT if pos < 16 else sWsP
                gpsimd.dma_start(out=wshard[pos * 128:(pos + 1) * 128, :],
                                 in_=wq_T[:, :]).then_inc(sem, 16)
            (cWsT if pos < 16 else cWsP).add(16)

        with contextlib.ExitStack() as actx:
            Ea = actx.enter_context
            xcm = Ea(nc.sbuf_tensor("xcm", [128, DT * TC], BF16))
            xtmb = Ea(nc.sbuf_tensor("xtmb", [128, D], BF16))
            pA = Ea(nc.psum_tensor("pA", [128, 2048], BF16))

            for tt in range(NTT):
                bo = 0
                sXl = (sXl0, sXl1)[tt % 2]
                cXl_ = cXl[tt % 2]

                @block.gpsimd
                def _(gpsimd, tt=tt, bo=bo, sXl=sXl):
                    if tt >= 1:
                        gpsimd.wait_ge(sPe, cPe.v)
                    gpsimd.dma_start(out=xtmb[:, bo:bo + D],
                                     in_=x_in[tt * 128:(tt + 1) * 128, :]
                                     ).then_inc(sXl, 16)
                x_ld = cXl_.add(16)

                for grp in range(4 if A_LEVEL >= 2 else 0):
                    po = (grp % 2) * 1024

                    @block.tensor
                    def _(tensor, tt=tt, grp=grp, bo=bo, po=po, x_ld=x_ld,
                          sXl=sXl):
                        tensor.wait_ge(sXl, x_ld)
                        tensor.wait_ge(sCst, consts_ld)
                        if tt * 4 + grp >= 2:
                            tensor.wait_ge(sDv, cDv.v)
                        last = None
                        for j in range(4):
                            k = grp * 4 + j
                            last = tensor.transpose(
                                pA[:, po + j * 128: po + (j + 1) * 128],
                                xtmb[:, bo + k * 128: bo + (k + 1) * 128],
                                ident[:, :])
                        last.then_inc(sPe, 1)
                    cPe.add()

                    if A_LEVEL >= 3:
                        @block.vector
                        def _(vector, tt=tt, grp=grp, po=po):
                            vector.wait_ge(sPe, cPe.v)
                            dst = xcm.ap().rearrange("p (j t) -> p j t", t=TC)[
                                :, grp * 4:(grp + 1) * 4, tt * 128:(tt + 1) * 128]
                            src = pA.ap()[:, po:po + 512].rearrange(
                                "p (j t) -> p j t", t=128)
                            vector.tensor_copy(dst, src).then_inc(sDv, 1)
                        cDv.add()
                    else:
                        @block.vector
                        def _(vector):
                            vector.wait_ge(sPe, cPe.v)
                            nc.vector.engine_nop().then_inc(sDv, 1)
                        cDv.add()

            if stop == 2:
                @block.sync
                def _(sync):
                    sync.wait_ge(sDv, cDv.v)
                actx.close()
                g_gx.__exit__(None, None, None)
                g_ga.__exit__(None, None, None)
                g_xh.__exit__(None, None, None)
                return nc

            with contextlib.ExitStack() as bctx:
                Eb = bctx.enter_context
                winj = Eb(nc.sbuf_tensor("winj", [128, 2 * D], BF16))
                pB = Eb(nc.psum_tensor("pB", [128, 1024], F32))

                x_cm_rdy = cDv.v
                n = 0
                for j in range(DT):
                    wo = (j % 2) * D
                    sWn = (sWn0, sWn1)[j % 2]

                    @block.gpsimd
                    def _(gpsimd, j=j, wo=wo, sWn=sWn):
                        if j >= 2:
                            gpsimd.wait_ge(sPe, cPe.v)
                        dst = winj.ap()[:, wo:wo + D].rearrange(
                            "p (k f) -> p k f", f=128)
                        gpsimd.dma_start(out=dst, in_=w_in[j]
                                         ).then_inc(sWn, 16)
                    wj_ld = cWn[j % 2].add(16)

                    for ts_ in range(2):
                        po = (n % 2) * 512

                        @block.tensor
                        def _(tensor, j=j, ts_=ts_, po=po, n=n, wo=wo,
                              wj_ld=wj_ld, sWn=sWn):
                            tensor.wait_ge(sWn, wj_ld)
                            tensor.wait_ge(sDv, x_cm_rdy)
                            if n >= 2:
                                tensor.wait_ge(sSc, cSc.v - 1)
                            last = None
                            for k in range(DT):
                                last = tensor.matmul(
                                    pB[:, po:po + 512],
                                    winj[:, wo + k * 128: wo + (k + 1) * 128],
                                    xcm[:, k * TC + ts_ * 512:
                                        k * TC + ts_ * 512 + 512],
                                    start=(k == 0), stop=(k == DT - 1))
                            last.then_inc(sPe, 1)
                        cPe.add()

                        @block.scalar
                        def _(scalar, j=j, ts_=ts_, po=po):
                            scalar.wait_ge(sPe, cPe.v)
                            scalar.activation(
                                xh[:, j * TC + ts_ * 512: j * TC + ts_ * 512 + 512],
                                pB[:, po:po + 512], AF.Identity,
                                bias=consts[:, j:j + 1]).then_inc(sSc, 1)
                        cSc.add()
                        n += 1
                    if j > 0:
                        emit_p1_b(j - 1)
                    emit_p1_a(j)
                xh_rdy_sc = cSc.v

        if stop == 3:
            @block.sync
            def _(sync):
                sync.wait_ge(sSc, xh_rdy_sc)
            g_gx.__exit__(None, None, None)
            g_ga.__exit__(None, None, None)
            g_xh.__exit__(None, None, None)
            return nc

        if debug:
            @block.sync
            def _(sync):
                sync.wait_ge(sSc, xh_rdy_sc)
                sync.dma_start(out=dbg["dbg_xh"], in_=xh[:, :]).then_inc(sSt, 16)
            cSt.add(16)

        with contextlib.ExitStack() as cctx:
            Ec = cctx.enter_context
            xhb = Ec(nc.sbuf_tensor("xhb", [128, DT * TC], BF16))
            wgb = Ec(nc.sbuf_tensor("wgb", [128, GT * 4 * 128], BF16))
            pC = Ec(nc.psum_tensor("pC", [128, 1024], F32))

            @block.gpsimd
            def _(gpsimd):
                gpsimd.wait_ge(sSc, xh_rdy_sc)
                for r in range(DT):
                    gpsimd.dma_start(out=xhb[:, r * TC:(r + 1) * TC],
                                     in_=xh[:, r * TC:(r + 1) * TC]
                                     ).then_inc(sGts, 16)
                for g in range(GT):
                    for k in range(2):
                        gpsimd.dma_start(
                            out=wgb[:, (g * 4 + k) * 128:(g * 4 + k + 1) * 128],
                            in_=w_gx[g, k]).then_inc(sGts, 16)
                        gpsimd.dma_start(
                            out=wgb[:, (g * 4 + 2 + k) * 128:(g * 4 + 3 + k) * 128],
                            in_=w_ga[g, k]).then_inc(sGts, 16)
            gates_ld = cGts.add(16 * (DT + GT * 4))

            n = 0
            for g in range(GT):
                for half in range(2):
                    for ts_ in range(2):
                        po = (n % 2) * 512

                        @block.tensor
                        def _(tensor, g=g, half=half, ts_=ts_, po=po, n=n):
                            tensor.wait_ge(sGts, gates_ld)
                            if n >= 2:
                                tensor.wait_ge(sSc, cSc.v - 1)
                            last = None
                            for k in range(2):
                                lb = (g * 4 + half * 2 + k) * 128
                                rbase = (2 * g + k) * TC + ts_ * 512
                                last = tensor.matmul(
                                    pC[:, po:po + 512],
                                    wgb[:, lb:lb + 128],
                                    xhb[:, rbase:rbase + 512],
                                    start=(k == 0), stop=(k == 1))
                            last.then_inc(sPe, 1)
                        cPe.add()

                        @block.scalar
                        def _(scalar, g=g, half=half, ts_=ts_, po=po):
                            scalar.wait_ge(sPe, cPe.v)
                            dstbuf = gx if half == 0 else ga
                            bcol = (16 if half == 0 else 24) + g
                            scalar.activation(
                                dstbuf[:, g * TC + ts_ * 512:
                                       g * TC + ts_ * 512 + 512],
                                pC[:, po:po + 512], AF.Sigmoid,
                                bias=consts[:, bcol:bcol + 1]).then_inc(sSc, 1)
                        cSc.add()
                        n += 1
                        if n % 4 == 0:
                            m = n // 4 - 1
                            emit_p1_b(15 + m)
                            emit_p1_a(16 + m)
            gates_rdy = cSc.v

        if stop == 4:
            @block.sync
            def _(sync):
                sync.wait_ge(sSc, gates_rdy)
            g_gx.__exit__(None, None, None)
            g_ga.__exit__(None, None, None)
            g_xh.__exit__(None, None, None)
            return nc

        if debug:
            @block.sync
            def _(sync):
                sync.wait_ge(sSc, gates_rdy)
                sync.dma_start(out=dbg["dbg_ga"], in_=ga[:, :]).then_inc(sSt, 16)
                sync.dma_start(out=dbg["dbg_gx"], in_=gx[:, :]).then_inc(sSt, 16)
            cSt.add(32)

        if not skip_prepass:
            emit_p1_b(23)
        pctx.close()

        def emit_wq_ag():
            if skip_prepass:
                return cCc.v, cCc.v
            if skip_ag:
                @block.gpsimd
                def _(gpsimd):
                    gpsimd.wait_ge(sWsT, cWsT.v)
                    gpsimd.wait_ge(sWsP, cWsP.v)
                    gpsimd.nop().then_inc(sCc, 2)
            else:
                @block.gpsimd
                def _(gpsimd):
                    gpsimd.wait_ge(sWsT, 16 * 16)
                    gpsimd.collective_compute(
                        "AllGather", OP.bypass,
                        replica_groups=[list(range(NCORES))],
                        ins=[wshard[0:16 * 128, :]], outs=[wgath1],
                    ).then_inc(sCc, 1)
                    gpsimd.wait_ge(sWsP, cWsP.v)
                    gpsimd.collective_compute(
                        "AllGather", OP.bypass,
                        replica_groups=[list(range(NCORES))],
                        ins=[wshard[16 * 128:, :]], outs=[wgath2],
                    ).then_inc(sCc, 1)
            return cCc.add(), cCc.add()

        wq_ag1, wq_ag = emit_wq_ag()

        if stop == 1:
            @block.sync
            def _(sync):
                sync.wait_ge(sCc, wq_ag)
            return nc

        @block.vector
        def _(vector):
            vector.wait_ge(sSc, gates_rdy)
            vector.wait_ge(sDv, consts_rdy)
            nc.vector.engine_nop().then_inc(sDv, 1)
        xc_rdy = cDv.add()

        NSLOT = 16
        g_pool = nc.sbuf_tensor("pool", [128, NSLOT * 1024], F32)
        poolt = g_pool.__enter__()

        def _mkP(par):
            def sl(s):
                if s in (16, 17):
                    return s - 16
                if par and s in (2, 3, 4, 5):
                    return s + 10
                return s
            Pf = lambda s: poolt[:, sl(s) * 1024:(sl(s) + 1) * 1024]
            P3f = lambda s, t=CH: poolt.ap()[:, sl(s) * 1024:
                                             (sl(s) + 1) * 1024].rearrange(
                "p (r t) -> p r t", t=t)
            return Pf, P3f

        scr_mark = []
        for c in range(NCH):
            t0 = c * CH
            P_, P3 = _mkP(c % 2)

            def gsl3(buf, off=0, t0=t0):
                return buf.ap().rearrange("p (r t) -> p r t", t=TC)[
                    :, off:off + GT, t0:t0 + CH]

            @block.scalar
            def _(scalar, c=c, t0=t0):
                scalar.wait_ge(sSc, gates_rdy)
                if c == 0:
                    scalar.wait_ge(sDv, xc_rdy)
                if c > 0:
                    scalar.wait_ge(sDv, cDv.v)
                rs = _R(scalar, "lrli")
                last = None
                for r in range(GT):
                    sl = ga[:, r * TC + t0: r * TC + t0 + CH]
                    rs.mul(P_(0)[:, r * CH:(r + 1) * CH], sl,
                           consts[:, 32 + r:33 + r])
                    last = rs.mul(P_(1)[:, r * CH:(r + 1) * CH], sl,
                                  consts[:, 40 + r:41 + r])
                last.then_inc(sSc, 1)
            cSc.add()

            @block.vector
            def _(vector, c=c, t0=t0,
                  sm=(scr_mark[c - 2] if c >= 2 else 0)):
                vector.wait_ge(sSc, cSc.v)
                if c == 0:
                    vector.wait_ge(sWsT, cWsT.v)
                    vector.wait_ge(sWsP, cWsP.v)
                if c >= 2:
                    vector.wait_ge((sScr, sScr1)[c % 2], sm)
                rv = _R(vector, "tts1")
                last = None
                for r in range(GT):
                    for w in range(2):
                        a = r * CH + w * W
                        rv.tensor_tensor_scan(
                            P_(2)[:, a:a + W], ones64[:, :], P_(0)[:, a:a + W],
                            0.0, OP.mult, OP.add)
                        last = rv.tensor_tensor_scan(
                            P_(3)[:, a:a + W], ones64[:, :], P_(1)[:, a:a + W],
                            0.0, OP.mult, OP.add)
                vector.drain()
                last.then_inc(sDv, 1)
            cDv.add()

            @block.scalar
            def _(scalar, c=c):
                scalar.wait_ge(sDv, cDv.v)
                rs = _R(scalar, "luts")
                rs.activation(P_(4), P_(0), AF.Exp, scale=2.0)
                rs.activation(P_(5), P_(0), AF.Exp, scale=4.0)
                rs.activation(P_(6), P_(2), AF.Exp)
                rs.activation(P_(7), P_(2), AF.Exp, scale=-1.0)
                last = rs.activation(P_(8), P_(1), AF.Sin, scale=2.0,
                                     bias=HALF_PI)
                last.then_inc(sSc, 1)
            cSc.add()

            @block.vector
            def _(vector, c=c):
                vector.wait_ge(sSc, cSc.v)
                rv = _R(vector, "rr")
                rv.tensor_scalar(P_(0), P_(3), INV_2PI, 0.25, OP.mult, OP.add)
                rv.tensor_scalar(P_(9), P_(3), INV_2PI, None, OP.mult)
                rv.tensor_scalar(P_(0), P_(0), MAGIC, MAGIC, OP.add, OP.subtract)
                rv.tensor_scalar(P_(9), P_(9), MAGIC, MAGIC, OP.add, OP.subtract)
                rv.scalar_tensor_tensor(P_(0), P_(0), -TWO_PI, P_(3),
                                        OP.mult, OP.add)
                rv.scalar_tensor_tensor(P_(2), P_(9), -TWO_PI, P_(3),
                                        OP.mult, OP.add)
                rv.tensor_scalar(P_(0), P_(0), HALF_PI - 1e-6,
                                 -3.0 * HALF_PI + 1e-6, OP.min, OP.max)
                rv.tensor_scalar(P_(2), P_(2), TWO_PI / 2 - 1e-6,
                                 -TWO_PI / 2 + 1e-6, OP.min, OP.max)
                vector.tensor_mul(P_(9), P_(4), P_(8))
                vector.scalar_tensor_tensor(P_(4), P_(9), -2.0, P_(5),
                                            OP.mult, OP.add)
                vector.tensor_scalar(P_(4), P_(4), 1.0, TINY, OP.add, OP.max
                                     ).then_inc(sDv, 1)
            cDv.add()

            @block.scalar
            def _(scalar, c=c):
                scalar.wait_ge(sDv, cDv.v)
                scalar.activation(P_(5), P_(4), AF.Ln)
                scalar.activation(P_(5), P_(5), AF.Exp, scale=0.5
                                  ).then_inc(sSc, 1)
            cSc.add()

            @block.vector
            def _(vector, c=c):
                vector.wait_ge(sSc, cSc.v)
                rv = _R(vector, "w1w2")
                rv.tensor_sub(P_(8), P_(5), P_(9))
                rv.tensor_add(P_(9), P_(5), P_(9))
                rv.tensor_scalar(P_(8), P_(8), 1.0, TINY, OP.add, OP.max)
                vector.tensor_scalar(P_(9), P_(9), -1.0, TINY, OP.add, OP.max
                                     ).then_inc(sDv, 1)
            cDv.add()

            @block.scalar
            def _(scalar, c=c):
                scalar.wait_ge(sDv, cDv.v)
                scalar.activation(P_(10), P_(8), AF.Ln)
                scalar.activation(P_(10), P_(10), AF.Exp, scale=0.5, bias=LNH)
                scalar.activation(P_(11), P_(9), AF.Ln)
                scalar.activation(P_(11), P_(11), AF.Exp, scale=0.5, bias=LNH)
                scalar.activation(P_(16), P_(0), AF.Sin, bias=HALF_PI)
                scalar.activation(P_(17), P_(2), AF.Sin).then_inc(sSc, 1)
            cSc.add()

            @block.vector
            def _(vector, c=c, t0=t0):
                vector.wait_ge(sSc, cSc.v)
                rv = _R(vector, "big")
                rv.tensor_mul(P_(2), P_(6), P_(16))
                rv.tensor_mul(P_(3), P_(6), P_(17))
                rv.tensor_mul(P_(16), P_(7), P_(16))
                rv.tensor_mul(P_(17), P_(7), P_(17))
                rv.tensor_mul(P_(6), P_(10), P_(16))
                rv.tensor_mul(P_(7), P_(11), P_(17))
                rv.tensor_mul(P_(9), P_(10), P_(17))
                rv.tensor_mul(P_(8), P_(11), P_(16))
                rv.tensor_sub(P_(6), P_(6), P_(7))
                rv.tensor_add(P_(7), P_(9), P_(8))
                rv.tensor_tensor(P3(6), gsl3(gx), P3(6), OP.mult)
                rv.tensor_tensor(P3(7), gsl3(gx), P3(7), OP.mult)
                rv.tensor_tensor(P3(10), gsl3(xh), P3(6), OP.mult)
                rv.tensor_tensor(P3(16), gsl3(xh, 8), P3(6), OP.mult)
                rv.tensor_tensor(P3(11), gsl3(xh, 8), P3(7), OP.mult)
                rv.tensor_tensor(P3(17), gsl3(xh), P3(7), OP.mult)
                rv.tensor_add(P_(10), P_(10), P_(11))
                rv.tensor_sub(P_(9), P_(16), P_(17))
                vector.drain()
                rv2 = _R(vector, "tts2")
                for r in range(GT):
                    for w in range(2):
                        a = r * CH + w * W
                        rv2.tensor_tensor_scan(
                            P_(4)[:, a:a + W], ones64[:, :], P_(10)[:, a:a + W],
                            0.0, OP.mult, OP.add)
                        rv2.tensor_tensor_scan(
                            P_(5)[:, a:a + W], ones64[:, :], P_(9)[:, a:a + W],
                            0.0, OP.mult, OP.add)
                vector.drain()
                rv3 = _R(vector, "ext")
                last = None
                for src, col in ((2, 0), (3, 1), (4, 2), (5, 3)):
                    sap = P3(src)[:, :, W - 1::W]
                    dap = summ.ap()[:, col * 128:(col + 1) * 128].rearrange(
                        "p (r w) -> p r w", w=16)[:, :, 2 * c:2 * c + 2]
                    last = rv3.tensor_copy(dap, sap)
                vector.drain()
                last.then_inc(sDv, 1)
            cDv.add()

            @block.gpsimd
            def _(gpsimd, c=c):
                gpsimd.wait_ge(sDv, cDv.v)
                for i, s in enumerate((4, 5, 2, 3)):
                    base = (c * 4 + i) * 128
                    gpsimd.dma_start(out=scanscr[base:base + 128, :],
                                     in_=P_(s)).then_inc((sScr, sScr1)[c % 2], 16)
            (cScr, cScr1)[c % 2].add(64)
            scr_mark.append((cScr, cScr1)[c % 2].v)
        scan_dv = cDv.v
        g_pool.__exit__(None, None, None)
        g_gx.__exit__(None, None, None)
        g_ga.__exit__(None, None, None)
        _FO = {0: (0, 512), 1: (512, 256), 2: (768, 40), 3: (808, 32),
               4: (840, 16), 5: (856, 256)}
        P_ = lambda s: fbuf[:, _FO[s][0]:_FO[s][0] + _FO[s][1]]

        if stop == 5:
            @block.sync
            def _(sync):
                sync.wait_ge(sDv, scan_dv)
                sync.wait_ge(sScr, cScr.v)
                sync.wait_ge(sScr1, cScr1.v)
            g_fb.__exit__(None, None, None)
            g_xh.__exit__(None, None, None)
            return nc

        g_ln1cm = nc.sbuf_tensor("ln1cm", [128, DT * TC], BF16, side="right")
        ln1cm = g_ln1cm.__enter__()
        g_wob = nc.sbuf_tensor("woutb", [128, DT * D], BF16, side="right")
        woutb = g_wob.__enter__()
        g_bob = nc.sbuf_tensor("b_out_r", [1, D], BF16, side="right")
        b_out_r = g_bob.__enter__()
        g_xres = nc.sbuf_tensor("xres", [128, 2 * D], F32, side="right")
        xres = g_xres.__enter__()

        @block.gpsimd
        def _(gpsimd):
            gpsimd.wait_ge(sDv, scan_dv)
            gpsimd.wait_ge(sScr, cScr.v)
            gpsimd.wait_ge(sScr1, cScr1.v)
            for k in range(DT):
                gpsimd.dma_start(out=woutb[:, k * D:(k + 1) * D],
                                 in_=w_out[k * 128:(k + 1) * 128, :]
                                 ).then_inc(sWo, 16)
            gpsimd.dma_start(out=b_out_r[:, :],
                             in_=b_out_bc[0:1, :]).then_inc(sBo, 16)
            gpsimd.dma_start(out=xres[:, 0:D],
                             in_=x_in[0:128, :]).then_inc(sOsc, 16)
            gpsimd.dma_start(out=xres[:, D:2 * D],
                             in_=x_in[128:256, :]).then_inc(sOsc1, 16)
        wout_ld = cWo.add(16 * DT)
        cOsc.add(16)
        cOsc1.add(16)

        def wsl(buf, base, w):
            return buf.ap()[:, base:base + 128].rearrange(
                "p (r w) -> p r w", w=16)[:, :, w:w + 1]

        @block.vector
        def _(vector):
            vector.wait_ge(sDv, scan_dv)
            vector.wait_ge(sScr, cScr.v)
            vector.wait_ge(sScr1, cScr1.v)
            vector.tensor_mul(P_(0)[:, 0:128], summ[:, 0:128], summ[:, 256:384])
            vector.tensor_mul(P_(0)[:, 128:256], summ[:, 128:256], summ[:, 384:512])
            vector.tensor_mul(P_(0)[:, 256:384], summ[:, 0:128], summ[:, 384:512])
            vector.tensor_mul(P_(0)[:, 384:512], summ[:, 128:256], summ[:, 256:384])
            vector.tensor_sub(summ[:, 256:384], P_(0)[:, 0:128], P_(0)[:, 128:256])
            vector.tensor_add(summ[:, 384:512], P_(0)[:, 256:384], P_(0)[:, 384:512])
            vector.memset(hinb[:, :], 0.0)
            vector.memset(P_(1)[:, 0:128], 1.0)
            vector.memset(P_(1)[:, 128:256], 0.0)
            t0_ = P_(2)[:, 0:8].rearrange("p (r w) -> p r w", w=1)
            t1_ = P_(2)[:, 8:16].rearrange("p (r w) -> p r w", w=1)
            for w in range(1, NW):
                ArP, AiP = wsl(summ, 0, w - 1), wsl(summ, 128, w - 1)
                BrP, BiP = wsl(summ, 256, w - 1), wsl(summ, 384, w - 1)
                HrP, HiP = wsl(hinb, 0, w - 1), wsl(hinb, 128, w - 1)
                Hr, Hi = wsl(hinb, 0, w), wsl(hinb, 128, w)
                cArP, cAiP = wsl(fbuf, 512, w - 1), wsl(fbuf, 640, w - 1)
                cAr, cAi = wsl(fbuf, 512, w), wsl(fbuf, 640, w)
                vector.tensor_tensor(t0_, ArP, HrP, OP.mult)
                vector.tensor_tensor(t1_, AiP, HiP, OP.mult)
                vector.tensor_tensor(t0_, t0_, t1_, OP.subtract)
                vector.tensor_tensor(Hr, t0_, BrP, OP.add)
                vector.tensor_tensor(t0_, ArP, HiP, OP.mult)
                vector.tensor_tensor(t1_, AiP, HrP, OP.mult)
                vector.tensor_tensor(t0_, t0_, t1_, OP.add)
                vector.tensor_tensor(Hi, t0_, BiP, OP.add)
                vector.tensor_tensor(t0_, cArP, ArP, OP.mult)
                vector.tensor_tensor(t1_, cAiP, AiP, OP.mult)
                vector.tensor_tensor(cAr, t0_, t1_, OP.subtract)
                vector.tensor_tensor(t0_, cArP, AiP, OP.mult)
                vector.tensor_tensor(t1_, cAiP, ArP, OP.mult)
                vector.tensor_tensor(cAi, t0_, t1_, OP.add)
            ArL, AiL = wsl(summ, 0, NW - 1), wsl(summ, 128, NW - 1)
            BrL, BiL = wsl(summ, 256, NW - 1), wsl(summ, 384, NW - 1)
            HrL, HiL = wsl(hinb, 0, NW - 1), wsl(hinb, 128, NW - 1)
            cArL, cAiL = wsl(fbuf, 512, NW - 1), wsl(fbuf, 640, NW - 1)
            pk = lambda a, b: P_(3)[:, a:b].rearrange("p (r w) -> p r w", w=1)
            vector.tensor_tensor(t0_, cArL, ArL, OP.mult)
            vector.tensor_tensor(t1_, cAiL, AiL, OP.mult)
            vector.tensor_tensor(pk(0, 8), t0_, t1_, OP.subtract)
            vector.tensor_tensor(t0_, cArL, AiL, OP.mult)
            vector.tensor_tensor(t1_, cAiL, ArL, OP.mult)
            vector.tensor_tensor(pk(8, 16), t0_, t1_, OP.add)
            vector.tensor_tensor(t0_, ArL, HrL, OP.mult)
            vector.tensor_tensor(t1_, AiL, HiL, OP.mult)
            vector.tensor_tensor(t0_, t0_, t1_, OP.subtract)
            vector.tensor_tensor(pk(16, 24), t0_, BrL, OP.add)
            vector.tensor_tensor(t0_, ArL, HiL, OP.mult)
            vector.tensor_tensor(t1_, AiL, HrL, OP.mult)
            vector.tensor_tensor(t0_, t0_, t1_, OP.add)
            vector.tensor_tensor(pk(24, 32), t0_, BiL, OP.add).then_inc(sDv, 1)
        summ_rdy = cDv.add()

        @block.gpsimd
        def _(gpsimd):
            gpsimd.wait_ge(sDv, summ_rdy)
            gpsimd.dma_start(out=cc_in, in_=P_(3)[:, 0:32]).then_inc(sCgI, 16)
            gpsimd.wait_ge(sCgI, cCgI.add(16))
            gpsimd.collective_compute(
                "AllGather", OP.bypass,
                replica_groups=[list(range(NCORES))],
                ins=[cc_in], outs=[cc_out],
            ).then_inc(sCc, 1)
            gpsimd.wait_ge(sCc, cCc.add())
            last = None
            for j in range(NCORES):
                last = gpsimd.dma_start(out=sgath[:, 32 * j:32 * (j + 1)],
                                        in_=cc_out[128 * j:128 * (j + 1), :])
                last.then_inc(sCgI, 16)
            cCgI.add(16 * NCORES)
        carry_ld = cCgI.v

        @block.vector
        def _(vector):
            vector.wait_ge(sCgI, carry_ld)
            vector.memset(P_(4)[:, 0:16], 0.0)
            hr = P_(4)[:, 0:8].rearrange("p (r w) -> p r w", w=1)
            hi = P_(4)[:, 8:16].rearrange("p (r w) -> p r w", w=1)
            t0_ = P_(2)[:, 0:8].rearrange("p (r w) -> p r w", w=1)
            t1_ = P_(2)[:, 8:16].rearrange("p (r w) -> p r w", w=1)
            t2_ = P_(2)[:, 16:24].rearrange("p (r w) -> p r w", w=1)
            t3_ = P_(2)[:, 24:32].rearrange("p (r w) -> p r w", w=1)
            t4_ = P_(2)[:, 32:40].rearrange("p (r w) -> p r w", w=1)
            for j in range(NCORES):
                g = lambda a, b: sgath.ap()[:, 32 * j + a:32 * j + b].rearrange(
                    "p (r w) -> p r w", w=1)
                m = consts[:, 48 + j:49 + j]
                vector.tensor_scalar(t2_, g(0, 8), -1.0, None, OP.add)
                vector.tensor_scalar(t2_, t2_, m, 1.0, OP.mult, OP.add)
                vector.tensor_scalar(t3_, g(8, 16), m, None, OP.mult)
                vector.tensor_tensor(t0_, t2_, hr, OP.mult)
                vector.tensor_tensor(t1_, t3_, hi, OP.mult)
                vector.tensor_tensor(t0_, t0_, t1_, OP.subtract)
                vector.tensor_tensor(t1_, t2_, hi, OP.mult)
                vector.tensor_tensor(t4_, t3_, hr, OP.mult)
                vector.tensor_tensor(hi, t1_, t4_, OP.add)
                vector.tensor_scalar(t1_, g(16, 24), m, None, OP.mult)
                vector.tensor_tensor(hr, t0_, t1_, OP.add)
                vector.tensor_scalar(t1_, g(24, 32), m, None, OP.mult)
                vector.tensor_tensor(hi, hi, t1_, OP.add)
            hcr = P_(4)[:, 0:8][:, :, None].broadcast_to([128, 8, 16])
            hci = P_(4)[:, 8:16][:, :, None].broadcast_to([128, 8, 16])
            cAr = P_(1)[:, 0:128].rearrange("p (r w) -> p r w", w=16)
            cAi = P_(1)[:, 128:256].rearrange("p (r w) -> p r w", w=16)
            hR = hinb.ap()[:, 0:128].rearrange("p (r w) -> p r w", w=16)
            hI = hinb.ap()[:, 128:256].rearrange("p (r w) -> p r w", w=16)
            p5a = P_(5)[:, 0:128].rearrange("p (r w) -> p r w", w=16)
            p5b = P_(5)[:, 128:256].rearrange("p (r w) -> p r w", w=16)
            vector.tensor_tensor(p5a, cAr, hcr, OP.mult)
            vector.tensor_tensor(p5b, cAi, hci, OP.mult)
            vector.tensor_tensor(p5a, p5a, p5b, OP.subtract)
            vector.tensor_tensor(hR, hR, p5a, OP.add)
            vector.tensor_tensor(p5a, cAr, hci, OP.mult)
            vector.tensor_tensor(p5b, cAi, hcr, OP.mult)
            vector.tensor_tensor(p5a, p5a, p5b, OP.add)
            vector.tensor_tensor(hI, hI, p5a, OP.add).then_inc(sDv, 1)
        hin_rdy = cDv.add()

        if stop == 6:
            @block.sync
            def _(sync):
                sync.wait_ge(sDv, hin_rdy)
            g_fb.__exit__(None, None, None)
            g_xh.__exit__(None, None, None)
            return nc

        if debug:
            @block.sync
            def _(sync):
                sync.wait_ge(sDv, hin_rdy)
                sync.dma_start(out=dbg["dbg_hin"], in_=hinb[:, :]).then_inc(sSt, 16)
                sync.dma_start(out=dbg["dbg_summ"], in_=summ[:, :]).then_inc(sSt, 16)
                sync.dma_start(out=dbg["dbg_scr"], in_=scanscr).then_inc(sSt, 16)
            cSt.add(48)

        g_fb.__exit__(None, None, None)
        g_xh.__exit__(None, None, None)

        g_hb = nc.sbuf_tensor("hb", [128, DT * TC], BF16, side="right")
        hb = g_hb.__enter__()
        g_corr = nc.sbuf_tensor("corr", [128, 10 * 1024], F32, side="right")
        corr = g_corr.__enter__()
        C_ = lambda s: corr[:, s * 1024:(s + 1) * 1024]
        C3 = lambda s, t=CH: corr.ap()[:, s * 1024:(s + 1) * 1024].rearrange(
            "p (r t) -> p r t", t=t)

        g_mark = []
        for c in range(NCH):
            b = 2 + (c % 2) * 4
            sCo = (sCo0, sCo1)[c % 2]

            @block.sync
            def _(sync, c=c, b=b, sCo=sCo):
                sync.wait_ge(sDv, hin_rdy)
                if c == 0:
                    sync.wait_ge(sWsT, cWsT.v)
                    sync.wait_ge(sWsP, cWsP.v)
                if c >= 2:
                    sync.wait_ge(sDv, cDv.v)
                for i in range(4):
                    base = (c * 4 + i) * 128
                    sync.dma_start(out=C_(b + i), in_=scanscr[base:base + 128, :]
                                   ).then_inc(sCo, 16)
            co_ld = cCo[c % 2].add(64)

            @block.vector
            def _(vector, c=c, b=b, sCo=sCo, co_ld=co_ld):
                vector.wait_ge(sCo, co_ld)
                Sr, Si, cP, sP = b, b + 1, b + 2, b + 3
                r4 = lambda s: C3(s).rearrange("p r (w t) -> p r w t", t=W)
                hR = hinb.ap()[:, 0:128].rearrange("p (r w) -> p r w", w=16)[
                    :, :, 2 * c:2 * c + 2][:, :, :, None].broadcast_to(
                    [128, 8, 2, W])
                hI = hinb.ap()[:, 128:256].rearrange("p (r w) -> p r w", w=16)[
                    :, :, 2 * c:2 * c + 2][:, :, :, None].broadcast_to(
                    [128, 8, 2, W])
                vector.tensor_tensor(r4(Sr), r4(Sr), hR, OP.add)
                vector.tensor_tensor(r4(Si), r4(Si), hI, OP.add)
                vector.tensor_mul(C_(0), C_(cP), C_(Sr))
                vector.tensor_mul(C_(1), C_(sP), C_(Si))
                hre = hb.ap().rearrange("p (r t) -> p r t", t=TC)[
                    :, 0:8, c * CH:(c + 1) * CH]
                vector.tensor_tensor(hre, C3(0), C3(1), OP.subtract)
                vector.tensor_mul(C_(0), C_(cP), C_(Si))
                vector.tensor_mul(C_(1), C_(sP), C_(Sr))
                him = hb.ap().rearrange("p (r t) -> p r t", t=TC)[
                    :, 8:16, c * CH:(c + 1) * CH]
                vector.tensor_tensor(him, C3(0), C3(1), OP.add).then_inc(sDv, 1)
            g_mark.append(cDv.add())
        h_rdy = cDv.v
        g_corr.__exit__(None, None, None)

        if stop == 7:
            @block.sync
            def _(sync):
                sync.wait_ge(sDv, h_rdy)
            return nc

        if debug:
            @block.sync
            def _(sync):
                sync.wait_ge(sDv, h_rdy)
                sync.dma_start(out=dbg["dbg_h"], in_=hb[:, :]).then_inc(sSt, 16)
            cSt.add(16)

        with contextlib.ExitStack() as hctx:
            Eh = hctx.enter_context
            r1b = Eh(nc.sbuf_tensor("r1b", [128, 2 * D], F32))
            ln1b = Eh(nc.sbuf_tensor("ln1b", [128, D], F32))
            ln1bb = Eh(nc.sbuf_tensor("ln1bb", [128, 2 * D], BF16))
            sq_s = Eh(nc.sbuf_tensor("sq_s", [128, D], BF16))
            stat = Eh(nc.sbuf_tensor("stat", [128, 8], F32))
            pH = Eh(nc.psum_tensor("pH", [128, 1024], F32))
            pT = Eh(nc.psum_tensor("pT", [128, 512], BF16))

            cast_mark = []
            tp_mark = {}
            tp_ev = []

            def emit_ln1_tp(tt):
                for grp in range(4):
                    @block.tensor
                    def _(tensor, tt=tt, grp=grp,
                          ev=(tp_ev[-1] if tp_ev else None)):
                        tensor.wait_ge(sLbb, cast_mark[tt])
                        if ev is not None:
                            tensor.wait_ge(sDv, ev)
                        last = None
                        for j in range(4):
                            k = grp * 4 + j
                            last = tensor.transpose(
                                pT[:, j * 128:(j + 1) * 128],
                                ln1bb[:, (tt % 2) * D + k * 128:
                                      (tt % 2) * D + (k + 1) * 128],
                                ident[:, :])
                        last.then_inc(sPe, 1)
                    cPe.add()

                    @block.vector
                    def _(vector, tt=tt, grp=grp):
                        vector.wait_ge(sPe, cPe.v)
                        dst = ln1cm.ap().rearrange("p (j t) -> p j t", t=TC)[
                            :, grp * 4:(grp + 1) * 4, tt * 128:(tt + 1) * 128]
                        src = pT.ap().rearrange("p (j t) -> p j t", t=128)
                        vector.tensor_copy(dst, src).then_inc(sDv, 1)
                    tp_ev.append(cDv.add())
                tp_mark[tt] = cPe.v

            nH = 0
            for tt in range(NTT):
                bo = (tt % 2) * D

                sXr = (sXr0, sXr1)[tt % 2]

                if tt >= 2:
                    @block.sync
                    def _(sync, tt=tt, bo=bo, sXr=sXr):
                        sync.wait_ge(sDv, cDv.v)
                        sync.dma_start(out=xres[:, bo:bo + D],
                                       in_=x_in[tt * 128:(tt + 1) * 128, :]
                                       ).then_inc(sXr, 16)
                    cXr[tt % 2].add(16)
                xr_ld = cXr[tt % 2].v

                for cs in range(4):
                    po = (nH % 2) * 512

                    @block.tensor
                    def _(tensor, tt=tt, cs=cs, po=po, nH=nH):
                        tensor.wait_ge(sDv, g_mark[tt])
                        tensor.wait_ge(sWo, wout_ld)
                        if tt == 0 and cs == 0:
                            tensor.wait_ge(sBo, 16)
                        if nH >= 2:
                            tensor.wait_ge(sDv, cDv.v - 1)
                        tensor.matmul(pH[:, po:po + 512], ones_row[0:1, :],
                                      b_out_r[0:1, cs * 512:(cs + 1) * 512],
                                      start=True, stop=False)
                        last = None
                        for k in range(DT):
                            last = tensor.matmul(
                                pH[:, po:po + 512],
                                hb[:, k * TC + tt * 128: k * TC + (tt + 1) * 128],
                                woutb[:, k * D + cs * 512: k * D + (cs + 1) * 512],
                                start=False, stop=(k == DT - 1))
                        last.then_inc(sPe, 1)
                    cPe.add()

                    @block.vector
                    def _(vector, cs=cs, po=po, bo=bo, tt=tt, xr_ld=xr_ld,
                          sXr=sXr):
                        vector.wait_ge(sPe, cPe.v)
                        if cs == 0:
                            if tt < 2:
                                vector.wait_ge((sOsc, sOsc1)[tt], 16)
                            else:
                                vector.wait_ge(sXr, xr_ld)
                        vector.scalar_tensor_tensor(
                            r1b[:, bo + cs * 512: bo + (cs + 1) * 512],
                            pH[:, po:po + 512], 1.0,
                            xres[:, bo + cs * 512: bo + (cs + 1) * 512],
                            OP.mult, OP.add).then_inc(sDv, 1)
                    cDv.add()
                    nH += 1

                @block.vector
                def _(vector, bo=bo):
                    vector.wait_ge(sDv, cDv.v)
                    vector.tensor_reduce(stat[:, 0:1], r1b[:, bo:bo + D], AX,
                                         OP.add).then_inc(sDv, 1)
                cDv.add()

                @block.scalar
                def _(scalar, bo=bo):
                    scalar.wait_ge(sDv, cDv.v)
                    scalar.activation(sq_s[:, :], r1b[:, bo:bo + D], AF.Square,
                                      accum_out=stat[:, 1:2]).then_inc(sSc, 1)
                cSc.add()

                @block.vector
                def _(vector):
                    vector.wait_ge(sSc, cSc.v)
                    vector.tensor_scalar_mul(stat[:, 2:3], stat[:, 0:1], 1.0 / D)
                    vector.tensor_scalar_mul(stat[:, 3:4], stat[:, 1:2], 1.0 / D)
                    vector.tensor_mul(stat[:, 4:5], stat[:, 2:3], stat[:, 2:3])
                    vector.tensor_sub(stat[:, 3:4], stat[:, 3:4], stat[:, 4:5])
                    vector.tensor_scalar_add(stat[:, 3:4], stat[:, 3:4], LN_EPS
                                             ).then_inc(sDv, 1)
                cDv.add()

                @block.scalar
                def _(scalar):
                    scalar.wait_ge(sDv, cDv.v)
                    scalar.activation(stat[:, 5:6], stat[:, 3:4], AF.Ln)
                    scalar.activation(stat[:, 5:6], stat[:, 5:6], AF.Exp,
                                      scale=-0.5).then_inc(sSc, 1)
                cSc.add()

                @block.vector
                def _(vector, bo=bo):
                    vector.wait_ge(sSc, cSc.v)
                    vector.wait_ge(sL1s, cL1s.v)
                    vector.wait_ge(sLbb, cLbb.v)
                    vector.tensor_scalar(ln1b[:, :], r1b[:, bo:bo + D],
                                         stat[:, 2:3], stat[:, 5:6],
                                         OP.subtract, OP.mult).then_inc(sDv, 1)
                cDv.add()
                ln1_rdy = cDv.v

                @block.sync
                def _(sync, tt=tt, ln1_rdy=ln1_rdy):
                    sync.wait_ge(sDv, ln1_rdy)
                    sync.dma_start(out=ln1scr[tt * 128:(tt + 1) * 128, :],
                                   in_=ln1b[:, :]).then_inc(sL1s, 16)
                    if debug:
                        sync.dma_start(out=dbg["dbg_ln1"][tt * 128:(tt + 1) * 128, :],
                                       in_=ln1b[:, :]).then_inc(sSt, 16)
                cL1s.add(16)
                if debug:
                    cSt.add(16)

                @block.gpsimd
                def _(gpsimd, tt=tt, ln1_rdy=ln1_rdy,
                      tm=(tp_mark[tt - 2] if tt >= 2 else 0)):
                    gpsimd.wait_ge(sDv, ln1_rdy)
                    if tt >= 2:
                        gpsimd.wait_ge(sPe, tm)
                    gpsimd.dma_start(out=ln1bb[:, (tt % 2) * D:(tt % 2 + 1) * D],
                                     in_=ln1b[:, :]).then_inc(sLbb, 16)
                cast_mark.append(cLbb.add(16))

                if tt > 0:
                    emit_ln1_tp(tt - 1)
            emit_ln1_tp(NTT - 1)
            ln1_all = cDv.v

        if stop == 8:
            @block.sync
            def _(sync):
                sync.wait_ge(sDv, ln1_all)
                sync.wait_ge(sLd, cLd.v)
            return nc

        g_hb.__exit__(None, None, None)
        g_xres.__exit__(None, None, None)
        g_bob.__exit__(None, None, None)
        g_wob.__exit__(None, None, None)
        with contextlib.ExitStack() as ictx:
            Ei = ictx.enter_context
            hhres = Ei(nc.sbuf_tensor("hhres", [128, NPAIR * TC], BF16))
            silu_s = Ei(nc.sbuf_tensor("silu_s", [128, TC], F32))
            sqh = Ei(nc.sbuf_tensor("sqh", [128, 2 * TC], BF16))
            ssrow = Ei(nc.sbuf_tensor("ssrow", [128, TC], F32))
            sscol = Ei(nc.sbuf_tensor("sscol", [128, 16], F32))
            ssb = Ei(nc.sbuf_tensor("ssb", [128, TC], BF16))
            gwctx = contextlib.ExitStack()
            gw = gwctx.enter_context(nc.sbuf_tensor("gw", [128, 2 * D], BF16))
            uw = gwctx.enter_context(nc.sbuf_tensor("uw", [128, 2 * D], BF16))
            gctx = contextlib.ExitStack()
            Eg = gctx.enter_context
            pG = Eg(nc.psum_tensor("pG", [128, 1024], F32))
            pU = Eg(nc.psum_tensor("pU", [128, 1024], F32))
            pS = Eg(nc.psum_tensor("pS", [1, 1024], F32))
            pR = Eg(nc.psum_tensor("pR", [128, 128], BF16))

            mm_mark = {}
            sq_mark = {}
            for i in range(NPAIR):
                bo = (i % 2) * D
                ho = (i % 2) * TC
                sGu = (sGu0, sGu1)[i % 2]

                @block.sync
                def _(sync, i=i, bo=bo, sGu=sGu,
                      mw=(mm_mark[i - 2] if i >= 2 else 0)):
                    sync.wait_ge(sCc, wq_ag1)
                    if i >= 2:
                        sync.wait_ge(sPe, mw)
                    sync.dma_start(out=gw[:, bo:bo + D],
                                   in_=wg_tile(i % 8, 2 * (i // 8))
                                   ).then_inc(sGu, 16)
                    sync.dma_start(out=uw[:, bo:bo + D],
                                   in_=wg_tile(i % 8, 2 * (i // 8) + 1)
                                   ).then_inc(sGu, 16)
                gu_ld = cGu[i % 2].add(32)

                for ts_ in range(2):
                    po = ts_ * 512

                    @block.tensor
                    def _(tensor, i=i, ts_=ts_, po=po, bo=bo, gu_ld=gu_ld,
                          sGu=sGu):
                        tensor.wait_ge(sGu, gu_ld)
                        tensor.wait_ge(sDv, ln1_all)
                        if i > 0:
                            tensor.wait_ge(sDv, cDv.v - 1)
                        lastg = None
                        for k in range(DT):
                            tensor.matmul(
                                pG[:, po:po + 512],
                                gw[:, bo + k * 128: bo + (k + 1) * 128],
                                ln1cm[:, k * TC + ts_ * 512:
                                      k * TC + ts_ * 512 + 512],
                                start=(k == 0), stop=(k == DT - 1))
                        for k in range(DT):
                            lastg = tensor.matmul(
                                pU[:, po:po + 512],
                                uw[:, bo + k * 128: bo + (k + 1) * 128],
                                ln1cm[:, k * TC + ts_ * 512:
                                      k * TC + ts_ * 512 + 512],
                                start=(k == 0), stop=(k == DT - 1))
                        lastg.then_inc(sPe, 1)
                    cPe.add()

                    @block.scalar
                    def _(scalar, ts_=ts_, po=po):
                        scalar.wait_ge(sPe, cPe.v)
                        scalar.activation(silu_s[:, po:po + 512],
                                          pG[:, po:po + 512], AF.Silu
                                          ).then_inc(sSc, 1)
                    cSc.add()

                    @block.vector
                    def _(vector, i=i, ts_=ts_, po=po):
                        vector.wait_ge(sSc, cSc.v)
                        if i == 0 and ts_ == 0:
                            vector.wait_ge(sL1s, cL1s.v)
                            vector.wait_ge(sLbb, cLbb.v)
                            if debug:
                                vector.wait_ge(sSt, cSt.v)
                        vector.tensor_mul(hhres[:, i * TC + po: i * TC + po + 512],
                                          silu_s[:, po:po + 512],
                                          pU[:, po:po + 512]).then_inc(sDv, 1)
                    cDv.add()

                    @block.scalar
                    def _(scalar, i=i, ts_=ts_, po=po):
                        scalar.wait_ge(sDv, cDv.v)
                        scalar.activation(sqh[:, ((i % 2) * 2 + ts_) * 512:
                                              ((i % 2) * 2 + ts_) * 512 + 512],
                                          hhres[:, i * TC + po: i * TC + po + 512],
                                          AF.Square).then_inc(sSc, 1)
                    cSc.add()
                mm_mark[i] = cPe.v
                sq_mark[i] = cSc.v

                if i > 0:
                    @block.tensor
                    def _(tensor, i=i, sw=sq_mark[i - 1]):
                        tensor.wait_ge(sSc, sw)
                        for ts_ in range(2):
                            tensor.matmul(pS[:, ts_ * 512:(ts_ + 1) * 512],
                                          onesb[:, :],
                                          sqh[:, (((i - 1) % 2) * 2 + ts_) * 512:
                                              (((i - 1) % 2) * 2 + ts_) * 512 + 512],
                                          start=(i - 1 == 0), stop=False,
                                          skip_group_check=True).then_inc(sPe, 1)
                            cPe.add()

            @block.tensor
            def _(tensor):
                tensor.wait_ge(sSc, cSc.v)
                last = None
                for ts_ in range(2):
                    last = tensor.matmul(
                        pS[:, ts_ * 512:(ts_ + 1) * 512], onesb[:, :],
                        sqh[:, (((NPAIR - 1) % 2) * 2 + ts_) * 512:
                            (((NPAIR - 1) % 2) * 2 + ts_) * 512 + 512],
                        start=False, stop=True, skip_group_check=True)
                last.then_inc(sPe, 1)
            cPe.add()
            gu_done_pe = cPe.v

            @block.vector
            def _(vector):
                vector.wait_ge(sPe, gu_done_pe)
                vector.memset(ssrow[:, :], 0.0)
                vector.tensor_copy(ssrow[0:1, :], pS[:, :])
                vector.tensor_copy(ssb[:, :], ssrow[:, :]).then_inc(sDv, 1)
            cDv.add()
            ss_rdy = cDv.v

            for tt in range(NTT):
                @block.tensor
                def _(tensor, tt=tt, ss_rdy=ss_rdy):
                    tensor.wait_ge(sDv, ss_rdy)
                    if tt > 0:
                        tensor.wait_ge(sDv, cDv.v)
                    tensor.transpose(pR[:, :], ssb[:, tt * 128:(tt + 1) * 128],
                                     ident[:, :]).then_inc(sPe, 1)
                cPe.add()

                @block.vector
                def _(vector, tt=tt):
                    vector.wait_ge(sPe, cPe.v)
                    vector.tensor_copy(sscol[:, tt:tt + 1], pR[:, 0:1]
                                       ).then_inc(sDv, 1)
                cDv.add()

            @block.vector
            def _(vector):
                vector.wait_ge(sDv, cDv.v)
                vector.tensor_scalar(sscol[:, 0:8], sscol[:, 0:8], 1.0 / H,
                                     RMS_EPS, OP.mult, OP.add).then_inc(sDv, 1)
            cDv.add()

            @block.scalar
            def _(scalar):
                scalar.wait_ge(sDv, cDv.v)
                scalar.activation(sscol[:, 8:16], sscol[:, 0:8], AF.Ln)
                scalar.activation(sscol[:, 8:16], sscol[:, 8:16], AF.Exp,
                                  scale=-0.5).then_inc(sSc, 1)
            cSc.add()
            inv_rdy = cSc.v

            if stop == 9:
                @block.sync
                def _(sync):
                    sync.wait_ge(sSc, inv_rdy)
                return nc

            if debug:
                @block.sync
                def _(sync):
                    sync.wait_ge(sSc, inv_rdy)
                    sync.dma_start(out=dbg["dbg_inv"], in_=sscol[:, :]
                                   ).then_inc(sSt, 16)
                cSt.add(16)

            gctx.close()
            gwctx.close()
            g_ln1cm.__exit__(None, None, None)
            outpre = Ei(nc.sbuf_tensor("outpre", [128, NTT * D], BF16))
            sq2 = Ei(nc.sbuf_tensor("sq2", [128, 512], BF16))
            st2 = Ei(nc.sbuf_tensor("st2", [128, 48], F32))
            pwlctx = contextlib.ExitStack()
            Epw = pwlctx.enter_context
            pwb = Epw(nc.sbuf_tensor("pwb", [128, 4 * 512], BF16))
            lnrb = Epw(nc.sbuf_tensor("lnrb", [128, NTT * 512], F32))
            pP = Ei(nc.psum_tensor("pP", [128, 4096], F32))

            for cs in range(4):
                @block.scalar
                def _(scalar, cs=cs):
                    if cs == 0:
                        scalar.wait_ge(sL1s, cL1s.v)
                        scalar.wait_ge(sPe, gu_done_pe)
                    else:
                        scalar.wait_ge(sDv, cDv.v)
                    for tt in range(NTT):
                        scalar.dma_start(
                            out=lnrb[:, tt * 512:(tt + 1) * 512],
                            in_=ln1scr[tt * 128:(tt + 1) * 128,
                                       cs * 512:(cs + 1) * 512]
                        ).then_inc(sLnr, 16)
                lnr_all = cLnr.add(16 * NTT)

                for k in range(NPK):
                    po = (k % 4) * 512
                    sPj = (sPj0, sPj1, sPj2, sPj3)[k % 4]

                    @block.sync
                    def _(sync, cs=cs, k=k, po=po, sPj=sPj):
                        if cs == 0 and k == 0:
                            sync.wait_ge(sCc, wq_ag)
                            sync.wait_ge(sL1s, cL1s.v)
                            sync.wait_ge(sPe, gu_done_pe)
                        if k >= 4 or cs > 0:
                            sync.wait_ge(sPe, cPe.v - 3)
                        src = wg_tile(k % 8, 16 + k // 8)
                        sync.dma_start(out=pwb[:, po:po + 512],
                                       in_=src[:, cs * 512:(cs + 1) * 512]
                                       ).then_inc(sPj, 16)
                    pj_ld = cPj[k % 4].add(16)

                    @block.tensor
                    def _(tensor, cs=cs, k=k, po=po, pj_ld=pj_ld,
                          sPj=sPj):
                        tensor.wait_ge(sPj, pj_ld)
                        if k == 0 and cs == 0:
                            tensor.wait_ge(sSc, inv_rdy)
                        if k == 0 and cs > 0:
                            tensor.wait_ge(sDv, cDv.v)
                        last = None
                        for tt in range(NTT):
                            last = tensor.matmul(
                                pP[:, tt * 512:(tt + 1) * 512],
                                hhres[:, k * TC + tt * 128: k * TC + (tt + 1) * 128],
                                pwb[:, po:po + 512],
                                start=(k == 0), stop=(k == NPK - 1),
                                skip_group_check=True)
                        last.then_inc(sPe, 1)
                    cPe.add()

                for tt in range(NTT):
                    osl = outpre[:, tt * D + cs * 512: tt * D + (cs + 1) * 512]

                    @block.vector
                    def _(vector, cs=cs, tt=tt, osl=osl, lnr_all=lnr_all):
                        vector.wait_ge(sLnr, lnr_all)
                        vector.wait_ge(sPe, cPe.v)
                        vector.wait_ge(sSc, inv_rdy)
                        vector.scalar_tensor_tensor(
                            osl, pP[:, tt * 512:(tt + 1) * 512],
                            sscol[:, 8 + tt:9 + tt],
                            lnrb[:, tt * 512:(tt + 1) * 512],
                            OP.mult, OP.add).then_inc(sDv, 1)
                    stt_mark = cDv.add()

                    @block.vector
                    def _(vector, cs=cs, tt=tt, osl=osl):
                        if cs == 0:
                            vector.tensor_reduce(st2[:, tt:tt + 1], osl, AX,
                                                 OP.add).then_inc(sDv, 1)
                        else:
                            vector.tensor_reduce(st2[:, 16:17], osl, AX,
                                                 OP.add)
                            vector.tensor_add(st2[:, tt:tt + 1],
                                              st2[:, tt:tt + 1],
                                              st2[:, 16:17]).then_inc(sDv, 1)
                    cDv.add()

                    @block.scalar
                    def _(scalar, cs=cs, tt=tt, osl=osl, m=stt_mark):
                        scalar.wait_ge(sDv, m)
                        ac = st2[:, 8 + tt:9 + tt] if cs == 0 else st2[:, 17:18]
                        scalar.activation(sq2[:, :], osl, AF.Square,
                                          accum_out=ac).then_inc(sSc, 1)
                    sq_mk = cSc.add()

                    if cs > 0:
                        @block.vector
                        def _(vector, tt=tt, m=sq_mk):
                            vector.wait_ge(sSc, m)
                            vector.tensor_add(st2[:, 8 + tt:9 + tt],
                                              st2[:, 8 + tt:9 + tt],
                                              st2[:, 17:18]).then_inc(sDv, 1)
                        cDv.add()
            proj_done = cDv.v

            if stop == 10:
                @block.sync
                def _(sync):
                    sync.wait_ge(sDv, proj_done)
                return nc

            pwlctx.close()
            ln2b = Ei(nc.sbuf_tensor("ln2b", [128, 2 * D], F32))

            @block.vector
            def _(vector):
                vector.wait_ge(sSc, cSc.v)
                vector.tensor_scalar_mul(st2[:, 24:32], st2[:, 0:8], 1.0 / D)
                vector.tensor_scalar_mul(st2[:, 32:40], st2[:, 8:16], 1.0 / D)
                vector.tensor_mul(st2[:, 16:24], st2[:, 24:32], st2[:, 24:32])
                vector.tensor_sub(st2[:, 32:40], st2[:, 32:40], st2[:, 16:24])
                vector.tensor_scalar_add(st2[:, 32:40], st2[:, 32:40], LN_EPS
                                         ).then_inc(sDv, 1)
            cDv.add()

            @block.scalar
            def _(scalar):
                scalar.wait_ge(sDv, cDv.v)
                scalar.activation(st2[:, 40:48], st2[:, 32:40], AF.Ln)
                scalar.activation(st2[:, 40:48], st2[:, 40:48], AF.Exp,
                                  scale=-0.5).then_inc(sSc, 1)
            cSc.add()
            j_inv = cSc.v

            for tt in range(NTT):
                bo = (tt % 2) * D
                src = outpre[:, tt * D:(tt + 1) * D]

                @block.vector
                def _(vector, tt=tt, bo=bo, src=src):
                    vector.wait_ge(sSc, j_inv)
                    if tt >= 2:
                        vector.wait_ge((sJl0, sJl1)[tt % 2], cJl[tt % 2].v)
                    vector.tensor_scalar(ln2b[:, bo:bo + D], src,
                                         st2[:, 24 + tt:25 + tt],
                                         st2[:, 40 + tt:41 + tt],
                                         OP.subtract, OP.mult).then_inc(sDv, 1)
                cDv.add()

                @block.sync
                def _(sync, tt=tt, bo=bo):
                    sync.wait_ge(sDv, cDv.v)
                    sync.dma_start(out=out_t[tt * 128:(tt + 1) * 128, :],
                                   in_=ln2b[:, bo:bo + D]
                                   ).then_inc((sJl0, sJl1)[tt % 2], 16)
                cJl[tt % 2].add(16)

            @block.sync
            def _(sync):
                sync.wait_ge(sJl0, cJl[0].v)
                sync.wait_ge(sJl1, cJl[1].v)
                sync.wait_ge(sSt, cSt.v)

    return nc


def _stage_inputs(inputs):
    f32 = np.float32
    x = np.ascontiguousarray(np.asarray(inputs["L0_attn_x"], f32).reshape(8192, D))
    W_in = np.ascontiguousarray(np.asarray(inputs["L0_attn_W_in"], f32))
    b_in = np.asarray(inputs["L0_attn_b_in"], f32)
    W_gx = np.ascontiguousarray(np.asarray(inputs["L0_attn_W_gx"], f32)
                                .reshape(GT, 2, 128, 128))
    W_ga = np.ascontiguousarray(np.asarray(inputs["L0_attn_W_ga"], f32)
                                .reshape(GT, 2, 128, 128))
    b_gx = np.asarray(inputs["L0_attn_b_gx"], f32)
    b_ga = np.asarray(inputs["L0_attn_b_ga"], f32)
    a_real = np.asarray(inputs["L0_attn_a_real"], f32)
    a_imag = np.asarray(inputs["L0_attn_a_imag"], f32)
    W_out = np.ascontiguousarray(np.asarray(inputs["L0_attn_W_out"], f32))
    b_out = np.asarray(inputs["L0_attn_b_out"], f32)
    wgu = np.asarray(inputs["L0_ffn_w_gate_up"], f32)
    wpj = np.asarray(inputs["L0_ffn_w_proj"], f32)

    common = dict(
        w_in=np.ascontiguousarray(
            W_in.reshape(DT, 128, DT, 128).transpose(2, 1, 0, 3)),
        b_in_rs=np.ascontiguousarray(b_in.reshape(DT, 128).T),
        w_gx=W_gx, w_ga=W_ga,
        b_gx_rs=np.ascontiguousarray(b_gx.T),
        b_ga_rs=np.ascontiguousarray(b_ga.T),
        ar_rs=np.ascontiguousarray(a_real.reshape(GT, 128).T),
        ai_rs=np.ascontiguousarray(a_imag.reshape(GT, 128).T),
        w_out=W_out,
        b_out_bc=np.ascontiguousarray(np.broadcast_to(b_out, (128, D))),
        ident=np.eye(128, dtype=ml_dtypes.bfloat16),
    )

    in_maps = []
    for c in range(NCORES):
        gu_rows = []
        for j in range(8):
            i = 8 * j + c
            gu_rows.append(wgu[128 * i:128 * (i + 1)])
            gu_rows.append(wgu[H + 128 * i:H + 128 * (i + 1)])
        wgu_shard = np.ascontiguousarray(np.stack(gu_rows))
        pj_blocks = []
        for m in range(8):
            k = 8 * m + c
            pj_blocks.append(wpj[:, 128 * k:128 * (k + 1)].reshape(16, 128, 128))
        wpj_shard = np.ascontiguousarray(np.stack(pj_blocks))
        m = dict(common)
        m.update(
            x=np.ascontiguousarray(x[TC * c:TC * (c + 1)]),
            wgu_sh=wgu_shard, wpj_sh=wpj_shard,
            mask_lt=np.ascontiguousarray(np.broadcast_to(
                (np.arange(NCORES) < c).astype(f32), (128, NCORES))),
        )
        in_maps.append(m)
    return in_maps


def kernel(**inputs):
    in_maps = _stage_inputs(inputs)
    nc = build_nc(debug=_DEBUG)
    res = run_bass_kernel_spmd(nc, in_maps, core_ids=list(range(NCORES)))
    out = np.concatenate([np.asarray(r["out"]) for r in res.results], axis=0)
    return out.reshape(1, 8192, D)


# version 17 (final)
# speedup vs baseline: 1.0532x; 1.0012x over previous
import contextlib
import numpy as np
import ml_dtypes

import concourse.bass as bass
import concourse.mybir as mybir
from concourse.bass_utils import run_bass_kernel_spmd

F32 = mybir.dt.float32
BF16 = mybir.dt.bfloat16
AF = mybir.ActivationFunctionType
OP = mybir.AluOpType
AX = mybir.AxisListType.X

NCORES = 8
TC = 1024
NTT = 8
D = 2048
DT = 16
GT = 8
H = 8192
NPAIR = 64
NPK = 64
W = 64
CH = 128
NCH = 8
NW = 16

MAGIC = 12582912.0
TWO_PI = 6.283185307179586
INV_2PI = 0.15915494309189535
HALF_PI = 1.5707963267948966
LNH = -0.34657359027997264
TINY = 1e-35
LN_EPS = 1e-5
RMS_EPS = 1.1920929e-07

_DEBUG = False
BARRIER_AG = False
RACE_DETECT = False
RAW_GROUPS = {"lrli", "tts1", "rr", "w1w2", "big", "tts2", "ext"}
A_LEVEL = 3


class _SerEngine:

    _PASS = {"wait_ge", "drain", "sem_inc", "nop", "engine_nop", "wait_op"}
    _HAZARD = {"tensor_reduce", "reduce_sum", "reduce_max", "reciprocal"}

    def __init__(self, eng, always_drain):
        self._e = eng
        self._always = always_drain

    def __getattr__(self, n):
        f = getattr(self._e, n)
        if n in self._PASS or not callable(f):
            return f
        def wrap(*a, **k):
            r = f(*a, **k)
            if RACE_DETECT or self._always:
                self._e.drain()
            return r
        return wrap


class Ctr:
    def __init__(self):
        self.v = 0

    def add(self, n=1):
        self.v += n
        return self.v


def build_nc(debug=False, stop=None, skip_prepass=False, skip_ag=False):
    def _R(eng, grp="x"):
        return eng._e if (grp in RAW_GROUPS and not RACE_DETECT) else eng

    nc = bass.Bass("TRN2", target_bir_lowering=False, debug=False,
                   detect_race_conditions=RACE_DETECT)

    x_in = nc.dram_tensor("x", [TC, D], F32, kind="ExternalInput").ap()
    w_in = nc.dram_tensor("w_in", [DT, 128, DT, 128], F32,
                          kind="ExternalInput").ap()
    b_in_rs = nc.dram_tensor("b_in_rs", [128, DT], F32, kind="ExternalInput").ap()
    w_gx = nc.dram_tensor("w_gx", [GT, 2, 128, 128], F32, kind="ExternalInput").ap()
    w_ga = nc.dram_tensor("w_ga", [GT, 2, 128, 128], F32, kind="ExternalInput").ap()
    b_gx_rs = nc.dram_tensor("b_gx_rs", [128, GT], F32, kind="ExternalInput").ap()
    b_ga_rs = nc.dram_tensor("b_ga_rs", [128, GT], F32, kind="ExternalInput").ap()
    ar_rs = nc.dram_tensor("ar_rs", [128, GT], F32, kind="ExternalInput").ap()
    ai_rs = nc.dram_tensor("ai_rs", [128, GT], F32, kind="ExternalInput").ap()
    w_out = nc.dram_tensor("w_out", [D, D], F32, kind="ExternalInput").ap()
    b_out_bc = nc.dram_tensor("b_out_bc", [128, D], F32, kind="ExternalInput").ap()
    wgu_sh = nc.dram_tensor("wgu_sh", [16, 128, D], F32, kind="ExternalInput").ap()
    wpj_sh = nc.dram_tensor("wpj_sh", [8, 16, 128, 128], F32,
                            kind="ExternalInput").ap()
    mask_lt = nc.dram_tensor("mask_lt", [128, NCORES], F32,
                             kind="ExternalInput").ap()
    ident_in = nc.dram_tensor("ident", [128, 128], BF16, kind="ExternalInput").ap()

    out_t = nc.dram_tensor("out", [TC, D], F32, kind="ExternalOutput").ap()

    dbg = {}
    if debug:
        for name, shape, dt_ in [
            ("dbg_xh", [128, DT * TC], F32),
            ("dbg_ga", [128, GT * TC], F32),
            ("dbg_gx", [128, GT * TC], F32),
            ("dbg_h", [128, DT * TC], BF16),
            ("dbg_ln1", [NTT * 128, D], F32),
            ("dbg_inv", [128, 16], F32),
            ("dbg_wq", [24 * 128, D], BF16),
            ("dbg_hin", [128, 256], F32),
            ("dbg_scr", [NCH * 4 * 128, 1024], BF16),
            ("dbg_summ", [128, 512], F32),
        ]:
            dbg[name] = nc.dram_tensor(name, shape, dt_, kind="ExternalOutput").ap()

    wshard = nc.dram_tensor("wshard", [24 * 128, D], BF16).ap()
    wgath1 = nc.dram_tensor("wgath1", [NCORES * 16 * 128, D], BF16).ap()
    wgath2 = nc.dram_tensor("wgath2", [NCORES * 8 * 128, D], BF16).ap()
    scanscr = nc.dram_tensor("scanscr", [NCH * 4 * 128, 1024], BF16).ap()
    cc_in = nc.dram_tensor("cc_in", [128, 32], F32).ap()
    cc_out = nc.dram_tensor("cc_out", [NCORES * 128, 32], F32).ap()
    ln1scr = nc.dram_tensor("ln1scr", [NTT * 128, D], F32).ap()
    hhscr = nc.dram_tensor("hhscr", [NPK * 128, TC], BF16).ap()
    outscr = nc.dram_tensor("outscr", [NTT * 128, D], F32).ap()

    def wg_tile(owner, pos):
        if pos < 16:
            base = (owner * 16 + pos) * 128
            return wgath1[base:base + 128, :]
        base = (owner * 8 + pos - 16) * 128
        return wgath2[base:base + 128, :]

    with contextlib.ExitStack() as ctx:
        E = ctx.enter_context
        sLd = E(nc.semaphore("sLd"))
        sSt = E(nc.semaphore("sSt"))
        sGp = E(nc.semaphore("sGp"))
        sCc = E(nc.semaphore("sCc"))
        sPe = E(nc.semaphore("sPe"))
        sDv = E(nc.semaphore("sDv"))
        sSc = E(nc.semaphore("sSc"))
        cLd, cSt, cGp, cCc, cPe, cDv, cSc = (Ctr() for _ in range(7))

        _block = E(nc.Block())

        class _BlockShim:
            def __init__(self, b):
                self._b = b

            def vector(self, f):
                return self._b.vector(lambda eng: f(_SerEngine(eng, True)))

            def scalar(self, f):
                return self._b.scalar(lambda eng: f(_SerEngine(eng, False)))

            def __getattr__(self, n):
                return getattr(self._b, n)

        block = _BlockShim(_block)

        sCst = E(nc.semaphore("sCst"))
        sWqL = E(nc.semaphore("sWqL"))
        sWsT = E(nc.semaphore("sWsT"))
        sWsP = E(nc.semaphore("sWsP"))
        sXl0 = E(nc.semaphore("sXl0"))
        sXl1 = E(nc.semaphore("sXl1"))
        sWn0 = E(nc.semaphore("sWn0"))
        sWn1 = E(nc.semaphore("sWn1"))
        sGts = E(nc.semaphore("sGts"))
        sScr = E(nc.semaphore("sScr"))
        sScr1 = E(nc.semaphore("sScr1"))
        sCo0 = E(nc.semaphore("sCo0"))
        sCo1 = E(nc.semaphore("sCo1"))
        sXr0 = E(nc.semaphore("sXr0"))
        sXr1 = E(nc.semaphore("sXr1"))
        sBo = E(nc.semaphore("sBo"))
        sWo = E(nc.semaphore("sWo"))
        sL1s = E(nc.semaphore("sL1s"))
        sLbb = E(nc.semaphore("sLbb"))
        sGu0 = E(nc.semaphore("sGu0"))
        sGu1 = E(nc.semaphore("sGu1"))
        sHhS = E(nc.semaphore("sHhS"))
        sPj0 = E(nc.semaphore("sPj0"))
        sPj1 = E(nc.semaphore("sPj1"))
        sPj2 = E(nc.semaphore("sPj2"))
        sPj3 = E(nc.semaphore("sPj3"))
        sLnr = E(nc.semaphore("sLnr"))
        sOsc = E(nc.semaphore("sOsc"))
        sOsc1 = E(nc.semaphore("sOsc1"))
        sJl0 = E(nc.semaphore("sJl0"))
        sJl1 = E(nc.semaphore("sJl1"))
        sCgI = E(nc.semaphore("sCgI"))
        (cCst, cWqL, cWsT, cWsP, cGts, cScr, cScr1, cWo, cL1s, cLbb, cHhS,
         cLnr, cOsc, cOsc1, cCgI) = (Ctr() for _ in range(15))
        cXl = [Ctr(), Ctr()]
        cWn = [Ctr(), Ctr()]
        cCo = [Ctr(), Ctr()]
        cXr = [Ctr(), Ctr()]
        cGu = [Ctr(), Ctr()]
        cPj = [Ctr(), Ctr(), Ctr(), Ctr()]
        cJl = [Ctr(), Ctr()]

        ident = E(nc.sbuf_tensor("identB", [128, 128], BF16))
        consts = E(nc.sbuf_tensor("consts", [128, 64], F32))
        onesb = E(nc.sbuf_tensor("onesb", [128, 1], BF16))
        ones_row = E(nc.sbuf_tensor("ones_row", [1, 128], BF16))
        nc.const_aps.register(F32, HALF_PI, consts[:, 56:57])
        nc.const_aps.register(F32, LNH, consts[:, 57:58])
        nc.const_aps.register(F32, 0.25, consts[:, 58:59])
        nc.const_aps.register(F32, MAGIC, consts[:, 59:60])
        nc.const_aps.register(F32, -MAGIC, consts[:, 60:61])
        ones64 = E(nc.sbuf_tensor("ones64", [128, W], F32))
        summ = E(nc.sbuf_tensor("summ", [128, 4 * 128], F32))
        hinb = E(nc.sbuf_tensor("hinb", [128, 256], F32))
        sgath = E(nc.sbuf_tensor("sgath", [128, NCORES * 32], F32))

        @block.sync
        def _(sync):
            sync.dma_start(out=ident[:, :], in_=ident_in).then_inc(sCst, 16)
            sync.dma_start(out=consts[:, 0:16], in_=b_in_rs).then_inc(sCst, 16)
            sync.dma_start(out=consts[:, 16:24], in_=b_gx_rs).then_inc(sCst, 16)
            sync.dma_start(out=consts[:, 24:32], in_=b_ga_rs).then_inc(sCst, 16)
            sync.dma_start(out=consts[:, 32:40], in_=ar_rs).then_inc(sCst, 16)
            sync.dma_start(out=consts[:, 40:48], in_=ai_rs).then_inc(sCst, 16)
            sync.dma_start(out=consts[:, 48:56], in_=mask_lt).then_inc(sCst, 16)
        consts_ld = cCst.add(16 * 7)

        @block.scalar
        def _(scalar):
            scalar.wait_ge(sCst, consts_ld)
            scalar.activation(consts[:, 32:40], consts[:, 32:40], AF.Exp)
            scalar.activation(consts[:, 32:40], consts[:, 32:40], AF.Ln,
                              bias=1.0).then_inc(sSc, 1)
        cSc.add()

        @block.vector
        def _(vector):
            vector.wait_ge(sSc, cSc.v)
            vector.tensor_scalar_mul(consts[:, 32:40], consts[:, 32:40], -8.0)
            vector.memset(consts[:, 56:57], HALF_PI)
            vector.memset(consts[:, 57:58], LNH)
            vector.memset(consts[:, 58:59], 0.25)
            vector.memset(consts[:, 59:60], MAGIC)
            vector.memset(consts[:, 60:61], -MAGIC)
            vector.memset(onesb[:, :], 1.0)
            vector.memset(ones_row[:, :], 1.0)
            vector.memset(ones64[:, :], 1.0).then_inc(sDv, 1)
        consts_rdy = cDv.add()

        g_xh = nc.sbuf_tensor("xh", [128, DT * TC], F32)
        xh = g_xh.__enter__()
        g_fb = nc.sbuf_tensor("fbuf", [128, 1112], F32)
        fbuf = g_fb.__enter__()
        g_ga = nc.sbuf_tensor("ga", [128, GT * TC], F32)
        ga = g_ga.__enter__()
        g_gx = nc.sbuf_tensor("gx", [128, GT * TC], F32)
        gx = g_gx.__enter__()

        pctx = contextlib.ExitStack()
        p1_qrdy = {}
        if not skip_prepass:
            Ep = pctx.enter_context
            wq_b = Ep(nc.sbuf_tensor("wq_b", [128, D], BF16, side="right"))
            wq_s = Ep(nc.sbuf_tensor("wq_s", [128, 32], F32, side="right"))
            wq_sb = Ep(nc.sbuf_tensor("wq_sb", [128, 32], BF16, side="right"))
            wq_q = Ep(nc.sbuf_tensor("wq_q", [128, D], F32, side="right"))
            wq_hat = Ep(nc.sbuf_tensor("wq_hat", [128, D], BF16, side="right"))
            wq_T = Ep(nc.sbuf_tensor("wq_T", [128, D], BF16, side="right"))
            pq = Ep(nc.psum_tensor("pq", [128, 512], BF16, side="right"))

        def emit_p1_a(pos):
            if skip_prepass:
                return

            @block.gpsimd
            def _(gpsimd, pos=pos):
                if pos > 0:
                    gpsimd.wait_ge(sDv, cDv.v)
                if pos < 16:
                    gpsimd.dma_start(out=wq_b[:, :], in_=wgu_sh[pos]
                                     ).then_inc(sWqL, 16)
                else:
                    src = wpj_sh[pos - 16].rearrange("a p f -> p a f")
                    dst = wq_b.ap().rearrange("p (a f) -> p a f", f=128)
                    gpsimd.dma_start(out=dst, in_=src).then_inc(sWqL, 16)
            cWqL.add(16)

            @block.vector
            def _(vector, pos=pos):
                vector.wait_ge(sWqL, cWqL.v)
                if pos > 0:
                    vector.wait_ge(sPe, cPe.v)
                    if pos <= 16:
                        vector.wait_ge(sWsT, cWsT.v)
                    else:
                        vector.wait_ge(sWsP, cWsP.v)
                b3 = wq_b.ap().rearrange("p (g j) -> p g j", j=W)
                vector.tensor_reduce(wq_s[:, :], b3, AX, OP.add,
                                     apply_absolute_value=True)
                vector.tensor_scalar_mul(wq_sb[:, :], wq_s[:, :], 1.0 / W)
                vector.tensor_scalar_max(wq_q[:, 0:32], wq_sb[:, :], 1e-8)
                vector.reciprocal(wq_s[:, :], wq_q[:, 0:32])
                rb = wq_s.ap()[:, :, None].broadcast_to([128, 32, W])
                h3 = wq_hat.ap().rearrange("p (g j) -> p g j", j=W)
                vector.tensor_tensor(h3, b3, rb, OP.mult)
                vector.tensor_scalar(wq_q[:, :], wq_hat[:, :], MAGIC, MAGIC,
                                     OP.add, OP.subtract)
                vector.tensor_scalar(wq_q[:, :], wq_q[:, :], 1.0, -1.0,
                                     OP.min, OP.max)
                sb3 = wq_sb.ap()[:, :, None].broadcast_to([128, 32, W])
                q3 = wq_q.ap().rearrange("p (g j) -> p g j", j=W)
                vector.tensor_tensor(h3, q3, sb3, OP.mult).then_inc(sDv, 1)
            cDv.add()
            p1_qrdy[pos] = cDv.v

        def emit_p1_b(pos):
            if skip_prepass:
                return
            q_rdy = p1_qrdy[pos]
            for grp in range(4):
                @block.tensor
                def _(tensor, pos=pos, grp=grp, q_rdy=q_rdy):
                    tensor.wait_ge(sDv, q_rdy)
                    tensor.wait_ge(sCst, consts_ld)
                    if not (pos == 0 and grp == 0):
                        tensor.wait_ge(sDv, cDv.v)
                    last = None
                    for j in range(4):
                        k = grp * 4 + j
                        last = tensor.transpose(
                            pq[:, j * 128:(j + 1) * 128],
                            wq_hat[:, k * 128:(k + 1) * 128], ident[:, :])
                    last.then_inc(sPe, 1)
                cPe.add()

                @block.vector
                def _(vector, grp=grp):
                    vector.wait_ge(sPe, cPe.v)
                    vector.tensor_copy(wq_T[:, grp * 512:(grp + 1) * 512],
                                       pq[:, :]).then_inc(sDv, 1)
                cDv.add()

            @block.gpsimd
            def _(gpsimd, pos=pos):
                gpsimd.wait_ge(sDv, cDv.v)
                sem = sWsT if pos < 16 else sWsP
                gpsimd.dma_start(out=wshard[pos * 128:(pos + 1) * 128, :],
                                 in_=wq_T[:, :]).then_inc(sem, 16)
            (cWsT if pos < 16 else cWsP).add(16)

        with contextlib.ExitStack() as actx:
            Ea = actx.enter_context
            xcm = Ea(nc.sbuf_tensor("xcm", [128, DT * TC], BF16))
            xtmb = Ea(nc.sbuf_tensor("xtmb", [128, D], BF16))
            pA = Ea(nc.psum_tensor("pA", [128, 2048], BF16))

            for tt in range(NTT):
                bo = 0
                sXl = (sXl0, sXl1)[tt % 2]
                cXl_ = cXl[tt % 2]

                @block.gpsimd
                def _(gpsimd, tt=tt, bo=bo, sXl=sXl):
                    if tt >= 1:
                        gpsimd.wait_ge(sPe, cPe.v)
                    gpsimd.dma_start(out=xtmb[:, bo:bo + D],
                                     in_=x_in[tt * 128:(tt + 1) * 128, :]
                                     ).then_inc(sXl, 16)
                x_ld = cXl_.add(16)

                for grp in range(4 if A_LEVEL >= 2 else 0):
                    po = (grp % 2) * 1024

                    @block.tensor
                    def _(tensor, tt=tt, grp=grp, bo=bo, po=po, x_ld=x_ld,
                          sXl=sXl):
                        tensor.wait_ge(sXl, x_ld)
                        tensor.wait_ge(sCst, consts_ld)
                        if tt * 4 + grp >= 2:
                            tensor.wait_ge(sDv, cDv.v)
                        last = None
                        for j in range(4):
                            k = grp * 4 + j
                            last = tensor.transpose(
                                pA[:, po + j * 128: po + (j + 1) * 128],
                                xtmb[:, bo + k * 128: bo + (k + 1) * 128],
                                ident[:, :])
                        last.then_inc(sPe, 1)
                    cPe.add()

                    if A_LEVEL >= 3:
                        @block.vector
                        def _(vector, tt=tt, grp=grp, po=po):
                            vector.wait_ge(sPe, cPe.v)
                            dst = xcm.ap().rearrange("p (j t) -> p j t", t=TC)[
                                :, grp * 4:(grp + 1) * 4, tt * 128:(tt + 1) * 128]
                            src = pA.ap()[:, po:po + 512].rearrange(
                                "p (j t) -> p j t", t=128)
                            vector.tensor_copy(dst, src).then_inc(sDv, 1)
                        cDv.add()
                    else:
                        @block.vector
                        def _(vector):
                            vector.wait_ge(sPe, cPe.v)
                            nc.vector.engine_nop().then_inc(sDv, 1)
                        cDv.add()

            if stop == 2:
                @block.sync
                def _(sync):
                    sync.wait_ge(sDv, cDv.v)
                actx.close()
                g_gx.__exit__(None, None, None)
                g_ga.__exit__(None, None, None)
                g_xh.__exit__(None, None, None)
                return nc

            with contextlib.ExitStack() as bctx:
                Eb = bctx.enter_context
                winj = Eb(nc.sbuf_tensor("winj", [128, 2 * D], BF16))
                pB = Eb(nc.psum_tensor("pB", [128, 1024], F32))

                x_cm_rdy = cDv.v
                n = 0
                for j in range(DT):
                    wo = (j % 2) * D
                    sWn = (sWn0, sWn1)[j % 2]

                    @block.gpsimd
                    def _(gpsimd, j=j, wo=wo, sWn=sWn):
                        if j >= 2:
                            gpsimd.wait_ge(sPe, cPe.v)
                        dst = winj.ap()[:, wo:wo + D].rearrange(
                            "p (k f) -> p k f", f=128)
                        gpsimd.dma_start(out=dst, in_=w_in[j]
                                         ).then_inc(sWn, 16)
                    wj_ld = cWn[j % 2].add(16)

                    for ts_ in range(2):
                        po = (n % 2) * 512

                        @block.tensor
                        def _(tensor, j=j, ts_=ts_, po=po, n=n, wo=wo,
                              wj_ld=wj_ld, sWn=sWn):
                            tensor.wait_ge(sWn, wj_ld)
                            tensor.wait_ge(sDv, x_cm_rdy)
                            if n >= 2:
                                tensor.wait_ge(sSc, cSc.v - 1)
                            last = None
                            for k in range(DT):
                                last = tensor.matmul(
                                    pB[:, po:po + 512],
                                    winj[:, wo + k * 128: wo + (k + 1) * 128],
                                    xcm[:, k * TC + ts_ * 512:
                                        k * TC + ts_ * 512 + 512],
                                    start=(k == 0), stop=(k == DT - 1))
                            last.then_inc(sPe, 1)
                        cPe.add()

                        @block.scalar
                        def _(scalar, j=j, ts_=ts_, po=po):
                            scalar.wait_ge(sPe, cPe.v)
                            scalar.activation(
                                xh[:, j * TC + ts_ * 512: j * TC + ts_ * 512 + 512],
                                pB[:, po:po + 512], AF.Identity,
                                bias=consts[:, j:j + 1]).then_inc(sSc, 1)
                        cSc.add()
                        n += 1
                    if j > 0:
                        emit_p1_b(j - 1)
                    emit_p1_a(j)
                xh_rdy_sc = cSc.v

        if stop == 3:
            @block.sync
            def _(sync):
                sync.wait_ge(sSc, xh_rdy_sc)
            g_gx.__exit__(None, None, None)
            g_ga.__exit__(None, None, None)
            g_xh.__exit__(None, None, None)
            return nc

        if debug:
            @block.sync
            def _(sync):
                sync.wait_ge(sSc, xh_rdy_sc)
                sync.dma_start(out=dbg["dbg_xh"], in_=xh[:, :]).then_inc(sSt, 16)
            cSt.add(16)

        with contextlib.ExitStack() as cctx:
            Ec = cctx.enter_context
            xhb = Ec(nc.sbuf_tensor("xhb", [128, DT * TC], BF16))
            wgb = Ec(nc.sbuf_tensor("wgb", [128, GT * 4 * 128], BF16))
            pC = Ec(nc.psum_tensor("pC", [128, 1024], F32))

            @block.gpsimd
            def _(gpsimd):
                gpsimd.wait_ge(sSc, xh_rdy_sc)
                for r in range(DT):
                    gpsimd.dma_start(out=xhb[:, r * TC:(r + 1) * TC],
                                     in_=xh[:, r * TC:(r + 1) * TC]
                                     ).then_inc(sGts, 16)
                for g in range(GT):
                    for k in range(2):
                        gpsimd.dma_start(
                            out=wgb[:, (g * 4 + k) * 128:(g * 4 + k + 1) * 128],
                            in_=w_gx[g, k]).then_inc(sGts, 16)
                        gpsimd.dma_start(
                            out=wgb[:, (g * 4 + 2 + k) * 128:(g * 4 + 3 + k) * 128],
                            in_=w_ga[g, k]).then_inc(sGts, 16)
            gates_ld = cGts.add(16 * (DT + GT * 4))

            n = 0
            for g in range(GT):
                for half in range(2):
                    for ts_ in range(2):
                        po = (n % 2) * 512

                        @block.tensor
                        def _(tensor, g=g, half=half, ts_=ts_, po=po, n=n):
                            tensor.wait_ge(sGts, gates_ld)
                            if n >= 2:
                                tensor.wait_ge(sSc, cSc.v - 1)
                            last = None
                            for k in range(2):
                                lb = (g * 4 + half * 2 + k) * 128
                                rbase = (2 * g + k) * TC + ts_ * 512
                                last = tensor.matmul(
                                    pC[:, po:po + 512],
                                    wgb[:, lb:lb + 128],
                                    xhb[:, rbase:rbase + 512],
                                    start=(k == 0), stop=(k == 1))
                            last.then_inc(sPe, 1)
                        cPe.add()

                        @block.scalar
                        def _(scalar, g=g, half=half, ts_=ts_, po=po):
                            scalar.wait_ge(sPe, cPe.v)
                            dstbuf = gx if half == 0 else ga
                            bcol = (16 if half == 0 else 24) + g
                            scalar.activation(
                                dstbuf[:, g * TC + ts_ * 512:
                                       g * TC + ts_ * 512 + 512],
                                pC[:, po:po + 512], AF.Sigmoid,
                                bias=consts[:, bcol:bcol + 1]).then_inc(sSc, 1)
                        cSc.add()
                        n += 1
                        if n % 4 == 0:
                            m = n // 4 - 1
                            emit_p1_b(15 + m)
                            emit_p1_a(16 + m)
            gates_rdy = cSc.v

        if stop == 4:
            @block.sync
            def _(sync):
                sync.wait_ge(sSc, gates_rdy)
            g_gx.__exit__(None, None, None)
            g_ga.__exit__(None, None, None)
            g_xh.__exit__(None, None, None)
            return nc

        if debug:
            @block.sync
            def _(sync):
                sync.wait_ge(sSc, gates_rdy)
                sync.dma_start(out=dbg["dbg_ga"], in_=ga[:, :]).then_inc(sSt, 16)
                sync.dma_start(out=dbg["dbg_gx"], in_=gx[:, :]).then_inc(sSt, 16)
            cSt.add(32)

        if not skip_prepass:
            emit_p1_b(23)
        pctx.close()

        def emit_wq_ag():
            if skip_prepass:
                return cCc.v, cCc.v
            if skip_ag:
                @block.gpsimd
                def _(gpsimd):
                    gpsimd.wait_ge(sWsT, cWsT.v)
                    gpsimd.wait_ge(sWsP, cWsP.v)
                    gpsimd.nop().then_inc(sCc, 2)
            else:
                @block.gpsimd
                def _(gpsimd):
                    gpsimd.wait_ge(sWsT, 16 * 16)
                    gpsimd.collective_compute(
                        "AllGather", OP.bypass,
                        replica_groups=[list(range(NCORES))],
                        ins=[wshard[0:16 * 128, :]], outs=[wgath1],
                    ).then_inc(sCc, 1)
                    gpsimd.wait_ge(sWsP, cWsP.v)
                    gpsimd.collective_compute(
                        "AllGather", OP.bypass,
                        replica_groups=[list(range(NCORES))],
                        ins=[wshard[16 * 128:, :]], outs=[wgath2],
                    ).then_inc(sCc, 1)
            return cCc.add(), cCc.add()

        wq_ag1, wq_ag = emit_wq_ag()

        if stop == 1:
            @block.sync
            def _(sync):
                sync.wait_ge(sCc, wq_ag)
            return nc

        @block.vector
        def _(vector):
            vector.wait_ge(sSc, gates_rdy)
            vector.wait_ge(sDv, consts_rdy)
            nc.vector.engine_nop().then_inc(sDv, 1)
        xc_rdy = cDv.add()

        NSLOT = 16
        g_pool = nc.sbuf_tensor("pool", [128, NSLOT * 1024], F32)
        poolt = g_pool.__enter__()

        def _mkP(par):
            def sl(s):
                if s in (16, 17):
                    return s - 16
                if par and s in (2, 3, 4, 5):
                    return s + 10
                return s
            Pf = lambda s: poolt[:, sl(s) * 1024:(sl(s) + 1) * 1024]
            P3f = lambda s, t=CH: poolt.ap()[:, sl(s) * 1024:
                                             (sl(s) + 1) * 1024].rearrange(
                "p (r t) -> p r t", t=t)
            return Pf, P3f

        scr_mark = []
        for c in range(NCH):
            t0 = c * CH
            P_, P3 = _mkP(c % 2)

            def gsl3(buf, off=0, t0=t0):
                return buf.ap().rearrange("p (r t) -> p r t", t=TC)[
                    :, off:off + GT, t0:t0 + CH]

            @block.scalar
            def _(scalar, c=c, t0=t0):
                scalar.wait_ge(sSc, gates_rdy)
                if c == 0:
                    scalar.wait_ge(sDv, xc_rdy)
                if c > 0:
                    scalar.wait_ge(sDv, cDv.v)
                rs = _R(scalar, "lrli")
                last = None
                for r in range(GT):
                    sl = ga[:, r * TC + t0: r * TC + t0 + CH]
                    rs.mul(P_(0)[:, r * CH:(r + 1) * CH], sl,
                           consts[:, 32 + r:33 + r])
                    last = rs.mul(P_(1)[:, r * CH:(r + 1) * CH], sl,
                                  consts[:, 40 + r:41 + r])
                last.then_inc(sSc, 1)
            cSc.add()

            @block.vector
            def _(vector, c=c, t0=t0,
                  sm=(scr_mark[c - 2] if c >= 2 else 0)):
                vector.wait_ge(sSc, cSc.v)
                if c == 0:
                    vector.wait_ge(sWsT, cWsT.v)
                    vector.wait_ge(sWsP, cWsP.v)
                if c >= 2:
                    vector.wait_ge((sScr, sScr1)[c % 2], sm)
                rv = _R(vector, "tts1")
                last = None
                for r in range(GT):
                    for w in range(2):
                        a = r * CH + w * W
                        rv.tensor_tensor_scan(
                            P_(2)[:, a:a + W], ones64[:, :], P_(0)[:, a:a + W],
                            0.0, OP.mult, OP.add)
                        last = rv.tensor_tensor_scan(
                            P_(3)[:, a:a + W], ones64[:, :], P_(1)[:, a:a + W],
                            0.0, OP.mult, OP.add)
                vector.drain()
                last.then_inc(sDv, 1)
            cDv.add()

            @block.scalar
            def _(scalar, c=c):
                scalar.wait_ge(sDv, cDv.v)
                rs = _R(scalar, "luts")
                rs.activation(P_(4), P_(0), AF.Exp, scale=2.0)
                rs.activation(P_(5), P_(0), AF.Exp, scale=4.0)
                rs.activation(P_(6), P_(2), AF.Exp)
                rs.activation(P_(7), P_(2), AF.Exp, scale=-1.0)
                last = rs.activation(P_(8), P_(1), AF.Sin, scale=2.0,
                                     bias=HALF_PI)
                last.then_inc(sSc, 1)
            cSc.add()

            @block.vector
            def _(vector, c=c):
                vector.wait_ge(sSc, cSc.v)
                rv = _R(vector, "rr")
                rv.tensor_scalar(P_(0), P_(3), INV_2PI, 0.25, OP.mult, OP.add)
                rv.tensor_scalar(P_(9), P_(3), INV_2PI, None, OP.mult)
                rv.tensor_scalar(P_(0), P_(0), MAGIC, MAGIC, OP.add, OP.subtract)
                rv.tensor_scalar(P_(9), P_(9), MAGIC, MAGIC, OP.add, OP.subtract)
                rv.scalar_tensor_tensor(P_(0), P_(0), -TWO_PI, P_(3),
                                        OP.mult, OP.add)
                rv.scalar_tensor_tensor(P_(2), P_(9), -TWO_PI, P_(3),
                                        OP.mult, OP.add)
                rv.tensor_scalar(P_(0), P_(0), HALF_PI - 1e-6,
                                 -3.0 * HALF_PI + 1e-6, OP.min, OP.max)
                rv.tensor_scalar(P_(2), P_(2), TWO_PI / 2 - 1e-6,
                                 -TWO_PI / 2 + 1e-6, OP.min, OP.max)
                vector.tensor_mul(P_(9), P_(4), P_(8))
                vector.scalar_tensor_tensor(P_(4), P_(9), -2.0, P_(5),
                                            OP.mult, OP.add)
                vector.tensor_scalar(P_(4), P_(4), 1.0, TINY, OP.add, OP.max
                                     ).then_inc(sDv, 1)
            cDv.add()

            @block.scalar
            def _(scalar, c=c):
                scalar.wait_ge(sDv, cDv.v)
                scalar.activation(P_(5), P_(4), AF.Ln)
                scalar.activation(P_(5), P_(5), AF.Exp, scale=0.5
                                  ).then_inc(sSc, 1)
            cSc.add()

            @block.vector
            def _(vector, c=c):
                vector.wait_ge(sSc, cSc.v)
                rv = _R(vector, "w1w2")
                rv.tensor_sub(P_(8), P_(5), P_(9))
                rv.tensor_add(P_(9), P_(5), P_(9))
                rv.tensor_scalar(P_(8), P_(8), 1.0, TINY, OP.add, OP.max)
                vector.tensor_scalar(P_(9), P_(9), -1.0, TINY, OP.add, OP.max
                                     ).then_inc(sDv, 1)
            cDv.add()

            @block.scalar
            def _(scalar, c=c):
                scalar.wait_ge(sDv, cDv.v)
                scalar.activation(P_(10), P_(8), AF.Ln)
                scalar.activation(P_(10), P_(10), AF.Exp, scale=0.5, bias=LNH)
                scalar.activation(P_(11), P_(9), AF.Ln)
                scalar.activation(P_(11), P_(11), AF.Exp, scale=0.5, bias=LNH)
                scalar.activation(P_(16), P_(0), AF.Sin, bias=HALF_PI)
                scalar.activation(P_(17), P_(2), AF.Sin).then_inc(sSc, 1)
            cSc.add()

            @block.vector
            def _(vector, c=c, t0=t0):
                vector.wait_ge(sSc, cSc.v)
                rv = _R(vector, "big")
                rv.tensor_mul(P_(2), P_(6), P_(16))
                rv.tensor_mul(P_(3), P_(6), P_(17))
                rv.tensor_mul(P_(16), P_(7), P_(16))
                rv.tensor_mul(P_(17), P_(7), P_(17))
                rv.tensor_mul(P_(6), P_(10), P_(16))
                rv.tensor_mul(P_(7), P_(11), P_(17))
                rv.tensor_mul(P_(9), P_(10), P_(17))
                rv.tensor_mul(P_(8), P_(11), P_(16))
                rv.tensor_sub(P_(6), P_(6), P_(7))
                rv.tensor_add(P_(7), P_(9), P_(8))
                rv.tensor_tensor(P3(6), gsl3(gx), P3(6), OP.mult)
                rv.tensor_tensor(P3(7), gsl3(gx), P3(7), OP.mult)
                rv.tensor_tensor(P3(10), gsl3(xh), P3(6), OP.mult)
                rv.tensor_tensor(P3(16), gsl3(xh, 8), P3(6), OP.mult)
                rv.tensor_tensor(P3(11), gsl3(xh, 8), P3(7), OP.mult)
                rv.tensor_tensor(P3(17), gsl3(xh), P3(7), OP.mult)
                rv.tensor_add(P_(10), P_(10), P_(11))
                rv.tensor_sub(P_(9), P_(16), P_(17))
                vector.drain()
                rv2 = _R(vector, "tts2")
                for r in range(GT):
                    for w in range(2):
                        a = r * CH + w * W
                        rv2.tensor_tensor_scan(
                            P_(4)[:, a:a + W], ones64[:, :], P_(10)[:, a:a + W],
                            0.0, OP.mult, OP.add)
                        rv2.tensor_tensor_scan(
                            P_(5)[:, a:a + W], ones64[:, :], P_(9)[:, a:a + W],
                            0.0, OP.mult, OP.add)
                vector.drain()
                rv3 = _R(vector, "ext")
                last = None
                for src, col in ((2, 0), (3, 1), (4, 2), (5, 3)):
                    sap = P3(src)[:, :, W - 1::W]
                    dap = summ.ap()[:, col * 128:(col + 1) * 128].rearrange(
                        "p (r w) -> p r w", w=16)[:, :, 2 * c:2 * c + 2]
                    last = rv3.tensor_copy(dap, sap)
                vector.drain()
                last.then_inc(sDv, 1)
            cDv.add()

            @block.gpsimd
            def _(gpsimd, c=c):
                gpsimd.wait_ge(sDv, cDv.v)
                for i, s in enumerate((4, 5, 2, 3)):
                    base = (c * 4 + i) * 128
                    gpsimd.dma_start(out=scanscr[base:base + 128, :],
                                     in_=P_(s)).then_inc((sScr, sScr1)[c % 2], 16)
            (cScr, cScr1)[c % 2].add(64)
            scr_mark.append((cScr, cScr1)[c % 2].v)
        scan_dv = cDv.v
        g_pool.__exit__(None, None, None)
        g_gx.__exit__(None, None, None)
        g_ga.__exit__(None, None, None)
        _FO = {0: (0, 512), 1: (512, 256), 2: (768, 40), 3: (808, 32),
               4: (840, 16), 5: (856, 256)}
        P_ = lambda s: fbuf[:, _FO[s][0]:_FO[s][0] + _FO[s][1]]

        if stop == 5:
            @block.sync
            def _(sync):
                sync.wait_ge(sDv, scan_dv)
                sync.wait_ge(sScr, cScr.v)
                sync.wait_ge(sScr1, cScr1.v)
            g_fb.__exit__(None, None, None)
            g_xh.__exit__(None, None, None)
            return nc

        g_ln1cm = nc.sbuf_tensor("ln1cm", [128, DT * TC], BF16, side="right")
        ln1cm = g_ln1cm.__enter__()
        g_wob = nc.sbuf_tensor("woutb", [128, DT * D], BF16, side="right")
        woutb = g_wob.__enter__()
        g_bob = nc.sbuf_tensor("b_out_r", [1, D], BF16, side="right")
        b_out_r = g_bob.__enter__()
        g_xres = nc.sbuf_tensor("xres", [128, 2 * D], F32, side="right")
        xres = g_xres.__enter__()

        @block.gpsimd
        def _(gpsimd):
            gpsimd.wait_ge(sDv, scan_dv)
            gpsimd.wait_ge(sScr, cScr.v)
            gpsimd.wait_ge(sScr1, cScr1.v)
            for k in range(DT):
                gpsimd.dma_start(out=woutb[:, k * D:(k + 1) * D],
                                 in_=w_out[k * 128:(k + 1) * 128, :]
                                 ).then_inc(sWo, 16)
            gpsimd.dma_start(out=b_out_r[:, :],
                             in_=b_out_bc[0:1, :]).then_inc(sBo, 16)
            gpsimd.dma_start(out=xres[:, 0:D],
                             in_=x_in[0:128, :]).then_inc(sOsc, 16)
            gpsimd.dma_start(out=xres[:, D:2 * D],
                             in_=x_in[128:256, :]).then_inc(sOsc1, 16)
        wout_ld = cWo.add(16 * DT)
        cOsc.add(16)
        cOsc1.add(16)

        def wsl(buf, base, w):
            return buf.ap()[:, base:base + 128].rearrange(
                "p (r w) -> p r w", w=16)[:, :, w:w + 1]

        @block.vector
        def _(vector):
            vector.wait_ge(sDv, scan_dv)
            vector.wait_ge(sScr, cScr.v)
            vector.wait_ge(sScr1, cScr1.v)
            vector.tensor_mul(P_(0)[:, 0:128], summ[:, 0:128], summ[:, 256:384])
            vector.tensor_mul(P_(0)[:, 128:256], summ[:, 128:256], summ[:, 384:512])
            vector.tensor_mul(P_(0)[:, 256:384], summ[:, 0:128], summ[:, 384:512])
            vector.tensor_mul(P_(0)[:, 384:512], summ[:, 128:256], summ[:, 256:384])
            vector.tensor_sub(summ[:, 256:384], P_(0)[:, 0:128], P_(0)[:, 128:256])
            vector.tensor_add(summ[:, 384:512], P_(0)[:, 256:384], P_(0)[:, 384:512])
            vector.memset(hinb[:, :], 0.0)
            vector.memset(P_(1)[:, 0:128], 1.0)
            vector.memset(P_(1)[:, 128:256], 0.0)
            t0_ = P_(2)[:, 0:8].rearrange("p (r w) -> p r w", w=1)
            t1_ = P_(2)[:, 8:16].rearrange("p (r w) -> p r w", w=1)
            for w in range(1, NW):
                ArP, AiP = wsl(summ, 0, w - 1), wsl(summ, 128, w - 1)
                BrP, BiP = wsl(summ, 256, w - 1), wsl(summ, 384, w - 1)
                HrP, HiP = wsl(hinb, 0, w - 1), wsl(hinb, 128, w - 1)
                Hr, Hi = wsl(hinb, 0, w), wsl(hinb, 128, w)
                cArP, cAiP = wsl(fbuf, 512, w - 1), wsl(fbuf, 640, w - 1)
                cAr, cAi = wsl(fbuf, 512, w), wsl(fbuf, 640, w)
                vector.tensor_tensor(t0_, ArP, HrP, OP.mult)
                vector.tensor_tensor(t1_, AiP, HiP, OP.mult)
                vector.tensor_tensor(t0_, t0_, t1_, OP.subtract)
                vector.tensor_tensor(Hr, t0_, BrP, OP.add)
                vector.tensor_tensor(t0_, ArP, HiP, OP.mult)
                vector.tensor_tensor(t1_, AiP, HrP, OP.mult)
                vector.tensor_tensor(t0_, t0_, t1_, OP.add)
                vector.tensor_tensor(Hi, t0_, BiP, OP.add)
                vector.tensor_tensor(t0_, cArP, ArP, OP.mult)
                vector.tensor_tensor(t1_, cAiP, AiP, OP.mult)
                vector.tensor_tensor(cAr, t0_, t1_, OP.subtract)
                vector.tensor_tensor(t0_, cArP, AiP, OP.mult)
                vector.tensor_tensor(t1_, cAiP, ArP, OP.mult)
                vector.tensor_tensor(cAi, t0_, t1_, OP.add)
            ArL, AiL = wsl(summ, 0, NW - 1), wsl(summ, 128, NW - 1)
            BrL, BiL = wsl(summ, 256, NW - 1), wsl(summ, 384, NW - 1)
            HrL, HiL = wsl(hinb, 0, NW - 1), wsl(hinb, 128, NW - 1)
            cArL, cAiL = wsl(fbuf, 512, NW - 1), wsl(fbuf, 640, NW - 1)
            pk = lambda a, b: P_(3)[:, a:b].rearrange("p (r w) -> p r w", w=1)
            vector.tensor_tensor(t0_, cArL, ArL, OP.mult)
            vector.tensor_tensor(t1_, cAiL, AiL, OP.mult)
            vector.tensor_tensor(pk(0, 8), t0_, t1_, OP.subtract)
            vector.tensor_tensor(t0_, cArL, AiL, OP.mult)
            vector.tensor_tensor(t1_, cAiL, ArL, OP.mult)
            vector.tensor_tensor(pk(8, 16), t0_, t1_, OP.add)
            vector.tensor_tensor(t0_, ArL, HrL, OP.mult)
            vector.tensor_tensor(t1_, AiL, HiL, OP.mult)
            vector.tensor_tensor(t0_, t0_, t1_, OP.subtract)
            vector.tensor_tensor(pk(16, 24), t0_, BrL, OP.add)
            vector.tensor_tensor(t0_, ArL, HiL, OP.mult)
            vector.tensor_tensor(t1_, AiL, HrL, OP.mult)
            vector.tensor_tensor(t0_, t0_, t1_, OP.add)
            vector.tensor_tensor(pk(24, 32), t0_, BiL, OP.add).then_inc(sDv, 1)
        summ_rdy = cDv.add()

        @block.gpsimd
        def _(gpsimd):
            gpsimd.wait_ge(sDv, summ_rdy)
            gpsimd.dma_start(out=cc_in, in_=P_(3)[:, 0:32]).then_inc(sCgI, 16)
            gpsimd.wait_ge(sCgI, cCgI.add(16))
            gpsimd.collective_compute(
                "AllGather", OP.bypass,
                replica_groups=[list(range(NCORES))],
                ins=[cc_in], outs=[cc_out],
            ).then_inc(sCc, 1)
            gpsimd.wait_ge(sCc, cCc.add())
            last = None
            for j in range(NCORES):
                last = gpsimd.dma_start(out=sgath[:, 32 * j:32 * (j + 1)],
                                        in_=cc_out[128 * j:128 * (j + 1), :])
                last.then_inc(sCgI, 16)
            cCgI.add(16 * NCORES)
        carry_ld = cCgI.v

        @block.vector
        def _(vector):
            vector.wait_ge(sCgI, carry_ld)
            vector.memset(P_(4)[:, 0:16], 0.0)
            hr = P_(4)[:, 0:8].rearrange("p (r w) -> p r w", w=1)
            hi = P_(4)[:, 8:16].rearrange("p (r w) -> p r w", w=1)
            t0_ = P_(2)[:, 0:8].rearrange("p (r w) -> p r w", w=1)
            t1_ = P_(2)[:, 8:16].rearrange("p (r w) -> p r w", w=1)
            t2_ = P_(2)[:, 16:24].rearrange("p (r w) -> p r w", w=1)
            t3_ = P_(2)[:, 24:32].rearrange("p (r w) -> p r w", w=1)
            t4_ = P_(2)[:, 32:40].rearrange("p (r w) -> p r w", w=1)
            for j in range(NCORES):
                g = lambda a, b: sgath.ap()[:, 32 * j + a:32 * j + b].rearrange(
                    "p (r w) -> p r w", w=1)
                m = consts[:, 48 + j:49 + j]
                vector.tensor_scalar(t2_, g(0, 8), -1.0, None, OP.add)
                vector.tensor_scalar(t2_, t2_, m, 1.0, OP.mult, OP.add)
                vector.tensor_scalar(t3_, g(8, 16), m, None, OP.mult)
                vector.tensor_tensor(t0_, t2_, hr, OP.mult)
                vector.tensor_tensor(t1_, t3_, hi, OP.mult)
                vector.tensor_tensor(t0_, t0_, t1_, OP.subtract)
                vector.tensor_tensor(t1_, t2_, hi, OP.mult)
                vector.tensor_tensor(t4_, t3_, hr, OP.mult)
                vector.tensor_tensor(hi, t1_, t4_, OP.add)
                vector.tensor_scalar(t1_, g(16, 24), m, None, OP.mult)
                vector.tensor_tensor(hr, t0_, t1_, OP.add)
                vector.tensor_scalar(t1_, g(24, 32), m, None, OP.mult)
                vector.tensor_tensor(hi, hi, t1_, OP.add)
            hcr = P_(4)[:, 0:8][:, :, None].broadcast_to([128, 8, 16])
            hci = P_(4)[:, 8:16][:, :, None].broadcast_to([128, 8, 16])
            cAr = P_(1)[:, 0:128].rearrange("p (r w) -> p r w", w=16)
            cAi = P_(1)[:, 128:256].rearrange("p (r w) -> p r w", w=16)
            hR = hinb.ap()[:, 0:128].rearrange("p (r w) -> p r w", w=16)
            hI = hinb.ap()[:, 128:256].rearrange("p (r w) -> p r w", w=16)
            p5a = P_(5)[:, 0:128].rearrange("p (r w) -> p r w", w=16)
            p5b = P_(5)[:, 128:256].rearrange("p (r w) -> p r w", w=16)
            vector.tensor_tensor(p5a, cAr, hcr, OP.mult)
            vector.tensor_tensor(p5b, cAi, hci, OP.mult)
            vector.tensor_tensor(p5a, p5a, p5b, OP.subtract)
            vector.tensor_tensor(hR, hR, p5a, OP.add)
            vector.tensor_tensor(p5a, cAr, hci, OP.mult)
            vector.tensor_tensor(p5b, cAi, hcr, OP.mult)
            vector.tensor_tensor(p5a, p5a, p5b, OP.add)
            vector.tensor_tensor(hI, hI, p5a, OP.add).then_inc(sDv, 1)
        hin_rdy = cDv.add()

        if stop == 6:
            @block.sync
            def _(sync):
                sync.wait_ge(sDv, hin_rdy)
            g_fb.__exit__(None, None, None)
            g_xh.__exit__(None, None, None)
            return nc

        if debug:
            @block.sync
            def _(sync):
                sync.wait_ge(sDv, hin_rdy)
                sync.dma_start(out=dbg["dbg_hin"], in_=hinb[:, :]).then_inc(sSt, 16)
                sync.dma_start(out=dbg["dbg_summ"], in_=summ[:, :]).then_inc(sSt, 16)
                sync.dma_start(out=dbg["dbg_scr"], in_=scanscr).then_inc(sSt, 16)
            cSt.add(48)

        g_fb.__exit__(None, None, None)
        g_xh.__exit__(None, None, None)

        g_hb = nc.sbuf_tensor("hb", [128, DT * TC], BF16, side="right")
        hb = g_hb.__enter__()
        g_corr = nc.sbuf_tensor("corr", [128, 10 * 1024], F32, side="right")
        corr = g_corr.__enter__()
        C_ = lambda s: corr[:, s * 1024:(s + 1) * 1024]
        C3 = lambda s, t=CH: corr.ap()[:, s * 1024:(s + 1) * 1024].rearrange(
            "p (r t) -> p r t", t=t)

        g_mark = []
        for c in range(NCH):
            b = 2 + (c % 2) * 4
            sCo = (sCo0, sCo1)[c % 2]

            @block.gpsimd
            def _(gpsimd, c=c, b=b, sCo=sCo):
                gpsimd.wait_ge(sDv, hin_rdy)
                if c == 0:
                    gpsimd.wait_ge(sWsT, cWsT.v)
                    gpsimd.wait_ge(sWsP, cWsP.v)
                if c >= 2:
                    gpsimd.wait_ge(sDv, cDv.v)
                for i in range(4):
                    base = (c * 4 + i) * 128
                    gpsimd.dma_start(out=C_(b + i),
                                     in_=scanscr[base:base + 128, :]
                                     ).then_inc(sCo, 16)
            co_ld = cCo[c % 2].add(64)

            @block.vector
            def _(vector, c=c, b=b, sCo=sCo, co_ld=co_ld):
                vector.wait_ge(sCo, co_ld)
                Sr, Si, cP, sP = b, b + 1, b + 2, b + 3
                r4 = lambda s: C3(s).rearrange("p r (w t) -> p r w t", t=W)
                hR = hinb.ap()[:, 0:128].rearrange("p (r w) -> p r w", w=16)[
                    :, :, 2 * c:2 * c + 2][:, :, :, None].broadcast_to(
                    [128, 8, 2, W])
                hI = hinb.ap()[:, 128:256].rearrange("p (r w) -> p r w", w=16)[
                    :, :, 2 * c:2 * c + 2][:, :, :, None].broadcast_to(
                    [128, 8, 2, W])
                vector.tensor_tensor(r4(Sr), r4(Sr), hR, OP.add)
                vector.tensor_tensor(r4(Si), r4(Si), hI, OP.add)
                vector.tensor_mul(C_(0), C_(cP), C_(Sr))
                vector.tensor_mul(C_(1), C_(sP), C_(Si))
                hre = hb.ap().rearrange("p (r t) -> p r t", t=TC)[
                    :, 0:8, c * CH:(c + 1) * CH]
                vector.tensor_tensor(hre, C3(0), C3(1), OP.subtract)
                vector.tensor_mul(C_(0), C_(cP), C_(Si))
                vector.tensor_mul(C_(1), C_(sP), C_(Sr))
                him = hb.ap().rearrange("p (r t) -> p r t", t=TC)[
                    :, 8:16, c * CH:(c + 1) * CH]
                vector.tensor_tensor(him, C3(0), C3(1), OP.add).then_inc(sDv, 1)
            g_mark.append(cDv.add())
        h_rdy = cDv.v
        g_corr.__exit__(None, None, None)

        if stop == 7:
            @block.sync
            def _(sync):
                sync.wait_ge(sDv, h_rdy)
            return nc

        if debug:
            @block.sync
            def _(sync):
                sync.wait_ge(sDv, h_rdy)
                sync.dma_start(out=dbg["dbg_h"], in_=hb[:, :]).then_inc(sSt, 16)
            cSt.add(16)

        with contextlib.ExitStack() as hctx:
            Eh = hctx.enter_context
            r1b = Eh(nc.sbuf_tensor("r1b", [128, 2 * D], F32))
            ln1b = Eh(nc.sbuf_tensor("ln1b", [128, D], F32))
            ln1bb = Eh(nc.sbuf_tensor("ln1bb", [128, 2 * D], BF16))
            sq_s = Eh(nc.sbuf_tensor("sq_s", [128, D], BF16))
            stat = Eh(nc.sbuf_tensor("stat", [128, 8], F32))
            pH = Eh(nc.psum_tensor("pH", [128, 1024], F32))
            pT = Eh(nc.psum_tensor("pT", [128, 512], BF16))

            cast_mark = []
            tp_mark = {}
            tp_ev = []

            def emit_ln1_tp(tt):
                for grp in range(4):
                    @block.tensor
                    def _(tensor, tt=tt, grp=grp,
                          ev=(tp_ev[-1] if tp_ev else None)):
                        tensor.wait_ge(sLbb, cast_mark[tt])
                        if ev is not None:
                            tensor.wait_ge(sDv, ev)
                        last = None
                        for j in range(4):
                            k = grp * 4 + j
                            last = tensor.transpose(
                                pT[:, j * 128:(j + 1) * 128],
                                ln1bb[:, (tt % 2) * D + k * 128:
                                      (tt % 2) * D + (k + 1) * 128],
                                ident[:, :])
                        last.then_inc(sPe, 1)
                    cPe.add()

                    @block.vector
                    def _(vector, tt=tt, grp=grp):
                        vector.wait_ge(sPe, cPe.v)
                        dst = ln1cm.ap().rearrange("p (j t) -> p j t", t=TC)[
                            :, grp * 4:(grp + 1) * 4, tt * 128:(tt + 1) * 128]
                        src = pT.ap().rearrange("p (j t) -> p j t", t=128)
                        vector.tensor_copy(dst, src).then_inc(sDv, 1)
                    tp_ev.append(cDv.add())
                tp_mark[tt] = cPe.v

            nH = 0
            for tt in range(NTT):
                bo = (tt % 2) * D

                sXr = (sXr0, sXr1)[tt % 2]

                if tt >= 2:
                    @block.sync
                    def _(sync, tt=tt, bo=bo, sXr=sXr):
                        sync.wait_ge(sDv, cDv.v)
                        sync.dma_start(out=xres[:, bo:bo + D],
                                       in_=x_in[tt * 128:(tt + 1) * 128, :]
                                       ).then_inc(sXr, 16)
                    cXr[tt % 2].add(16)
                xr_ld = cXr[tt % 2].v

                for cs in range(4):
                    po = (nH % 2) * 512

                    @block.tensor
                    def _(tensor, tt=tt, cs=cs, po=po, nH=nH):
                        tensor.wait_ge(sDv, g_mark[tt])
                        tensor.wait_ge(sWo, wout_ld)
                        if tt == 0 and cs == 0:
                            tensor.wait_ge(sBo, 16)
                        if nH >= 2:
                            tensor.wait_ge(sDv, cDv.v - 1)
                        tensor.matmul(pH[:, po:po + 512], ones_row[0:1, :],
                                      b_out_r[0:1, cs * 512:(cs + 1) * 512],
                                      start=True, stop=False)
                        last = None
                        for k in range(DT):
                            last = tensor.matmul(
                                pH[:, po:po + 512],
                                hb[:, k * TC + tt * 128: k * TC + (tt + 1) * 128],
                                woutb[:, k * D + cs * 512: k * D + (cs + 1) * 512],
                                start=False, stop=(k == DT - 1))
                        last.then_inc(sPe, 1)
                    cPe.add()

                    @block.vector
                    def _(vector, cs=cs, po=po, bo=bo, tt=tt, xr_ld=xr_ld,
                          sXr=sXr):
                        vector.wait_ge(sPe, cPe.v)
                        if cs == 0:
                            if tt < 2:
                                vector.wait_ge((sOsc, sOsc1)[tt], 16)
                            else:
                                vector.wait_ge(sXr, xr_ld)
                        vector.scalar_tensor_tensor(
                            r1b[:, bo + cs * 512: bo + (cs + 1) * 512],
                            pH[:, po:po + 512], 1.0,
                            xres[:, bo + cs * 512: bo + (cs + 1) * 512],
                            OP.mult, OP.add).then_inc(sDv, 1)
                    cDv.add()
                    nH += 1

                @block.vector
                def _(vector, bo=bo):
                    vector.wait_ge(sDv, cDv.v)
                    vector.tensor_reduce(stat[:, 0:1], r1b[:, bo:bo + D], AX,
                                         OP.add).then_inc(sDv, 1)
                cDv.add()

                @block.scalar
                def _(scalar, bo=bo):
                    scalar.wait_ge(sDv, cDv.v)
                    scalar.activation(sq_s[:, :], r1b[:, bo:bo + D], AF.Square,
                                      accum_out=stat[:, 1:2]).then_inc(sSc, 1)
                cSc.add()

                @block.vector
                def _(vector):
                    vector.wait_ge(sSc, cSc.v)
                    vector.tensor_scalar_mul(stat[:, 2:3], stat[:, 0:1], 1.0 / D)
                    vector.tensor_scalar_mul(stat[:, 3:4], stat[:, 1:2], 1.0 / D)
                    vector.tensor_mul(stat[:, 4:5], stat[:, 2:3], stat[:, 2:3])
                    vector.tensor_sub(stat[:, 3:4], stat[:, 3:4], stat[:, 4:5])
                    vector.tensor_scalar_add(stat[:, 3:4], stat[:, 3:4], LN_EPS
                                             ).then_inc(sDv, 1)
                cDv.add()

                @block.scalar
                def _(scalar):
                    scalar.wait_ge(sDv, cDv.v)
                    scalar.activation(stat[:, 5:6], stat[:, 3:4], AF.Ln)
                    scalar.activation(stat[:, 5:6], stat[:, 5:6], AF.Exp,
                                      scale=-0.5).then_inc(sSc, 1)
                cSc.add()

                @block.vector
                def _(vector, bo=bo):
                    vector.wait_ge(sSc, cSc.v)
                    vector.wait_ge(sL1s, cL1s.v)
                    vector.wait_ge(sLbb, cLbb.v)
                    vector.tensor_scalar(ln1b[:, :], r1b[:, bo:bo + D],
                                         stat[:, 2:3], stat[:, 5:6],
                                         OP.subtract, OP.mult).then_inc(sDv, 1)
                cDv.add()
                ln1_rdy = cDv.v

                @block.sync
                def _(sync, tt=tt, ln1_rdy=ln1_rdy):
                    sync.wait_ge(sDv, ln1_rdy)
                    sync.dma_start(out=ln1scr[tt * 128:(tt + 1) * 128, :],
                                   in_=ln1b[:, :]).then_inc(sL1s, 16)
                    if debug:
                        sync.dma_start(out=dbg["dbg_ln1"][tt * 128:(tt + 1) * 128, :],
                                       in_=ln1b[:, :]).then_inc(sSt, 16)
                cL1s.add(16)
                if debug:
                    cSt.add(16)

                @block.gpsimd
                def _(gpsimd, tt=tt, ln1_rdy=ln1_rdy,
                      tm=(tp_mark[tt - 2] if tt >= 2 else 0)):
                    gpsimd.wait_ge(sDv, ln1_rdy)
                    if tt >= 2:
                        gpsimd.wait_ge(sPe, tm)
                    gpsimd.dma_start(out=ln1bb[:, (tt % 2) * D:(tt % 2 + 1) * D],
                                     in_=ln1b[:, :]).then_inc(sLbb, 16)
                cast_mark.append(cLbb.add(16))

                if tt > 0:
                    emit_ln1_tp(tt - 1)
            emit_ln1_tp(NTT - 1)
            ln1_all = cDv.v

        if stop == 8:
            @block.sync
            def _(sync):
                sync.wait_ge(sDv, ln1_all)
                sync.wait_ge(sLd, cLd.v)
            return nc

        g_hb.__exit__(None, None, None)
        g_xres.__exit__(None, None, None)
        g_bob.__exit__(None, None, None)
        g_wob.__exit__(None, None, None)
        with contextlib.ExitStack() as ictx:
            Ei = ictx.enter_context
            hhres = Ei(nc.sbuf_tensor("hhres", [128, NPAIR * TC], BF16))
            silu_s = Ei(nc.sbuf_tensor("silu_s", [128, TC], F32))
            sqh = Ei(nc.sbuf_tensor("sqh", [128, 2 * TC], BF16))
            ssrow = Ei(nc.sbuf_tensor("ssrow", [128, TC], F32))
            sscol = Ei(nc.sbuf_tensor("sscol", [128, 16], F32))
            ssb = Ei(nc.sbuf_tensor("ssb", [128, TC], BF16))
            gwctx = contextlib.ExitStack()
            gw = gwctx.enter_context(nc.sbuf_tensor("gw", [128, 2 * D], BF16))
            uw = gwctx.enter_context(nc.sbuf_tensor("uw", [128, 2 * D], BF16))
            gctx = contextlib.ExitStack()
            Eg = gctx.enter_context
            pG = Eg(nc.psum_tensor("pG", [128, 1024], F32))
            pU = Eg(nc.psum_tensor("pU", [128, 1024], F32))
            pS = Eg(nc.psum_tensor("pS", [1, 1024], F32))
            pR = Eg(nc.psum_tensor("pR", [128, 128], BF16))

            mm_mark = {}
            sq_mark = {}
            for i in range(NPAIR):
                bo = (i % 2) * D
                ho = (i % 2) * TC
                sGu = (sGu0, sGu1)[i % 2]

                @block.sync
                def _(sync, i=i, bo=bo, sGu=sGu,
                      mw=(mm_mark[i - 2] if i >= 2 else 0)):
                    sync.wait_ge(sCc, wq_ag1)
                    if i >= 2:
                        sync.wait_ge(sPe, mw)
                    sync.dma_start(out=gw[:, bo:bo + D],
                                   in_=wg_tile(i % 8, 2 * (i // 8))
                                   ).then_inc(sGu, 16)
                    sync.dma_start(out=uw[:, bo:bo + D],
                                   in_=wg_tile(i % 8, 2 * (i // 8) + 1)
                                   ).then_inc(sGu, 16)
                gu_ld = cGu[i % 2].add(32)

                for ts_ in range(2):
                    po = ts_ * 512

                    @block.tensor
                    def _(tensor, i=i, ts_=ts_, po=po, bo=bo, gu_ld=gu_ld,
                          sGu=sGu):
                        tensor.wait_ge(sGu, gu_ld)
                        tensor.wait_ge(sDv, ln1_all)
                        if i > 0:
                            tensor.wait_ge(sDv, cDv.v - 1)
                        lastg = None
                        for k in range(DT):
                            tensor.matmul(
                                pG[:, po:po + 512],
                                gw[:, bo + k * 128: bo + (k + 1) * 128],
                                ln1cm[:, k * TC + ts_ * 512:
                                      k * TC + ts_ * 512 + 512],
                                start=(k == 0), stop=(k == DT - 1))
                        for k in range(DT):
                            lastg = tensor.matmul(
                                pU[:, po:po + 512],
                                uw[:, bo + k * 128: bo + (k + 1) * 128],
                                ln1cm[:, k * TC + ts_ * 512:
                                      k * TC + ts_ * 512 + 512],
                                start=(k == 0), stop=(k == DT - 1))
                        lastg.then_inc(sPe, 1)
                    cPe.add()

                    @block.scalar
                    def _(scalar, ts_=ts_, po=po):
                        scalar.wait_ge(sPe, cPe.v)
                        scalar.activation(silu_s[:, po:po + 512],
                                          pG[:, po:po + 512], AF.Silu
                                          ).then_inc(sSc, 1)
                    cSc.add()

                    @block.vector
                    def _(vector, i=i, ts_=ts_, po=po):
                        vector.wait_ge(sSc, cSc.v)
                        if i == 0 and ts_ == 0:
                            vector.wait_ge(sL1s, cL1s.v)
                            vector.wait_ge(sLbb, cLbb.v)
                            if debug:
                                vector.wait_ge(sSt, cSt.v)
                        vector.tensor_mul(hhres[:, i * TC + po: i * TC + po + 512],
                                          silu_s[:, po:po + 512],
                                          pU[:, po:po + 512]).then_inc(sDv, 1)
                    cDv.add()

                    @block.scalar
                    def _(scalar, i=i, ts_=ts_, po=po):
                        scalar.wait_ge(sDv, cDv.v)
                        scalar.activation(sqh[:, ((i % 2) * 2 + ts_) * 512:
                                              ((i % 2) * 2 + ts_) * 512 + 512],
                                          hhres[:, i * TC + po: i * TC + po + 512],
                                          AF.Square).then_inc(sSc, 1)
                    cSc.add()
                mm_mark[i] = cPe.v
                sq_mark[i] = cSc.v

                if i > 0:
                    @block.tensor
                    def _(tensor, i=i, sw=sq_mark[i - 1]):
                        tensor.wait_ge(sSc, sw)
                        for ts_ in range(2):
                            tensor.matmul(pS[:, ts_ * 512:(ts_ + 1) * 512],
                                          onesb[:, :],
                                          sqh[:, (((i - 1) % 2) * 2 + ts_) * 512:
                                              (((i - 1) % 2) * 2 + ts_) * 512 + 512],
                                          start=(i - 1 == 0), stop=False,
                                          skip_group_check=True).then_inc(sPe, 1)
                            cPe.add()

            @block.tensor
            def _(tensor):
                tensor.wait_ge(sSc, cSc.v)
                last = None
                for ts_ in range(2):
                    last = tensor.matmul(
                        pS[:, ts_ * 512:(ts_ + 1) * 512], onesb[:, :],
                        sqh[:, (((NPAIR - 1) % 2) * 2 + ts_) * 512:
                            (((NPAIR - 1) % 2) * 2 + ts_) * 512 + 512],
                        start=False, stop=True, skip_group_check=True)
                last.then_inc(sPe, 1)
            cPe.add()
            gu_done_pe = cPe.v

            @block.vector
            def _(vector):
                vector.wait_ge(sPe, gu_done_pe)
                vector.memset(ssrow[:, :], 0.0)
                vector.tensor_copy(ssrow[0:1, :], pS[:, :])
                vector.tensor_copy(ssb[:, :], ssrow[:, :]).then_inc(sDv, 1)
            cDv.add()
            ss_rdy = cDv.v

            for tt in range(NTT):
                @block.tensor
                def _(tensor, tt=tt, ss_rdy=ss_rdy):
                    tensor.wait_ge(sDv, ss_rdy)
                    if tt > 0:
                        tensor.wait_ge(sDv, cDv.v)
                    tensor.transpose(pR[:, :], ssb[:, tt * 128:(tt + 1) * 128],
                                     ident[:, :]).then_inc(sPe, 1)
                cPe.add()

                @block.vector
                def _(vector, tt=tt):
                    vector.wait_ge(sPe, cPe.v)
                    vector.tensor_copy(sscol[:, tt:tt + 1], pR[:, 0:1]
                                       ).then_inc(sDv, 1)
                cDv.add()

            @block.vector
            def _(vector):
                vector.wait_ge(sDv, cDv.v)
                vector.tensor_scalar(sscol[:, 0:8], sscol[:, 0:8], 1.0 / H,
                                     RMS_EPS, OP.mult, OP.add).then_inc(sDv, 1)
            cDv.add()

            @block.scalar
            def _(scalar):
                scalar.wait_ge(sDv, cDv.v)
                scalar.activation(sscol[:, 8:16], sscol[:, 0:8], AF.Ln)
                scalar.activation(sscol[:, 8:16], sscol[:, 8:16], AF.Exp,
                                  scale=-0.5).then_inc(sSc, 1)
            cSc.add()
            inv_rdy = cSc.v

            if stop == 9:
                @block.sync
                def _(sync):
                    sync.wait_ge(sSc, inv_rdy)
                return nc

            if debug:
                @block.sync
                def _(sync):
                    sync.wait_ge(sSc, inv_rdy)
                    sync.dma_start(out=dbg["dbg_inv"], in_=sscol[:, :]
                                   ).then_inc(sSt, 16)
                cSt.add(16)

            gctx.close()
            gwctx.close()
            g_ln1cm.__exit__(None, None, None)
            outpre = Ei(nc.sbuf_tensor("outpre", [128, NTT * D], BF16))
            sq2 = Ei(nc.sbuf_tensor("sq2", [128, 512], BF16))
            st2 = Ei(nc.sbuf_tensor("st2", [128, 48], F32))
            pwlctx = contextlib.ExitStack()
            Epw = pwlctx.enter_context
            pwb = Epw(nc.sbuf_tensor("pwb", [128, 4 * 512], BF16))
            lnrb = Epw(nc.sbuf_tensor("lnrb", [128, NTT * 512], F32))
            pP = Ei(nc.psum_tensor("pP", [128, 4096], F32))

            for cs in range(4):
                @block.scalar
                def _(scalar, cs=cs):
                    if cs == 0:
                        scalar.wait_ge(sL1s, cL1s.v)
                        scalar.wait_ge(sPe, gu_done_pe)
                    else:
                        scalar.wait_ge(sDv, cDv.v)
                    for tt in range(NTT):
                        scalar.dma_start(
                            out=lnrb[:, tt * 512:(tt + 1) * 512],
                            in_=ln1scr[tt * 128:(tt + 1) * 128,
                                       cs * 512:(cs + 1) * 512]
                        ).then_inc(sLnr, 16)
                lnr_all = cLnr.add(16 * NTT)

                for k in range(NPK):
                    po = (k % 4) * 512
                    sPj = (sPj0, sPj1, sPj2, sPj3)[k % 4]

                    @block.sync
                    def _(sync, cs=cs, k=k, po=po, sPj=sPj):
                        if cs == 0 and k == 0:
                            sync.wait_ge(sCc, wq_ag)
                            sync.wait_ge(sL1s, cL1s.v)
                            sync.wait_ge(sPe, gu_done_pe)
                        if k >= 4 or cs > 0:
                            sync.wait_ge(sPe, cPe.v - 3)
                        src = wg_tile(k % 8, 16 + k // 8)
                        sync.dma_start(out=pwb[:, po:po + 512],
                                       in_=src[:, cs * 512:(cs + 1) * 512]
                                       ).then_inc(sPj, 16)
                    pj_ld = cPj[k % 4].add(16)

                    @block.tensor
                    def _(tensor, cs=cs, k=k, po=po, pj_ld=pj_ld,
                          sPj=sPj):
                        tensor.wait_ge(sPj, pj_ld)
                        if k == 0 and cs == 0:
                            tensor.wait_ge(sSc, inv_rdy)
                        if k == 0 and cs > 0:
                            tensor.wait_ge(sDv, cDv.v)
                        last = None
                        for tt in range(NTT):
                            last = tensor.matmul(
                                pP[:, tt * 512:(tt + 1) * 512],
                                hhres[:, k * TC + tt * 128: k * TC + (tt + 1) * 128],
                                pwb[:, po:po + 512],
                                start=(k == 0), stop=(k == NPK - 1),
                                skip_group_check=True)
                        last.then_inc(sPe, 1)
                    cPe.add()

                for tt in range(NTT):
                    osl = outpre[:, tt * D + cs * 512: tt * D + (cs + 1) * 512]

                    @block.vector
                    def _(vector, cs=cs, tt=tt, osl=osl, lnr_all=lnr_all):
                        vector.wait_ge(sLnr, lnr_all)
                        vector.wait_ge(sPe, cPe.v)
                        vector.wait_ge(sSc, inv_rdy)
                        vector.scalar_tensor_tensor(
                            osl, pP[:, tt * 512:(tt + 1) * 512],
                            sscol[:, 8 + tt:9 + tt],
                            lnrb[:, tt * 512:(tt + 1) * 512],
                            OP.mult, OP.add).then_inc(sDv, 1)
                    stt_mark = cDv.add()

                    @block.vector
                    def _(vector, cs=cs, tt=tt, osl=osl):
                        if cs == 0:
                            vector.tensor_reduce(st2[:, tt:tt + 1], osl, AX,
                                                 OP.add).then_inc(sDv, 1)
                        else:
                            vector.tensor_reduce(st2[:, 16:17], osl, AX,
                                                 OP.add)
                            vector.tensor_add(st2[:, tt:tt + 1],
                                              st2[:, tt:tt + 1],
                                              st2[:, 16:17]).then_inc(sDv, 1)
                    cDv.add()

                    @block.scalar
                    def _(scalar, cs=cs, tt=tt, osl=osl, m=stt_mark):
                        scalar.wait_ge(sDv, m)
                        ac = st2[:, 8 + tt:9 + tt] if cs == 0 else st2[:, 17:18]
                        scalar.activation(sq2[:, :], osl, AF.Square,
                                          accum_out=ac).then_inc(sSc, 1)
                    sq_mk = cSc.add()

                    if cs > 0:
                        @block.vector
                        def _(vector, tt=tt, m=sq_mk):
                            vector.wait_ge(sSc, m)
                            vector.tensor_add(st2[:, 8 + tt:9 + tt],
                                              st2[:, 8 + tt:9 + tt],
                                              st2[:, 17:18]).then_inc(sDv, 1)
                        cDv.add()
            proj_done = cDv.v

            if stop == 10:
                @block.sync
                def _(sync):
                    sync.wait_ge(sDv, proj_done)
                return nc

            pwlctx.close()
            ln2b = Ei(nc.sbuf_tensor("ln2b", [128, 2 * D], F32))

            @block.vector
            def _(vector):
                vector.wait_ge(sSc, cSc.v)
                vector.tensor_scalar_mul(st2[:, 24:32], st2[:, 0:8], 1.0 / D)
                vector.tensor_scalar_mul(st2[:, 32:40], st2[:, 8:16], 1.0 / D)
                vector.tensor_mul(st2[:, 16:24], st2[:, 24:32], st2[:, 24:32])
                vector.tensor_sub(st2[:, 32:40], st2[:, 32:40], st2[:, 16:24])
                vector.tensor_scalar_add(st2[:, 32:40], st2[:, 32:40], LN_EPS
                                         ).then_inc(sDv, 1)
            cDv.add()

            @block.scalar
            def _(scalar):
                scalar.wait_ge(sDv, cDv.v)
                scalar.activation(st2[:, 40:48], st2[:, 32:40], AF.Ln)
                scalar.activation(st2[:, 40:48], st2[:, 40:48], AF.Exp,
                                  scale=-0.5).then_inc(sSc, 1)
            cSc.add()
            j_inv = cSc.v

            for tt in range(NTT):
                bo = (tt % 2) * D
                src = outpre[:, tt * D:(tt + 1) * D]

                @block.vector
                def _(vector, tt=tt, bo=bo, src=src):
                    vector.wait_ge(sSc, j_inv)
                    if tt >= 2:
                        vector.wait_ge((sJl0, sJl1)[tt % 2], cJl[tt % 2].v)
                    vector.tensor_scalar(ln2b[:, bo:bo + D], src,
                                         st2[:, 24 + tt:25 + tt],
                                         st2[:, 40 + tt:41 + tt],
                                         OP.subtract, OP.mult).then_inc(sDv, 1)
                cDv.add()

                @block.sync
                def _(sync, tt=tt, bo=bo):
                    sync.wait_ge(sDv, cDv.v)
                    sync.dma_start(out=out_t[tt * 128:(tt + 1) * 128, :],
                                   in_=ln2b[:, bo:bo + D]
                                   ).then_inc((sJl0, sJl1)[tt % 2], 16)
                cJl[tt % 2].add(16)

            @block.sync
            def _(sync):
                sync.wait_ge(sJl0, cJl[0].v)
                sync.wait_ge(sJl1, cJl[1].v)
                sync.wait_ge(sSt, cSt.v)

    return nc


def _stage_inputs(inputs):
    f32 = np.float32
    x = np.ascontiguousarray(np.asarray(inputs["L0_attn_x"], f32).reshape(8192, D))
    W_in = np.ascontiguousarray(np.asarray(inputs["L0_attn_W_in"], f32))
    b_in = np.asarray(inputs["L0_attn_b_in"], f32)
    W_gx = np.ascontiguousarray(np.asarray(inputs["L0_attn_W_gx"], f32)
                                .reshape(GT, 2, 128, 128))
    W_ga = np.ascontiguousarray(np.asarray(inputs["L0_attn_W_ga"], f32)
                                .reshape(GT, 2, 128, 128))
    b_gx = np.asarray(inputs["L0_attn_b_gx"], f32)
    b_ga = np.asarray(inputs["L0_attn_b_ga"], f32)
    a_real = np.asarray(inputs["L0_attn_a_real"], f32)
    a_imag = np.asarray(inputs["L0_attn_a_imag"], f32)
    W_out = np.ascontiguousarray(np.asarray(inputs["L0_attn_W_out"], f32))
    b_out = np.asarray(inputs["L0_attn_b_out"], f32)
    wgu = np.asarray(inputs["L0_ffn_w_gate_up"], f32)
    wpj = np.asarray(inputs["L0_ffn_w_proj"], f32)

    common = dict(
        w_in=np.ascontiguousarray(
            W_in.reshape(DT, 128, DT, 128).transpose(2, 1, 0, 3)),
        b_in_rs=np.ascontiguousarray(b_in.reshape(DT, 128).T),
        w_gx=W_gx, w_ga=W_ga,
        b_gx_rs=np.ascontiguousarray(b_gx.T),
        b_ga_rs=np.ascontiguousarray(b_ga.T),
        ar_rs=np.ascontiguousarray(a_real.reshape(GT, 128).T),
        ai_rs=np.ascontiguousarray(a_imag.reshape(GT, 128).T),
        w_out=W_out,
        b_out_bc=np.ascontiguousarray(np.broadcast_to(b_out, (128, D))),
        ident=np.eye(128, dtype=ml_dtypes.bfloat16),
    )

    in_maps = []
    for c in range(NCORES):
        gu_rows = []
        for j in range(8):
            i = 8 * j + c
            gu_rows.append(wgu[128 * i:128 * (i + 1)])
            gu_rows.append(wgu[H + 128 * i:H + 128 * (i + 1)])
        wgu_shard = np.ascontiguousarray(np.stack(gu_rows))
        pj_blocks = []
        for m in range(8):
            k = 8 * m + c
            pj_blocks.append(wpj[:, 128 * k:128 * (k + 1)].reshape(16, 128, 128))
        wpj_shard = np.ascontiguousarray(np.stack(pj_blocks))
        m = dict(common)
        m.update(
            x=np.ascontiguousarray(x[TC * c:TC * (c + 1)]),
            wgu_sh=wgu_shard, wpj_sh=wpj_shard,
            mask_lt=np.ascontiguousarray(np.broadcast_to(
                (np.arange(NCORES) < c).astype(f32), (128, NCORES))),
        )
        in_maps.append(m)
    return in_maps


def kernel(**inputs):
    in_maps = _stage_inputs(inputs)
    nc = build_nc(debug=_DEBUG)
    res = run_bass_kernel_spmd(nc, in_maps, core_ids=list(range(NCORES)))
    out = np.concatenate([np.asarray(r["out"]) for r in res.results], axis=0)
    return out.reshape(1, 8192, D)
